# version 01
# baseline (speedup 1.0000x reference)
import numpy as np
from contextlib import ExitStack

import concourse.bass as bass
import concourse.mybir as mybir
from concourse.bass_utils import run_bass_kernel_spmd

F32 = mybir.dt.float32
BF16 = mybir.dt.bfloat16
AF = mybir.ActivationFunctionType
ALU = mybir.AluOpType
AX = mybir.AxisListType

NCORES = 8
H = 4096
DD = 8192
NST = 16
RR = 256
T = 1024
TH = T + 3
HC = H // 128
DC = DD // 128
GG = 8
NEG = -1.0e30

DEV_TAPS = False


def _sb(es, nc, name, shape, dt):
    return es.enter_context(nc.sbuf_tensor(name, shape, dt))


def _ps(es, nc, name, shape, dt=F32):
    return es.enter_context(nc.psum_tensor(name, shape, dt))


class Ctr:
    def __init__(self, es, nc, name):
        self.h = es.enter_context(nc.semaphore(name))
        self.n = 0


class DRing:

    def __init__(self, es, nc, name, slots):
        self.sems = [es.enter_context(nc.semaphore(f"{name}{i}"))
                     for i in range(slots)]
        self.cnt = [0] * slots

    def inc(self, instr, slot=0, k=16):
        instr.then_inc(self.sems[slot], k)
        self.cnt[slot] += k
        return self.cnt[slot]

    def wait(self, eng, slot=0, val=None):
        eng.wait_ge(self.sems[slot], self.cnt[slot] if val is None else val)

    def wait_all(self, eng):
        for i, s in enumerate(self.sems):
            eng.wait_ge(s, self.cnt[i])


def engine_barrier(blk, bar, k):
    for name in ["sync", "tensor", "scalar", "vector", "gpsimd"]:
        dec = getattr(blk, name)

        def _f(eng):
            eng.nop().then_inc(bar.h, 1)
        dec(_f)
    bar.n += 5
    tgt = bar.n
    for name in ["sync", "tensor", "scalar", "vector", "gpsimd"]:
        dec = getattr(blk, name)

        def _g(eng, tgt=tgt):
            eng.wait_ge(bar.h, tgt)
        dec(_g)



def build_l1():
    nc = bass.Bass("TRN2", target_bir_lowering=False)
    din = {}
    for name, shape in [
        ("hsT", [H, TH]),
        ("winT", [128, 128, HC * 128]),
        ("wxT", [DC, 128, 288]),
        ("wdtT", [DC, 128, 256]),
        ("a_st", [128, DC * NST]),
        ("convw", [128, DC * 4]),
        ("convb", [128, 128]),
        ("dtb", [128, 128]),
        ("dparam", [128, 128]),
        ("dtlnw", [128, 128]),
        ("bclnw", [32, 128]),
        ("bc_ones", [32, 128]),
        ("bc_exp", [2, 128]),
        ("eye128", [128, 128]),
        ("wr_st", [128, HC * 64]),
        ("ffnA", [8, 128, HC * 128]),
        ("exp16", [8, 64, 128]),
        ("seln", [32, 32 * 128]),
    ]:
        din[name] = nc.dram_tensor(name, shape, F32, kind="ExternalInput").ap()

    dout = {}
    for name, shape, dt in [
        ("z1", [DD, T], BF16),
        ("sg_out", [DD, T], F32),
        ("ct_out", [DD, T], F32),
        ("s_out", [128, DC * NST], F32),
        ("ctot_out", [128, DC], F32),
        ("cn_out", [16, T], F32),
        ("hg_out", [1024, T], BF16),
    ]:
        dout[name] = nc.dram_tensor(name, shape, dt, kind="ExternalOutput").ap()
    if DEV_TAPS:
        for name, shape, dt in [
            ("xn_tap", [128, HC * TH], BF16),
            ("u_tap", [DD, T], F32),
            ("ssm_tap", [288, T], F32),
            ("dtn_tap", [RR, T], BF16),
            ("bcn_tap", [32, T], F32),
            ("g_tap", [T, 64], F32),
            ("dl_tap", [DD, T], F32),
        ]:
            dout[name] = nc.dram_tensor(name, shape, dt, kind="ExternalOutput").ap()

    u_dram = nc.dram_tensor("u_dram", [DD, T], F32, kind="Internal").ap()
    sg_dram = nc.dram_tensor("sg_dram", [DD, T], F32, kind="Internal").ap()

    es = ExitStack()
    dtn0 = _sb(es, nc, "dtn0", [128, T], BF16)
    dtn1 = _sb(es, nc, "dtn1", [128, T], BF16)
    bcn = _sb(es, nc, "bcn", [32, T], F32)
    bchi = _sb(es, nc, "bchi", [32, T], BF16)
    bclo = _sb(es, nc, "bclo", [32, T], BF16)
    seln_sb = _sb(es, nc, "seln_sb", [32, 32 * 128], BF16)
    a_sb = _sb(es, nc, "a_sb", [128, DC * NST], F32)
    convw_sb = _sb(es, nc, "convw_sb", [128, DC * 4], F32)
    convb_sb = _sb(es, nc, "convb_sb", [128, 128], F32)
    dtb_sb = _sb(es, nc, "dtb_sb", [128, 128], F32)
    dparam_sb = _sb(es, nc, "dparam_sb", [128, 128], F32)
    dtlnw_sb = _sb(es, nc, "dtlnw_sb", [128, 128], F32)
    bclnw_sb = _sb(es, nc, "bclnw_sb", [32, 128], F32)
    bco_sb = _sb(es, nc, "bco_sb", [32, 2], BF16)
    bce_sb = _sb(es, nc, "bce_sb", [2, 32], BF16)
    eye_sb = _sb(es, nc, "eye_sb", [128, 128], F32)
    ones_col = _sb(es, nc, "ones_col", [128, 1], BF16)
    ones_row = _sb(es, nc, "ones_row", [1, 128], BF16)
    ctot_sb = _sb(es, nc, "ctot_sb", [128, DC], F32)
    s_cap = _sb(es, nc, "s_cap", [128, DC * NST], F32)
    cpool = _sb(es, nc, "cpool", [128, 4], F32)

    sPE = Ctr(es, nc, "sPE")
    sA = Ctr(es, nc, "sA")
    sV = Ctr(es, nc, "sV")
    sP = Ctr(es, nc, "sP")
    rc = DRing(es, nc, "rc", 1)
    rx = DRing(es, nc, "rx", 2)
    rw = DRing(es, nc, "rw", 2)
    ru = DRing(es, nc, "ru", 2)
    rs = DRing(es, nc, "rs", 2)
    rt = DRing(es, nc, "rt", 2)
    rd = DRing(es, nc, "rd", 2)
    rct = DRing(es, nc, "rct", 2)
    rzr = DRing(es, nc, "rz", 2)
    rf = DRing(es, nc, "rf", 2)
    rfw = DRing(es, nc, "rfw", 1)
    rh = DRing(es, nc, "rh", 2)
    rfin = DRing(es, nc, "rfin", 1)
    rtap = DRing(es, nc, "rtap", 1)
    sBar = Ctr(es, nc, "sBar")

    xn_es = ExitStack()
    xn = _sb(xn_es, nc, "xn", [128, HC * TH], BF16)
    ssm_es = ExitStack()
    ssm0 = _sb(ssm_es, nc, "ssm0", [128, T], F32)
    ssm1 = _sb(ssm_es, nc, "ssm1", [128, T], F32)
    ssmBC = _sb(ssm_es, nc, "ssmBC", [32, T], F32)

    blk = es.enter_context(nc.Block())

    cst_es = ExitStack()
    bcof = _sb(cst_es, nc, "bcof", [32, 128], F32)
    bcef = _sb(cst_es, nc, "bcef", [2, 128], F32)
    selnf = _sb(cst_es, nc, "selnf", [32, 32 * 128], F32)

    @blk.sync
    def _(sync):
        for t_, src in [
            (a_sb, din["a_st"]), (convw_sb, din["convw"]), (convb_sb, din["convb"]),
            (dtb_sb, din["dtb"]), (dparam_sb, din["dparam"]), (dtlnw_sb, din["dtlnw"]),
            (bclnw_sb, din["bclnw"]), (eye_sb, din["eye128"]),
            (bcof, din["bc_ones"]), (bcef, din["bc_exp"]), (selnf, din["seln"]),
        ]:
            rc.inc(sync.dma_start(out=t_[:], in_=src))

    for eng_name in ["tensor", "scalar", "vector", "gpsimd"]:
        getattr(blk, eng_name)(lambda eng: rc.wait_all(eng))

    @blk.vector
    def _(v):
        v.tensor_copy(out=bco_sb[:], in_=bcof[:, 0:2])
        v.tensor_copy(out=bce_sb[:], in_=bcef[:, 0:32])
        v.tensor_copy(out=seln_sb[:], in_=selnf[:])
        for i, val in enumerate([1e-5, 1e-6, 1.0]):
            v.memset(cpool[:, i:i + 1], val)
            nc.const_aps.register(F32, val, cpool[:, i:i + 1])
        v.memset(ones_col[:], 1.0)
        v.memset(ones_row[:], 1.0).then_inc(sV.h, 1)
        sV.n += 1

    CONST_V = sV.n

    LNG = [(0, 512), (512, 512), (899, 128)]
    with ExitStack() as pes:
        psA_ = _ps(pes, nc, "psA", [1, 512])
        psB_ = _ps(pes, nc, "psB", [1, 512])
        psMU = _ps(pes, nc, "psMU", [128, 512])
        psRS = _ps(pes, nc, "psRS", [128, 512])
        xbuf = [_sb(pes, nc, f"xbuf{i}", [128, 512], F32) for i in range(2)]
        xbf = [_sb(pes, nc, f"xbf{i}", [128, 512], BF16) for i in range(2)]
        x2bf = [_sb(pes, nc, f"x2bf{i}", [128, 512], BF16) for i in range(2)]
        murow = _sb(pes, nc, "murow", [1, 512], F32)
        m2row = _sb(pes, nc, "m2row", [1, 512], F32)
        musq = _sb(pes, nc, "musq", [1, 512], F32)
        varrow = _sb(pes, nc, "varrow", [1, 512], F32)
        lnrow = _sb(pes, nc, "lnrow", [1, 512], F32)
        rstdrow = _sb(pes, nc, "rstdrow", [1, 512], F32)
        mu_hi = _sb(pes, nc, "mu_hi", [1, 512], BF16)
        mu_lo = _sb(pes, nc, "mu_lo", [1, 512], BF16)
        rs_hi = _sb(pes, nc, "rs_hi", [1, 512], BF16)
        rs_lo = _sb(pes, nc, "rs_lo", [1, 512], BF16)
        t1b = [_sb(pes, nc, f"t1b{i}", [128, 512], F32) for i in range(2)]

        for (c0, gw) in LNG:
            gV, gA, gPE = sV.n, sA.n, sPE.n
            src_g = din["hsT"][:, c0:c0 + gw].rearrange("(hc p) c -> hc p c", p=128)
            xv = []

            @blk.sync
            def _(sync, gV=gV, gA=gA, src_g=src_g, gw=gw, xv=xv):
                for k in range(HC):
                    if k >= 2:
                        sync.wait_ge(sV.h, gV + (k - 1))
                        sync.wait_ge(sA.h, gA + (k - 1))
                    else:
                        sync.wait_ge(sV.h, gV)
                    xv.append(rx.inc(sync.dma_start(out=xbuf[k % 2][:, :gw],
                                                    in_=src_g[k]), k % 2))

            @blk.vector
            def _(v, gPE=gPE, gw=gw, xv=xv):
                for k in range(HC):
                    v.wait_ge(rx.sems[k % 2], xv[k])
                    if k >= 2:
                        v.wait_ge(sPE.h, gPE + (k - 1))
                    else:
                        v.wait_ge(sPE.h, gPE)
                    v.tensor_copy(out=xbf[k % 2][:, :gw], in_=xbuf[k % 2][:, :gw]
                                  ).then_inc(sV.h, 1)
            sV.n += HC

            @blk.scalar
            def _(s, gPE=gPE, gw=gw, xv=xv):
                s.wait_ge(sA.h, sA.n)
                for k in range(HC):
                    s.wait_ge(rx.sems[k % 2], xv[k])
                    if k >= 2:
                        s.wait_ge(sPE.h, gPE + (k - 1))
                    else:
                        s.wait_ge(sPE.h, gPE)
                    s.activation(out=x2bf[k % 2][:, :gw], in_=xbuf[k % 2][:, :gw],
                                 func=AF.Square).then_inc(sA.h, 1)
            sA.n += HC

            @blk.tensor
            def _(te, gV=gV, gA=gA, gw=gw):
                te.wait_ge(sV.h, CONST_V)
                for k in range(HC):
                    te.wait_ge(sV.h, gV + (k + 1))
                    te.wait_ge(sA.h, gA + (k + 1))
                    te.matmul(out=psA_[:, :gw], lhsT=ones_col[:],
                              rhs=xbf[k % 2][:, :gw],
                              start=(k == 0), stop=(k == HC - 1))
                    te.matmul(out=psB_[:, :gw], lhsT=ones_col[:],
                              rhs=x2bf[k % 2][:, :gw],
                              start=(k == 0), stop=(k == HC - 1)).then_inc(sPE.h, 1)
            sPE.n += HC

            eV, eA, ePE = sV.n, sA.n, sPE.n

            @blk.vector
            def _(v, gw=gw, ePE=ePE, eA=eA):
                def edge(ins):
                    ins.then_inc(sV.h, 1)
                    sV.n += 1
                    v.wait_ge(sV.h, sV.n)
                v.wait_ge(sPE.h, ePE)
                edge(v.tensor_scalar_mul(out=murow[:, :gw], in0=psA_[:, :gw],
                                         scalar1=1.0 / H))
                edge(v.tensor_scalar_mul(out=m2row[:, :gw], in0=psB_[:, :gw],
                                         scalar1=1.0 / H))
                edge(v.tensor_mul(out=musq[:, :gw], in0=murow[:, :gw],
                                  in1=murow[:, :gw]))
                v.tensor_sub(out=varrow[:, :gw], in0=m2row[:, :gw],
                             in1=musq[:, :gw]).then_inc(sV.h, 1)
                sV.n += 1
                v.wait_ge(sA.h, eA + 2)
                edge(v.tensor_copy(out=mu_hi[:, :gw], in_=murow[:, :gw]))
                edge(v.tensor_copy(out=rs_hi[:, :gw], in_=rstdrow[:, :gw]))
                v.tensor_sub(out=mu_lo[:, :gw], in0=murow[:, :gw], in1=mu_hi[:, :gw])
                v.tensor_sub(out=rs_lo[:, :gw], in0=rstdrow[:, :gw],
                             in1=rs_hi[:, :gw]).then_inc(sV.h, 1)
                sV.n += 1
            pass

            @blk.scalar
            def _(s, gw=gw, eV=eV):
                s.wait_ge(sV.h, eV + 4)
                s.activation(out=lnrow[:, :gw], in_=varrow[:, :gw], func=AF.Ln,
                             bias=1e-5).then_inc(sA.h, 1)
                sA.n += 1
                s.wait_ge(sA.h, sA.n)
                s.activation(out=rstdrow[:, :gw], in_=lnrow[:, :gw], func=AF.Exp,
                             scale=-0.5).then_inc(sA.h, 1)
            sA.n += 1

            @blk.tensor
            def _(te, gw=gw, eV=eV):
                te.wait_ge(sV.h, eV + 7)
                te.matmul(out=psMU[:, :gw], lhsT=ones_row[:], rhs=mu_hi[:, :gw],
                          start=True, stop=False)
                te.matmul(out=psMU[:, :gw], lhsT=ones_row[:], rhs=mu_lo[:, :gw],
                          start=False, stop=True)
                te.matmul(out=psRS[:, :gw], lhsT=ones_row[:], rhs=rs_hi[:, :gw],
                          start=True, stop=False)
                te.matmul(out=psRS[:, :gw], lhsT=ones_row[:], rhs=rs_lo[:, :gw],
                          start=False, stop=True).then_inc(sPE.h, 1)
            sPE.n += 1

            p2V, p2PE = sV.n, sPE.n
            xv2 = []

            @blk.sync
            def _(sync, p2V=p2V, p2PE=p2PE, src_g=src_g, gw=gw, xv2=xv2):
                sync.wait_ge(sPE.h, p2PE)
                for k in range(HC):
                    if k >= 2:
                        sync.wait_ge(sV.h, p2V + 2 * (k - 1))
                    xv2.append(rx.inc(sync.dma_start(out=xbuf[k % 2][:, :gw],
                                                     in_=src_g[k]), k % 2))

            @blk.vector
            def _(v, c0=c0, gw=gw, p2PE=p2PE, xv2=xv2):
                v.wait_ge(sPE.h, p2PE)
                for k in range(HC):
                    v.wait_ge(rx.sems[k % 2], xv2[k])
                    v.tensor_sub(out=t1b[k % 2][:, :gw], in0=xbuf[k % 2][:, :gw],
                                 in1=psMU[:, :gw]).then_inc(sV.h, 1)
                    sV.n += 1
                    v.wait_ge(sV.h, sV.n)
                    v.tensor_mul(out=xn[:, k * TH + c0:k * TH + c0 + gw],
                                 in0=t1b[k % 2][:, :gw], in1=psRS[:, :gw]
                                 ).then_inc(sV.h, 1)
                    sV.n += 1
            pass

    LN_V = sV.n
    cst_es.close()
    engine_barrier(blk, sBar, 0)

    if DEV_TAPS:
        @blk.sync
        def _(sync):
            sync.wait_ge(sV.h, LN_V)
            rtap.inc(sync.dma_start(out=dout["xn_tap"], in_=xn[:]))

    with ExitStack() as pes:
        psP = [_ps(pes, nc, f"psP{i}", [128, 512]) for i in range(2)]
        psS0 = [_ps(pes, nc, f"psS0{i}", [128, 512]) for i in range(2)]
        psS1 = [_ps(pes, nc, f"psS1{i}", [128, 512]) for i in range(2)]
        psS2 = [_ps(pes, nc, f"psS2{i}", [32, 512]) for i in range(2)]
        wch = [_sb(pes, nc, f"wch{i}", [128, HC * 128], BF16) for i in range(2)]
        wxch = [_sb(pes, nc, f"wxch{i}", [128, 288], BF16) for i in range(2)]
        wchf = [_sb(pes, nc, f"wchf{i}", [128, HC * 128], F32) for i in range(2)]
        wxchf = [_sb(pes, nc, f"wxchf{i}", [128, 288], F32) for i in range(2)]
        upre = _sb(pes, nc, "upre", [128, TH], F32)
        conv1 = _sb(pes, nc, "conv1", [128, T], F32)
        ubuf = [_sb(pes, nc, f"ubuf{i}", [128, T], F32) for i in range(2)]
        ubf = [_sb(pes, nc, f"ubf{i}", [128, T], BF16) for i in range(2)]
        sgb = [_sb(pes, nc, f"sgb{i}", [128, T], F32) for i in range(2)]
        @blk.vector
        def _(v):
            v.memset(ssm0[:], 0.0)
            v.memset(ssm1[:], 0.0)
            v.memset(ssmBC[:], 0.0).then_inc(sV.h, 1)
        sV.n += 1

        MEMSET_V = sV.n
        pe_chunk = []
        evac_marks = {}
        silu_marks = {}
        store_marks = {}
        conv_marks = {}
        wx_marks = {}
        cast_marks = {}
        ssm_add_marks = []

        for j in range(128):
            is_u = j < 64
            jj = j if is_u else j - 64

            @blk.sync
            def _(sync, j=j, is_u=is_u):
                if j >= 2:
                    sync.wait_ge(sP.h, cast_marks[j - 2])
                rw.inc(sync.dma_start(out=wchf[j % 2][:], in_=din["winT"][j]),
                       j % 2)
                if is_u:
                    rw.inc(sync.dma_start(out=wxchf[j % 2][:], in_=din["wxT"][j]),
                           j % 2)
            wLd = rw.cnt[j % 2]

            @blk.gpsimd
            def _(g, j=j, is_u=is_u, wLd=wLd):
                g.wait_ge(rw.sems[j % 2], wLd)
                if j >= 2:
                    g.wait_ge(sPE.h, pe_chunk[j - 2])
                g.tensor_copy(out=wch[j % 2][:], in_=wchf[j % 2][:]).then_inc(sP.h, 1)
                sP.n += 1
                if is_u:
                    g.tensor_copy(out=wxch[j % 2][:], in_=wxchf[j % 2][:]
                                  ).then_inc(sP.h, 1)
                    sP.n += 1
            cast_marks[j] = sP.n
            wG = sP.n

            if is_u:
                groups = [(0, 512, 0), (512, 512, 1), (899, 128, 0)]
            else:
                groups = [(3, 512, 0), (515, 512, 1)]

            g_pe_marks = []
            for gi, (c0, gw, pb) in enumerate(groups):
                @blk.tensor
                def _(te, j=j, gi=gi, c0=c0, gw=gw, pb=pb, wG=wG):
                    if j == 0 and gi == 0:
                        te.wait_ge(sV.h, MEMSET_V)
                    te.wait_ge(sP.h, wG)
                    if gi >= 2:
                        te.wait_ge(sA.h, evac_marks[(j, 0)])
                    elif j >= 1:
                        prev_groups = 3 if j - 1 < 64 else 2
                        for pgi in range(prev_groups - 1, -1, -1):
                            ppb = [0, 1, 0][pgi] if j - 1 < 64 else [0, 1][pgi]
                            if ppb == pb:
                                te.wait_ge(sA.h, evac_marks[(j - 1, pgi)])
                                break
                    for hc in range(HC):
                        mm = te.matmul(out=psP[pb][:, :gw],
                                       lhsT=wch[j % 2][:, hc * 128:(hc + 1) * 128],
                                       rhs=xn[:, hc * TH + c0:hc * TH + c0 + gw],
                                       start=(hc == 0), stop=(hc == HC - 1))
                        if hc == HC - 1:
                            mm.then_inc(sPE.h, 1)
                sPE.n += 1
                g_pe_marks.append(sPE.n)

                if is_u:
                    dsts = [(0, 512), (512, 512), (899, 128)]
                    d0, dw = dsts[gi]

                    @blk.scalar
                    def _(s, j=j, gi=gi, pb=pb, d0=d0, dw=dw, mark=sPE.n):
                        s.wait_ge(sA.h, sA.n)
                        s.wait_ge(sPE.h, mark)
                        if gi == 0 and j >= 1:
                            s.wait_ge(sV.h, conv_marks[j - 1])
                        s.copy(out=upre[:, d0:d0 + dw], in_=psP[pb][:, :dw]
                               ).then_inc(sA.h, 1)
                    sA.n += 1
                    evac_marks[(j, gi)] = sA.n
                else:
                    dsts = [(0, 512), (512, 512)]
                    d0, dw = dsts[gi]

                    @blk.scalar
                    def _(s, j=j, jj=jj, gi=gi, pb=pb, d0=d0, dw=dw, mark=sPE.n):
                        s.wait_ge(sA.h, sA.n)
                        s.wait_ge(sPE.h, mark)
                        if gi == 0 and jj >= 2:
                            s.wait_ge(rs.sems[jj % 2], store_marks[64 + jj - 2])
                        s.activation(out=sgb[jj % 2][:, d0:d0 + dw],
                                     in_=psP[pb][:, :dw], func=AF.Silu
                                     ).then_inc(sA.h, 1)
                    sA.n += 1
                    evac_marks[(j, gi)] = sA.n

            pe_chunk.append(sPE.n)

            if is_u:
                up_done = evac_marks[(j, 2)]

                @blk.vector
                def _(v, jj=jj, up_done=up_done):
                    v.wait_ge(sV.h, sV.n)
                    v.wait_ge(sA.h, up_done)
                    if jj >= 1:
                        v.wait_ge(sA.h, silu_marks[jj - 1])
                    v.tensor_scalar(out=conv1[:], in0=upre[:, 0:T],
                                    scalar1=convw_sb[:, 4 * jj:4 * jj + 1],
                                    scalar2=None, op0=ALU.mult).then_inc(sV.h, 1)
                    sV.n += 1
                    for kk in range(1, 4):
                        v.wait_ge(sV.h, sV.n)
                        v.scalar_tensor_tensor(
                            out=conv1[:], in0=upre[:, kk:T + kk],
                            scalar=convw_sb[:, 4 * jj + kk:4 * jj + kk + 1],
                            in1=conv1[:], op0=ALU.mult, op1=ALU.add
                        ).then_inc(sV.h, 1)
                        sV.n += 1
                pass
                conv_done = sV.n
                conv_marks[jj] = conv_done

                @blk.scalar
                def _(s, jj=jj, conv_done=conv_done):
                    s.wait_ge(sA.h, sA.n)
                    s.wait_ge(sV.h, conv_done)
                    if jj >= 2:
                        s.wait_ge(ru.sems[jj % 2], store_marks[jj - 2])
                    s.activation(out=ubuf[jj % 2][:], in_=conv1[:], func=AF.Silu,
                                 bias=convb_sb[:, jj:jj + 1]).then_inc(sA.h, 1)
                sA.n += 1
                silu_marks[jj] = sA.n
                u_done = sA.n

                @blk.vector
                def _(v, jj=jj, u_done=u_done):
                    v.wait_ge(sV.h, sV.n)
                    v.wait_ge(sA.h, u_done)
                    if jj >= 2:
                        v.wait_ge(sPE.h, wx_marks[jj - 2])
                    v.tensor_copy(out=ubf[jj % 2][:], in_=ubuf[jj % 2][:]
                                  ).then_inc(sV.h, 1)
                sV.n += 1
                ubf_done = sV.n

                @blk.tensor
                def _(te, j=j, jj=jj, ubf_done=ubf_done):
                    te.wait_ge(sV.h, ubf_done)
                    st = (jj % 16 == 0)
                    sp = (jj % 16 == 15)
                    for th in range(2):
                        rhs = ubf[jj % 2][:, th * 512:(th + 1) * 512]
                        te.matmul(out=psS0[th][:, :], lhsT=wxch[j % 2][:, 0:128],
                                  rhs=rhs, start=st, stop=sp)
                        te.matmul(out=psS1[th][:, :], lhsT=wxch[j % 2][:, 128:256],
                                  rhs=rhs, start=st, stop=sp)
                        mm = te.matmul(out=psS2[th][:, :],
                                       lhsT=wxch[j % 2][:, 256:288],
                                       rhs=rhs, start=st, stop=sp)
                        if th == 1:
                            mm.then_inc(sPE.h, 1)
                sPE.n += 1
                wx_pe = sPE.n
                wx_marks[jj] = sPE.n
                pe_chunk[-1] = sPE.n

                if jj % 16 == 15:
                    @blk.vector
                    def _(v, wx_pe=wx_pe, prevm=ssm_add_marks[-1] if ssm_add_marks
                          else 0):
                        v.wait_ge(sPE.h, wx_pe)
                        if prevm:
                            v.wait_ge(sV.h, prevm)
                        for th in range(2):
                            sl = slice(th * 512, (th + 1) * 512)
                            v.tensor_add(out=ssm0[:, sl], in0=ssm0[:, sl],
                                         in1=psS0[th][:, :])
                            v.tensor_add(out=ssm1[:, sl], in0=ssm1[:, sl],
                                         in1=psS1[th][:, :])
                            ins = v.tensor_add(out=ssmBC[:, sl], in0=ssmBC[:, sl],
                                               in1=psS2[th][:, :])
                            if th == 1:
                                ins.then_inc(sV.h, 1)
                    sV.n += 1
                    ssm_add_marks.append(sV.n)

                @blk.sync
                def _(sync, jj=jj, u_done=u_done, ubf_done=ubf_done):
                    sync.wait_ge(sA.h, u_done)
                    sync.wait_ge(sV.h, ubf_done)
                    ru.inc(sync.dma_start(out=u_dram[jj * 128:(jj + 1) * 128, :],
                                          in_=ubuf[jj % 2][:]), jj % 2)
                    if DEV_TAPS:
                        ru.inc(sync.dma_start(
                            out=dout["u_tap"][jj * 128:(jj + 1) * 128, :],
                            in_=ubuf[jj % 2][:]), jj % 2)
                store_marks[jj] = ru.cnt[jj % 2]
            else:
                sg_done = evac_marks[(j, 1)]

                @blk.sync
                def _(sync, jj=jj, sg_done=sg_done):
                    sync.wait_ge(sA.h, sg_done)
                    rs.inc(sync.dma_start(out=sg_dram[jj * 128:(jj + 1) * 128, :],
                                          in_=sgb[jj % 2][:]), jj % 2)
                    rs.inc(sync.dma_start(
                        out=dout["sg_out"][jj * 128:(jj + 1) * 128, :],
                        in_=sgb[jj % 2][:]), jj % 2)
                store_marks[64 + jj] = rs.cnt[jj % 2]

    PROJ_V, PROJ_A, PROJ_PE = sV.n, sA.n, sPE.n

    if DEV_TAPS:
        @blk.sync
        def _(sync):
            sync.wait_ge(sV.h, PROJ_V)
            rtap.inc(sync.dma_start(out=dout["ssm_tap"][0:128, :], in_=ssm0[:]))
            rtap.inc(sync.dma_start(out=dout["ssm_tap"][128:256, :], in_=ssm1[:]))
            rtap.inc(sync.dma_start(out=dout["ssm_tap"][256:288, :], in_=ssmBC[:]))

    @blk.sync
    def _(sync):
        ru.wait_all(sync)
        rs.wait_all(sync)
        if DEV_TAPS:
            rtap.wait_all(sync)

    engine_barrier(blk, sBar, 0)

    with ExitStack() as pes:
        psq = [_ps(pes, nc, f"psq{i}", [1, 512]) for i in range(2)]
        psq2 = [_ps(pes, nc, f"psq2{i}", [2, 512]) for i in range(2)]
        psRD = [_ps(pes, nc, f"psRD{i}", [128, 512]) for i in range(2)]
        psRB = [_ps(pes, nc, f"psRB{i}", [32, 512]) for i in range(2)]
        dsq0 = _sb(pes, nc, "dsq0", [128, T], BF16)
        dsq1 = _sb(pes, nc, "dsq1", [128, T], BF16)
        bsq = _sb(pes, nc, "bsq", [32, T], BF16)
        r_dt = _sb(pes, nc, "r_dt", [1, T], F32)
        r_bc = _sb(pes, nc, "r_bc", [2, T], F32)
        lnt = _sb(pes, nc, "lnt", [2, T], F32)
        rdt_hi = _sb(pes, nc, "rdt_hi", [1, T], BF16)
        rdt_lo = _sb(pes, nc, "rdt_lo", [1, T], BF16)
        rbc_hi = _sb(pes, nc, "rbc_hi", [2, T], BF16)
        rbc_lo = _sb(pes, nc, "rbc_lo", [2, T], BF16)

        @blk.scalar
        def _(s):
            s.wait_ge(sV.h, PROJ_V)
            s.activation(out=dsq0[:], in_=ssm0[:], func=AF.Square)
            s.activation(out=dsq1[:], in_=ssm1[:], func=AF.Square)
            s.activation(out=bsq[:], in_=ssmBC[:], func=AF.Square).then_inc(sA.h, 1)
        sA.n += 1
        sq_done = sA.n

        @blk.tensor
        def _(te, sq_done=sq_done):
            te.wait_ge(sA.h, sq_done)
            te.wait_ge(sV.h, CONST_V)
            for th in range(2):
                sl = slice(th * 512, (th + 1) * 512)
                te.matmul(out=psq[th][:, :], lhsT=ones_col[:], rhs=dsq0[:, sl],
                          start=True, stop=False)
                te.matmul(out=psq[th][:, :], lhsT=ones_col[:], rhs=dsq1[:, sl],
                          start=False, stop=True)
                mm = te.matmul(out=psq2[th][:, :], lhsT=bco_sb[:], rhs=bsq[:, sl],
                               start=True, stop=True)
                if th == 1:
                    mm.then_inc(sPE.h, 1)
        sPE.n += 1
        q_done = sPE.n

        @blk.scalar
        def _(s, q_done=q_done):
            s.wait_ge(sPE.h, q_done)
            for th in range(2):
                sl = slice(th * 512, (th + 1) * 512)
                s.activation(out=r_dt[:, sl], in_=psq[th][:, :], func=AF.Ln,
                             scale=1.0 / RR, bias=1e-6)
                s.activation(out=lnt[:, sl], in_=psq2[th][:, :], func=AF.Ln,
                             scale=1.0 / NST, bias=1e-6).then_inc(sA.h, 1)
                sA.n += 1
            s.wait_ge(sA.h, sA.n)
            s.activation(out=r_dt[:], in_=r_dt[:], func=AF.Exp, scale=-0.5)
            s.activation(out=r_bc[:], in_=lnt[:], func=AF.Exp, scale=-0.5
                         ).then_inc(sA.h, 1)
        sA.n += 1
        r_done = sA.n

        @blk.vector
        def _(v, r_done=r_done):
            v.wait_ge(sA.h, r_done)
            v.tensor_copy(out=rdt_hi[:], in_=r_dt[:]).then_inc(sV.h, 1)
            sV.n += 1
            v.wait_ge(sV.h, sV.n)
            v.tensor_sub(out=rdt_lo[:], in0=r_dt[:], in1=rdt_hi[:])
            v.tensor_copy(out=rbc_hi[:], in_=r_bc[:]).then_inc(sV.h, 1)
            sV.n += 1
            v.wait_ge(sV.h, sV.n)
            v.tensor_sub(out=rbc_lo[:], in0=r_bc[:], in1=rbc_hi[:]).then_inc(sV.h, 1)
        sV.n += 1
        hl_done = sV.n

        @blk.tensor
        def _(te, hl_done=hl_done):
            te.wait_ge(sV.h, hl_done)
            for th in range(2):
                sl = slice(th * 512, (th + 1) * 512)
                te.matmul(out=psRD[th][:, :], lhsT=ones_row[:], rhs=rdt_hi[:, sl],
                          start=True, stop=False)
                te.matmul(out=psRD[th][:, :], lhsT=ones_row[:], rhs=rdt_lo[:, sl],
                          start=False, stop=True)
                te.matmul(out=psRB[th][:, :], lhsT=bce_sb[:], rhs=rbc_hi[:, sl],
                          start=True, stop=False)
                mm = te.matmul(out=psRB[th][:, :], lhsT=bce_sb[:], rhs=rbc_lo[:, sl],
                               start=False, stop=True)
                if th == 1:
                    mm.then_inc(sPE.h, 1)
        sPE.n += 1
        bb_done = sPE.n

        @blk.vector
        def _(v, bb_done=bb_done):
            v.wait_ge(sPE.h, bb_done)
            for th in range(2):
                sl = slice(th * 512, (th + 1) * 512)
                v.scalar_tensor_tensor(out=dtn0[:, sl], in0=ssm0[:, sl],
                                       scalar=dtlnw_sb[:, 0:1], in1=psRD[th][:, :],
                                       op0=ALU.mult, op1=ALU.mult)
                v.scalar_tensor_tensor(out=dtn1[:, sl], in0=ssm1[:, sl],
                                       scalar=dtlnw_sb[:, 1:2], in1=psRD[th][:, :],
                                       op0=ALU.mult, op1=ALU.mult)
                v.scalar_tensor_tensor(out=bcn[:, sl], in0=ssmBC[:, sl],
                                       scalar=bclnw_sb[:, 0:1], in1=psRB[th][:, :],
                                       op0=ALU.mult, op1=ALU.mult).then_inc(sV.h, 1)
                sV.n += 1
            v.wait_ge(sV.h, sV.n)
            v.tensor_copy(out=bchi[:], in_=bcn[:]).then_inc(sV.h, 1)
            sV.n += 1
            v.wait_ge(sV.h, sV.n)
            v.tensor_sub(out=bclo[:], in0=bcn[:], in1=bchi[:]).then_inc(sV.h, 1)
        sV.n += 1
        bcn_done = sV.n

        @blk.sync
        def _(sync, bcn_done=bcn_done):
            sync.wait_ge(sV.h, bcn_done)
            rfin.inc(sync.dma_start(out=dout["cn_out"], in_=bcn[16:32, :]))
            if DEV_TAPS:
                rtap.inc(sync.dma_start(out=dout["dtn_tap"][0:128, :], in_=dtn0[:]))
                rtap.inc(sync.dma_start(out=dout["dtn_tap"][128:256, :], in_=dtn1[:]))
                rtap.inc(sync.dma_start(out=dout["bcn_tap"], in_=bcn[:]))

    B_V = sV.n

    engine_barrier(blk, sBar, 0)

    with ExitStack() as pes:
        psSC = _ps(pes, nc, "psSC", [128, 64])
        psGT = _ps(pes, nc, "psGT", [64, 128])
        psH = [_ps(pes, nc, f"psH{i}", [128, 512]) for i in range(2)]
        psGB = [_ps(pes, nc, f"psGB{i}", [128, 512]) for i in range(2)]
        wrb = _sb(pes, nc, "wrb", [128, HC * 64], BF16)
        ach = [_sb(pes, nc, f"ach{i}", [128, HC * 128], BF16) for i in range(2)]
        e16b = [_sb(pes, nc, f"e16b{i}", [64, 128], BF16) for i in range(2)]
        sc = _sb(pes, nc, "sc", [128, 64], F32)
        wk = _sb(pes, nc, "wk", [128, 64], F32)
        wk2 = _sb(pes, nc, "wk2", [128, 64], F32)
        wk3 = _sb(pes, nc, "wk3", [128, 64], F32)
        vals = _sb(pes, nc, "vals", [128, 16], F32)
        nmax = _sb(pes, nc, "nmax", [128, 1], F32)
        e16v = _sb(pes, nc, "e16v", [128, 16], F32)
        zrow = _sb(pes, nc, "zrow", [128, 1], F32)
        rz = _sb(pes, nc, "rz", [128, 1], F32)
        mask = _sb(pes, nc, "mask", [128, 64], F32)
        eo = _sb(pes, nc, "eo", [128, 64], F32)
        gmat = _sb(pes, nc, "gmat", [128, 64], F32)
        gt_s = _sb(pes, nc, "gt_s", [64, T], F32)
        gt_hi = _sb(pes, nc, "gt_hi", [64, T], BF16)
        gt_lo = _sb(pes, nc, "gt_lo", [64, T], BF16)
        h_s = _sb(pes, nc, "h_s", [128, T], F32)
        hg_b = [_sb(pes, nc, f"hg_b{i}", [128, T], BF16) for i in range(2)]

        wrbf = _sb(pes, nc, "wrbf", [128, HC * 64], F32)
        achf = [_sb(pes, nc, f"achf{i}", [128, HC * 128], F32) for i in range(2)]
        e16f = [_sb(pes, nc, f"e16f{i}", [64, 128], F32) for i in range(2)]

        @blk.sync
        def _(sync):
            rfw.inc(sync.dma_start(out=wrbf[:], in_=din["wr_st"]))

        @blk.gpsimd
        def _(g):
            rfw.wait(g)
            g.tensor_copy(out=wrb[:], in_=wrbf[:]).then_inc(sP.h, 1)
            sP.n += 1
        wr_g = sP.n

        for i in range(8):
            @blk.tensor
            def _(te, i=i, wr_g=wr_g):
                te.wait_ge(sP.h, wr_g)
                te.wait_ge(sV.h, B_V if i == 0 else sV.n)
                for hc in range(HC):
                    mm = te.matmul(
                        out=psSC[:, :],
                        lhsT=xn[:, hc * TH + 3 + i * 128: hc * TH + 3 + (i + 1) * 128],
                        rhs=wrb[:, hc * 64:(hc + 1) * 64],
                        start=(hc == 0), stop=(hc == HC - 1))
                    if hc == HC - 1:
                        mm.then_inc(sPE.h, 1)
            sPE.n += 1
            sc_done = sPE.n

            @blk.vector
            def _(v, sc_done=sc_done):
                v.wait_ge(sV.h, sV.n)
                v.wait_ge(sPE.h, sc_done)
                v.tensor_copy(out=sc[:], in_=psSC[:, :]).then_inc(sV.h, 1)
                sV.n += 1
                v.wait_ge(sV.h, sV.n)
                v.tensor_copy(out=wk[:], in_=sc[:]).then_inc(sV.h, 1)
                sV.n += 1
                v.wait_ge(sV.h, sV.n)
                v.max(out=vals[:, 0:8], in_=wk[:]).then_inc(sV.h, 1)
                sV.n += 1
                v.wait_ge(sV.h, sV.n)
                v.match_replace(out=wk2[:], in_to_replace=vals[:, 0:8],
                                in_values=wk[:], imm_value=NEG).then_inc(sV.h, 1)
                sV.n += 1
                v.wait_ge(sV.h, sV.n)
                v.max(out=vals[:, 8:16], in_=wk2[:]).then_inc(sV.h, 1)
                sV.n += 1
                v.wait_ge(sV.h, sV.n)
                v.match_replace(out=wk3[:], in_to_replace=vals[:, 8:16],
                                in_values=wk2[:], imm_value=NEG).then_inc(sV.h, 1)
                sV.n += 1
                v.wait_ge(sV.h, sV.n)
                v.tensor_scalar_mul(out=nmax[:], in0=vals[:, 0:1], scalar1=-1.0
                                    ).then_inc(sV.h, 1)
            sV.n += 1
            tk_done = sV.n

            @blk.scalar
            def _(s, tk_done=tk_done):
                s.wait_ge(sA.h, sA.n)
                s.wait_ge(sV.h, tk_done)
                s.activation(out=e16v[:], in_=vals[:], func=AF.Exp,
                             bias=nmax[:, 0:1])
                s.activation(out=eo[:], in_=sc[:], func=AF.Exp,
                             bias=nmax[:, 0:1]).then_inc(sA.h, 1)
            sA.n += 1
            exp_done = sA.n

            @blk.vector
            def _(v, exp_done=exp_done):
                v.wait_ge(sV.h, sV.n)
                v.wait_ge(sA.h, exp_done)
                v.reduce_sum(out=zrow[:], in_=e16v[:], axis=AX.X).then_inc(sV.h, 1)
                sV.n += 1
                v.wait_ge(sV.h, sV.n)
                v.reciprocal(out=rz[:], in_=zrow[:]).then_inc(sV.h, 1)
                sV.n += 1
                v.wait_ge(sV.h, sV.n)
                v.tensor_tensor(out=mask[:], in0=sc[:], in1=wk3[:], op=ALU.is_gt
                                ).then_inc(sV.h, 1)
                sV.n += 1
                v.wait_ge(sV.h, sV.n)
                v.scalar_tensor_tensor(out=gmat[:], in0=eo[:], scalar=rz[:, 0:1],
                                       in1=mask[:], op0=ALU.mult, op1=ALU.mult
                                       ).then_inc(sV.h, 1)
            sV.n += 1
            g_done = sV.n

            @blk.tensor
            def _(te, g_done=g_done):
                te.wait_ge(sV.h, g_done)
                te.transpose(out=psGT[:, :], in_=gmat[:], identity=eye_sb[:]
                             ).then_inc(sPE.h, 1)
            sPE.n += 1
            t_done = sPE.n

            @blk.vector
            def _(v, i=i, t_done=t_done):
                v.wait_ge(sV.h, sV.n)
                v.wait_ge(sPE.h, t_done)
                v.tensor_copy(out=gt_s[:, i * 128:(i + 1) * 128], in_=psGT[:, :]
                              ).then_inc(sV.h, 1)
            sV.n += 1

            if DEV_TAPS:
                gm_done = sV.n

                @blk.sync
                def _(sync, i=i, gm_done=gm_done):
                    sync.wait_ge(sV.h, gm_done)
                    rtap.inc(sync.dma_start(
                        out=dout["g_tap"][i * 128:(i + 1) * 128, :], in_=gmat[:]))

        @blk.vector
        def _(v, at=sV.n):
            v.wait_ge(sV.h, at)
            v.tensor_copy(out=gt_hi[:], in_=gt_s[:]).then_inc(sV.h, 1)
            sV.n += 1
            v.wait_ge(sV.h, sV.n)
            v.tensor_sub(out=gt_lo[:], in0=gt_s[:], in1=gt_hi[:]).then_inc(sV.h, 1)
        sV.n += 1
        gt_done = sV.n

        hg_marks = {}
        for c in range(8):
            @blk.sync
            def _(sync, c=c, at=sP.n):
                if c >= 2:
                    sync.wait_ge(sP.h, at)
                rf.inc(sync.dma_start(out=achf[c % 2][:], in_=din["ffnA"][c]),
                       c % 2)
                rf.inc(sync.dma_start(out=e16f[c % 2][:], in_=din["exp16"][c]),
                       c % 2)

            @blk.gpsimd
            def _(g, c=c, at=rf.cnt[c % 2], pat=sPE.n):
                g.wait_ge(rf.sems[c % 2], at)
                if c >= 2:
                    g.wait_ge(sPE.h, pat)
                g.tensor_copy(out=ach[c % 2][:], in_=achf[c % 2][:])
                g.tensor_copy(out=e16b[c % 2][:], in_=e16f[c % 2][:]
                              ).then_inc(sP.h, 1)
                sP.n += 1
            a_g = sP.n

            @blk.tensor
            def _(te, c=c, a_g=a_g, gt_done=gt_done):
                te.wait_ge(sP.h, a_g)
                te.wait_ge(sV.h, gt_done)
                if c >= 1:
                    te.wait_ge(sV.h, hg_marks[c - 1])
                for th in range(2):
                    for hc in range(HC):
                        te.matmul(
                            out=psH[th][:, :],
                            lhsT=ach[c % 2][:, hc * 128:(hc + 1) * 128],
                            rhs=xn[:, hc * TH + 3 + th * 512:
                                    hc * TH + 3 + th * 512 + 512],
                            start=(hc == 0), stop=(hc == HC - 1))
                for th in range(2):
                    sl = slice(th * 512, (th + 1) * 512)
                    te.matmul(out=psGB[th][:, :], lhsT=e16b[c % 2][:],
                              rhs=gt_hi[:, sl], start=True, stop=False)
                    mm = te.matmul(out=psGB[th][:, :], lhsT=e16b[c % 2][:],
                                   rhs=gt_lo[:, sl], start=False, stop=True)
                    if th == 1:
                        mm.then_inc(sPE.h, 1)
            sPE.n += 1
            h_done = sPE.n

            @blk.scalar
            def _(s, h_done=h_done):
                s.wait_ge(sA.h, sA.n)
                s.wait_ge(sPE.h, h_done)
                s.copy(out=h_s[:, 0:512], in_=psH[0][:, :])
                s.copy(out=h_s[:, 512:1024], in_=psH[1][:, :]).then_inc(sA.h, 1)
            sA.n += 1
            he_done = sA.n

            @blk.vector
            def _(v, c=c, he_done=he_done):
                v.wait_ge(sV.h, sV.n)
                v.wait_ge(sA.h, he_done)
                if c >= 2:
                    v.wait_ge(rh.sems[c % 2], hg_marks[(c - 2, "st")])
                v.tensor_tensor(out=hg_b[c % 2][:, 0:512], in0=h_s[:, 0:512],
                                in1=psGB[0][:, :], op=ALU.mult)
                v.tensor_tensor(out=hg_b[c % 2][:, 512:1024], in0=h_s[:, 512:1024],
                                in1=psGB[1][:, :], op=ALU.mult).then_inc(sV.h, 1)
            sV.n += 1
            hg_marks[c] = sV.n

            @blk.sync
            def _(sync, c=c, at=sV.n):
                sync.wait_ge(sV.h, at)
                rh.inc(sync.dma_start(out=dout["hg_out"][c * 128:(c + 1) * 128, :],
                                      in_=hg_b[c % 2][:]), c % 2)
            hg_marks[(c, "st")] = rh.cnt[c % 2]

    ssm_es.close()
    xn_es.close()

    @blk.sync
    def _(sync):
        rh.wait_all(sync)
        if DEV_TAPS:
            rtap.wait_all(sync)

    engine_barrier(blk, sBar, 0)

    with ExitStack() as pes:
        psD = [_ps(pes, nc, f"psD{i}", [128, 512]) for i in range(2)]
        psBB = [_ps(pes, nc, f"psBB{i}", [128, 512]) for i in range(2)]
        psCB = [_ps(pes, nc, f"psCB{i}", [128, 512]) for i in range(2)]
        wdch = [_sb(pes, nc, f"wdch{i}", [128, 256], BF16) for i in range(2)]
        wdchf = [_sb(pes, nc, f"wdchf{i}", [128, 256], F32) for i in range(2)]
        dl_g = _sb(pes, nc, "dl_g", [128, GG * T], F32)
        du_g = _sb(pes, nc, "du_g", [128, GG * T], F32)
        y_g = _sb(pes, nc, "y_g", [128, GG * T], F32)
        ab1 = _sb(pes, nc, "ab1", [128, 512], F32)
        exn = _sb(pes, nc, "exn", [128, 512], F32)
        l1p = _sb(pes, nc, "l1p", [128, 512], F32)
        rlu = _sb(pes, nc, "rlu", [128, 512], F32)
        utmp = [_sb(pes, nc, f"utmp{i}", [128, T], F32) for i in range(2)]
        sgt = [_sb(pes, nc, f"sgt{i}", [128, T], F32) for i in range(2)]
        ypre = _sb(pes, nc, "ypre", [128, T], F32)
        z1b = [_sb(pes, nc, f"z1b{i}", [128, T], BF16) for i in range(2)]
        ctt = [_sb(pes, nc, f"ctt{i}", [128, T], F32) for i in range(2)]
        bb_s = _sb(pes, nc, "bb_s", [128, T], F32)
        cb_s = _sb(pes, nc, "cb_s", [128, T], F32)
        abuf = [_sb(pes, nc, f"abuf{i}", [128, T], F32) for i in range(2)]
        bbuf = [_sb(pes, nc, f"bbuf{i}", [128, T], F32) for i in range(2)]
        sbx = [_sb(pes, nc, f"sbx{i}", [128, T], F32) for i in range(2)]

        ct_store = {}
        z1_store = {}
        for g in range(DC // GG):
            dl_marks = {}
            yg_marks = {}
            for jj in range(GG):
                dc = g * GG + jj

                @blk.sync
                def _(sync, dc=dc, at=sA.n):
                    if dc >= 2:
                        sync.wait_ge(sA.h, at)
                    rd.inc(sync.dma_start(out=wdchf[dc % 2][:],
                                          in_=din["wdtT"][dc]), dc % 2)

                @blk.scalar
                def _(s, dc=dc, at=rd.cnt[dc % 2], pat=sPE.n):
                    s.wait_ge(sA.h, sA.n)
                    s.wait_ge(rd.sems[dc % 2], at)
                    if dc >= 2:
                        s.wait_ge(sPE.h, pat)
                    s.copy(out=wdch[dc % 2][:], in_=wdchf[dc % 2][:]
                           ).then_inc(sA.h, 1)
                    sA.n += 1
                wd_g = sA.n

                @blk.sync
                def _(sync, dc=dc, at=sV.n):
                    if dc == 0:
                        ru.wait_all(sync)
                        rs.wait_all(sync)
                    if dc >= 2:
                        sync.wait_ge(sV.h, at)
                    rt.inc(sync.dma_start(out=utmp[dc % 2][:],
                                          in_=u_dram[dc * 128:(dc + 1) * 128, :]),
                           dc % 2)
                u_l = rt.cnt[dc % 2]

                @blk.tensor
                def _(te, dc=dc, wd_g=wd_g, at=sA.n):
                    te.wait_ge(sA.h, wd_g)
                    te.wait_ge(sV.h, B_V)
                    if dc >= 1:
                        te.wait_ge(sA.h, at)
                    for th in range(2):
                        sl = slice(th * 512, (th + 1) * 512)
                        te.matmul(out=psD[th][:, :], lhsT=wdch[dc % 2][:, 0:128],
                                  rhs=dtn0[:, sl], start=True, stop=False)
                        mm = te.matmul(out=psD[th][:, :],
                                       lhsT=wdch[dc % 2][:, 128:256],
                                       rhs=dtn1[:, sl], start=False, stop=True)
                        if th == 1:
                            mm.then_inc(sPE.h, 1)
                sPE.n += 1
                d_pe = sPE.n

                for th in range(2):
                    @blk.scalar
                    def _(s, dc=dc, th=th, d_pe=d_pe, at=sV.n):
                        s.wait_ge(sPE.h, d_pe)
                        if dc == 0 and th == 0:
                            s.wait_ge(sV.h, CONST_V)
                        else:
                            s.wait_ge(sV.h, at)
                        s.activation(out=ab1[:], in_=psD[th][:, :], func=AF.Abs,
                                     bias=dtb_sb[:, dc:dc + 1]).then_inc(sA.h, 1)
                        sA.n += 1
                        s.wait_ge(sA.h, sA.n)
                        s.activation(out=exn[:], in_=ab1[:], func=AF.Exp,
                                     scale=-1.0).then_inc(sA.h, 1)
                        sA.n += 1
                        s.wait_ge(sA.h, sA.n)
                        s.activation(out=l1p[:], in_=exn[:], func=AF.Ln, bias=1.0)
                        s.activation(out=rlu[:], in_=psD[th][:, :], func=AF.Relu,
                                     bias=dtb_sb[:, dc:dc + 1]).then_inc(sA.h, 1)
                    sA.n += 1

                    @blk.vector
                    def _(v, dc=dc, jj=jj, th=th, at=sA.n, u_l=u_l):
                        v.wait_ge(sA.h, at)
                        sl = slice(jj * T + th * 512, jj * T + (th + 1) * 512)
                        ins = v.tensor_add(out=dl_g[:, sl], in0=rlu[:], in1=l1p[:])
                        ins.then_inc(sV.h, 1)
                        sV.n += 1
                        if th == 1:
                            v.wait_ge(sV.h, sV.n)
                            v.wait_ge(rt.sems[dc % 2], u_l)
                            if dc >= 2:
                                v.wait_ge(rct.sems[dc % 2], ct_store[dc - 2])
                            v.tensor_tensor_scan(
                                out=ctt[dc % 2][:],
                                data0=dl_g[:, jj * T:(jj + 1) * T],
                                data1=dl_g[:, jj * T:(jj + 1) * T],
                                initial=0.0, op0=ALU.add, op1=ALU.bypass
                            ).then_inc(sV.h, 1)
                            sV.n += 1
                            v.tensor_tensor(out=du_g[:, jj * T:(jj + 1) * T],
                                            in0=dl_g[:, jj * T:(jj + 1) * T],
                                            in1=utmp[dc % 2][:], op=ALU.mult)
                            v.wait_ge(sV.h, sV.n)
                            v.tensor_copy(out=ctot_sb[:, dc:dc + 1],
                                          in_=ctt[dc % 2][:, T - 1:T]
                                          ).then_inc(sV.h, 1)
                            sV.n += 1

                ct_v = sV.n
                dl_marks[jj] = sV.n

                @blk.sync
                def _(sync, dc=dc, ct_v=ct_v):
                    sync.wait_ge(sV.h, ct_v)
                    rct.inc(sync.dma_start(
                        out=dout["ct_out"][dc * 128:(dc + 1) * 128, :],
                        in_=ctt[dc % 2][:]), dc % 2)
                    if DEV_TAPS:
                        rtap.inc(sync.dma_start(
                            out=dout["dl_tap"][dc * 128:(dc + 1) * 128, :],
                            in_=dl_g[:, (dc % GG) * T:((dc % GG) + 1) * T]))
                ct_store[dc] = rct.cnt[dc % 2]

            for n in range(NST):
                @blk.tensor
                def _(te, g=g, n=n, aat=sA.n, vat=sV.n, pat=sP.n):
                    if g > 0 or n > 0:
                        te.wait_ge(sA.h, aat)
                        te.wait_ge(sV.h, vat)
                        te.wait_ge(sP.h, pat)
                    te.wait_ge(sV.h, B_V)
                    selB = seln_sb[:, n * 128:(n + 1) * 128]
                    selC = seln_sb[:, (16 + n) * 128:(16 + n + 1) * 128]
                    for th in range(2):
                        sl = slice(th * 512, (th + 1) * 512)
                        te.matmul(out=psBB[th][:, :], lhsT=selB,
                                  rhs=bchi[:, sl], start=True, stop=False)
                        te.matmul(out=psBB[th][:, :], lhsT=selB,
                                  rhs=bclo[:, sl], start=False, stop=True)
                        te.matmul(out=psCB[th][:, :], lhsT=selC,
                                  rhs=bchi[:, sl], start=True, stop=False)
                        mm = te.matmul(out=psCB[th][:, :], lhsT=selC,
                                       rhs=bclo[:, sl], start=False, stop=True)
                        if th == 1:
                            mm.then_inc(sPE.h, 1)
                sPE.n += 1
                bb_pe = sPE.n

                @blk.scalar
                def _(s, bb_pe=bb_pe):
                    s.wait_ge(sA.h, sA.n)
                    s.wait_ge(sPE.h, bb_pe)
                    for th in range(2):
                        sl = slice(th * 512, (th + 1) * 512)
                        s.copy(out=bb_s[:, sl], in_=psBB[th][:, :])
                        ins = s.copy(out=cb_s[:, sl], in_=psCB[th][:, :])
                        if th == 1:
                            ins.then_inc(sA.h, 1)
                sA.n += 1
                bb_a = sA.n

                for jj in range(GG):
                    dc = g * GG + jj
                    ii = n * GG + jj
                    pp = ii % 2

                    @blk.scalar
                    def _(s, dc=dc, jj=jj, n=n, pp=pp, ii=ii, vat=sV.n,
                          dlm=dl_marks[jj]):
                        s.wait_ge(sA.h, sA.n)
                        if ii >= 2:
                            s.wait_ge(sV.h, vat)
                        s.activation(out=abuf[pp][:],
                                     in_=dl_g[:, jj * T:(jj + 1) * T],
                                     func=AF.Exp,
                                     scale=a_sb[:, dc * NST + n:dc * NST + n + 1]
                                     ).then_inc(sA.h, 1)
                    sA.n += 1
                    a_a = sA.n

                    @blk.gpsimd
                    def _(gp, jj=jj, pp=pp, bb_a=bb_a, ii=ii, vat=sV.n):
                        gp.wait_ge(sP.h, sP.n)
                        if ii >= 2:
                            gp.wait_ge(sV.h, vat)
                        gp.wait_ge(sA.h, bb_a)
                        gp.tensor_tensor(out=bbuf[pp][:],
                                         in0=du_g[:, jj * T:(jj + 1) * T],
                                         in1=bb_s[:], op=ALU.mult).then_inc(sP.h, 1)
                    sP.n += 1
                    b_p = sP.n

                    @blk.vector
                    def _(v, dc=dc, jj=jj, n=n, pp=pp, a_a=a_a, b_p=b_p, pat=sP.n):
                        v.wait_ge(sA.h, a_a)
                        v.wait_ge(sP.h, b_p)
                        if n > 0:
                            v.wait_ge(sP.h, pat)
                        v.tensor_tensor_scan(out=sbx[pp][:], data0=abuf[pp][:],
                                             data1=bbuf[pp][:], initial=0.0,
                                             op0=ALU.mult, op1=ALU.add
                                             ).then_inc(sV.h, 1)
                        sV.n += 1
                        v.wait_ge(sV.h, sV.n)
                        if n == 0:
                            v.tensor_tensor(out=y_g[:, jj * T:(jj + 1) * T],
                                            in0=sbx[pp][:], in1=cb_s[:],
                                            op=ALU.mult)
                        else:
                            v.tensor_tensor(out=bbuf[pp][:], in0=sbx[pp][:],
                                            in1=cb_s[:], op=ALU.mult)
                        v.tensor_copy(out=s_cap[:, dc * NST + n:dc * NST + n + 1],
                                      in_=sbx[pp][:, T - 1:T]).then_inc(sV.h, 1)
                    sV.n += 1
                    t_v = sV.n

                    if n > 0:
                        @blk.gpsimd
                        def _(gp, jj=jj, pp=pp, t_v=t_v,
                              ym=yg_marks.get(jj, 0)):
                            gp.wait_ge(sV.h, t_v)
                            if ym:
                                gp.wait_ge(sP.h, ym)
                            gp.tensor_tensor(out=y_g[:, jj * T:(jj + 1) * T],
                                             in0=y_g[:, jj * T:(jj + 1) * T],
                                             in1=bbuf[pp][:], op=ALU.add
                                             ).then_inc(sP.h, 1)
                        sP.n += 1
                        yg_marks[jj] = sP.n

            for jj in range(GG):
                dc = g * GG + jj
                pp = jj % 2

                @blk.sync
                def _(sync, dc=dc, pp=pp, at=sV.n):
                    if jj >= 2:
                        sync.wait_ge(sV.h, at)
                    rt.inc(sync.dma_start(out=utmp[pp][:],
                                          in_=u_dram[dc * 128:(dc + 1) * 128, :]),
                           pp)
                    rt.inc(sync.dma_start(out=sgt[pp][:],
                                          in_=sg_dram[dc * 128:(dc + 1) * 128, :]),
                           pp)
                ul = rt.cnt[pp]
                yp = sP.n

                @blk.vector
                def _(v, dc=dc, jj=jj, pp=pp, ul=ul, yp=yp):
                    v.wait_ge(sV.h, sV.n)
                    v.wait_ge(rt.sems[pp], ul)
                    v.wait_ge(sP.h, yp)
                    if dc >= 2:
                        v.wait_ge(rzr.sems[pp], z1_store.get(dc - 2, 0))
                    v.scalar_tensor_tensor(out=ypre[:], in0=utmp[pp][:],
                                           scalar=dparam_sb[:, dc:dc + 1],
                                           in1=y_g[:, jj * T:(jj + 1) * T],
                                           op0=ALU.mult, op1=ALU.add
                                           ).then_inc(sV.h, 1)
                    sV.n += 1
                    v.wait_ge(sV.h, sV.n)
                    v.tensor_tensor(out=z1b[pp][:], in0=ypre[:], in1=sgt[pp][:],
                                    op=ALU.mult).then_inc(sV.h, 1)
                sV.n += 1
                z_v = sV.n

                @blk.sync
                def _(sync, dc=dc, pp=pp, z_v=z_v):
                    sync.wait_ge(sV.h, z_v)
                    rzr.inc(sync.dma_start(
                        out=dout["z1"][dc * 128:(dc + 1) * 128, :],
                        in_=z1b[pp][:]), pp)
                z1_store[dc] = rzr.cnt[pp]

    @blk.sync
    def _(sync, at=sV.n):
        sync.wait_ge(sV.h, at)
        rfin.inc(sync.dma_start(out=dout["s_out"], in_=s_cap[:]))
        rfin.inc(sync.dma_start(out=dout["ctot_out"], in_=ctot_sb[:]))
        for r_ in [rfin, rct, rzr, rh, rs]:
            r_.wait_all(sync)
        if DEV_TAPS:
            rtap.wait_all(sync)

    es.close()
    return nc



def build_l2():
    nc = bass.Bass("TRN2", target_bir_lowering=False)
    din = {}
    for name, shape, dt in [
        ("z1", [DD, T], BF16),
        ("sg", [DD, T], F32),
        ("ct", [DD, T], F32),
        ("cn", [16, T], F32),
        ("hg", [1024, T], BF16),
        ("s_all", [8, 128, DC * NST], F32),
        ("ctot_all", [8, 128, DC], F32),
        ("a_st", [128, DC * NST], F32),
        ("maskA", [128, 128], F32),
        ("maskB", [128, 128], F32),
        ("seln", [32, 32 * 128], F32),
        ("woT", [32, 128, 64 * 128], F32),
        ("fbT", [32, 128, 8 * 128], F32),
        ("hsT", [H, T], F32),
    ]:
        din[name] = nc.dram_tensor(name, shape, dt, kind="ExternalInput").ap()
    out_t = nc.dram_tensor("out", [H, T], F32, kind="ExternalOutput").ap()
    z_dram = nc.dram_tensor("z_dram", [DD, T], BF16, kind="Internal").ap()
    if DEV_TAPS:
        sin_tap = nc.dram_tensor("sin_tap", [128, DC * NST], F32,
                                 kind="ExternalOutput").ap()
        hj_tap = nc.dram_tensor("hj_tap", [128, 8 * DC], F32,
                                 kind="ExternalOutput").ap()
        lns_tap = nc.dram_tensor("lns_tap", [128, DC * NST], F32,
                                 kind="ExternalOutput").ap()
        sgn_tap = nc.dram_tensor("sgn_tap", [128, DC * NST], F32,
                                 kind="ExternalOutput").ap()

    es = ExitStack()
    cpool = _sb(es, nc, "cpool", [128, 4], F32)
    sPE = Ctr(es, nc, "sPE")
    sA = Ctr(es, nc, "sA")
    sV = Ctr(es, nc, "sV")
    sP = Ctr(es, nc, "sP")
    sBar = Ctr(es, nc, "sBar")
    qc = DRing(es, nc, "qc", 1)
    qt = DRing(es, nc, "qt", 1)
    qe = DRing(es, nc, "qe", 2)
    qz = DRing(es, nc, "qz", 2)
    qZ = DRing(es, nc, "qZ", 1)
    qw = DRing(es, nc, "qw", 2)
    qf = DRing(es, nc, "qf", 1)
    qo = DRing(es, nc, "qo", 1)
    qtap = DRing(es, nc, "qtap", 1)

    mid = ExitStack()
    a_sb = _sb(mid, nc, "a_sb", [128, DC * NST], F32)
    lnS = _sb(mid, nc, "lnS", [128, DC * NST], F32)
    sgn = _sb(mid, nc, "sgn", [128, DC * NST], F32)
    cnn = _sb(mid, nc, "cnn", [16, T], F32)
    cnhi = _sb(mid, nc, "cnhi", [16, T], BF16)
    cnlo = _sb(mid, nc, "cnlo", [16, T], BF16)
    seln_sb = _sb(mid, nc, "seln_sb", [32, 32 * 128], BF16)

    blk = es.enter_context(nc.Block())

    with ExitStack() as pes:
        selnf = _sb(pes, nc, "selnf", [32, 32 * 128], F32)
        sall = _sb(pes, nc, "sall", [128, 8 * DC * NST], F32)
        ctall = _sb(pes, nc, "ctall", [128, 8 * DC], F32)
        mA = _sb(pes, nc, "mA", [128, 128], F32)
        mB = _sb(pes, nc, "mB", [128, 128], F32)
        pfx = _sb(pes, nc, "pfx", [128, 8 * DC], F32)
        myp = _sb(pes, nc, "myp", [128, DC], F32)
        hj_sb = _sb(pes, nc, "hj_sb", [128, 8 * DC], F32)
        sin = _sb(pes, nc, "sin", [128, DC * NST], F32)
        tj = _sb(pes, nc, "tj", [128, DC], F32)

        @blk.sync
        def _(sync):
            for t_, srcn in [(sall, "s_all"), (ctall, "ctot_all")]:
                qc.inc(sync.dma_start(
                    out=t_[:].rearrange("p (j c) -> p j c", j=8),
                    in_=din[srcn].rearrange("j p c -> p j c")))
            for t_, srcn in [(a_sb, "a_st"), (mA, "maskA"), (mB, "maskB"),
                             (selnf, "seln")]:
                qc.inc(sync.dma_start(out=t_[:], in_=din[srcn]))
            qc.inc(sync.dma_start(out=cnn[:], in_=din["cn"]))

        def edge(v, ins):
            ins.then_inc(sV.h, 1)
            sV.n += 1
            v.wait_ge(sV.h, sV.n)

        @blk.vector
        def _(v):
            for i, val in enumerate([1.0]):
                v.memset(cpool[:, i:i + 1], val)
                nc.const_aps.register(F32, val, cpool[:, i:i + 1])
            qc.wait_all(v)
            v.tensor_copy(out=seln_sb[:], in_=selnf[:])
            edge(v, v.tensor_copy(out=cnhi[:], in_=cnn[:]))
            v.tensor_sub(out=cnlo[:], in0=cnn[:], in1=cnhi[:])
            edge(v, v.memset(pfx[:, 0:DC], 0.0))
            for j in range(1, 8):
                edge(v, v.tensor_add(out=pfx[:, j * DC:(j + 1) * DC],
                                     in0=pfx[:, (j - 1) * DC:j * DC],
                                     in1=ctall[:, (j - 1) * DC:j * DC]))
            edge(v, v.memset(myp[:], 0.0))
            for j in range(8):
                edge(v, v.scalar_tensor_tensor(
                    out=myp[:], in0=ctall[:, j * DC:(j + 1) * DC],
                    scalar=mA[:, j:j + 1], in1=myp[:],
                    op0=ALU.mult, op1=ALU.add))
            for j in range(8):
                sl = slice(j * DC, (j + 1) * DC)
                edge(v, v.tensor_sub(out=tj[:], in0=myp[:], in1=pfx[:, sl]))
                edge(v, v.tensor_sub(out=tj[:], in0=tj[:], in1=ctall[:, sl]))
                edge(v, v.tensor_scalar(out=hj_sb[:, sl], in0=tj[:],
                                        scalar1=mA[:, j:j + 1],
                                        scalar2=mB[:, j:j + 1],
                                        op0=ALU.mult, op1=ALU.add))
            v.memset(sin[:], 0.0).then_inc(sV.h, 1)
        sV.n += 1
        PREP_V = sV.n

        argb = [_sb(pes, nc, f"argb{i}", [128, DC * NST], F32) for i in range(2)]
        wb = [_sb(pes, nc, f"wb{i}", [128, DC * NST], F32) for i in range(2)]
        tb2 = _sb(pes, nc, "tb2", [128, DC * NST], F32)
        for j in range(8):
            pp = j % 2

            @blk.vector
            def _(v, j=j, pp=pp, aat=sA.n):
                if j == 0:
                    pass
                if j >= 2:
                    v.wait_ge(sA.h, aat)
                hb = hj_sb[:, j * DC:(j + 1) * DC].rearrange(
                    "p (c one) -> p c one", one=1).broadcast_to([128, DC, NST])
                v.tensor_tensor(out=argb[pp][:].rearrange("p (c n) -> p c n", n=NST),
                                in0=a_sb[:].rearrange("p (c n) -> p c n", n=NST),
                                in1=hb, op=ALU.mult).then_inc(sV.h, 1)
            sV.n += 1
            arg_v = sV.n

            @blk.scalar
            def _(s, j=j, pp=pp, arg_v=arg_v, vat=sV.n):
                s.wait_ge(sV.h, arg_v)
                s.activation(out=wb[pp][:], in_=argb[pp][:], func=AF.Exp
                             ).then_inc(sA.h, 1)
            sA.n += 1
            w_a = sA.n

            @blk.vector
            def _(v, j=j, pp=pp, w_a=w_a):
                v.wait_ge(sA.h, w_a)
                ins = v.tensor_mul(out=tb2[:], in0=wb[pp][:],
                                   in1=sall[:, j * DC * NST:(j + 1) * DC * NST])
                ins.then_inc(sV.h, 1)
                sV.n += 1
                v.wait_ge(sV.h, sV.n)
                v.tensor_add(out=sin[:], in0=sin[:], in1=tb2[:]).then_inc(sV.h, 1)
                sV.n += 1
                v.wait_ge(sV.h, sV.n)

        SIN_V = sV.n

        @blk.scalar
        def _(s, at=SIN_V):
            s.wait_ge(sV.h, at)
            s.activation(out=lnS[:], in_=sin[:], func=AF.Abs).then_inc(sA.h, 1)
            sA.n += 1
            s.wait_ge(sV.h, sV.n + 1)
            s.activation(out=lnS[:], in_=lnS[:], func=AF.Ln)
            s.activation(out=sgn[:], in_=sin[:], func=AF.Sign).then_inc(sA.h, 1)
            sA.n += 1

        @blk.vector
        def _(v, at=sA.n - 1):
            v.wait_ge(sA.h, at)
            v.tensor_scalar_max(out=lnS[:], in0=lnS[:], scalar1=1e-18
                                ).then_inc(sV.h, 1)
        sV.n += 1
        PREP_A = sA.n
        if DEV_TAPS:
            @blk.sync
            def _(sync, at=SIN_V):
                sync.wait_ge(sV.h, at)
                qtap.inc(sync.dma_start(out=sin_tap, in_=sin[:]))
                qtap.inc(sync.dma_start(out=hj_tap, in_=hj_sb[:]))

    if DEV_TAPS:
        @blk.sync
        def _(sync, at=sA.n, vt=sV.n):
            sync.wait_ge(sA.h, at)
            sync.wait_ge(sV.h, vt)
            qtap.inc(sync.dma_start(out=lns_tap, in_=lnS[:]))
            qtap.inc(sync.dma_start(out=sgn_tap, in_=sgn[:]))

    engine_barrier(blk, sBar, 1)

    with ExitStack() as pes:
        psCB = [_ps(pes, nc, f"psCB{i}", [128, 512]) for i in range(2)]
        ct_g = _sb(pes, nc, "ct_g", [128, GG * T], F32)
        yc_g = _sb(pes, nc, "yc_g", [128, GG * T], F32)
        sgt = [_sb(pes, nc, f"sgt{i}", [128, T], F32) for i in range(2)]
        z1t = [_sb(pes, nc, f"z1t{i}", [128, T], BF16) for i in range(2)]
        cb_s = _sb(pes, nc, "cb_s", [128, T], F32)
        corr = [_sb(pes, nc, f"corr{i}", [128, T], F32) for i in range(2)]
        tmpb = [_sb(pes, nc, f"tmpb{i}", [128, T], F32) for i in range(2)]
        z2b = _sb(pes, nc, "z2b", [128, T], F32)
        z_st = {}

        for g in range(DC // GG):
            yc_marks = {}
            @blk.sync
            def _(sync, g=g, at=sA.n):
                if g > 0:
                    sync.wait_ge(sA.h, at)
                for jj in range(GG):
                    dc = g * GG + jj
                    qt.inc(sync.dma_start(out=ct_g[:, jj * T:(jj + 1) * T],
                                          in_=din["ct"][dc * 128:(dc + 1) * 128, :]))
            ct_l = qt.cnt[0]

            for n in range(NST):
                @blk.tensor
                def _(te, g=g, n=n, aat=sA.n, vat=sV.n, pat=sP.n):
                    if g > 0 or n > 0:
                        te.wait_ge(sA.h, aat)
                        te.wait_ge(sV.h, vat)
                        te.wait_ge(sP.h, pat)
                    te.wait_ge(sV.h, PREP_V)
                    selC = seln_sb[:, n * 128:(n + 1) * 128]
                    for th in range(2):
                        sl = slice(th * 512, (th + 1) * 512)
                        te.matmul(out=psCB[th][:, :], lhsT=selC[0:16, :],
                                  rhs=cnhi[:, sl], start=True, stop=False)
                        mm = te.matmul(out=psCB[th][:, :], lhsT=selC[0:16, :],
                                       rhs=cnlo[:, sl], start=False, stop=True)
                        if th == 1:
                            mm.then_inc(sPE.h, 1)
                sPE.n += 1
                bb_pe = sPE.n

                @blk.scalar
                def _(s, bb_pe=bb_pe):
                    s.wait_ge(sPE.h, bb_pe)
                    for th in range(2):
                        sl = slice(th * 512, (th + 1) * 512)
                        ins = s.copy(out=cb_s[:, sl], in_=psCB[th][:, :])
                        if th == 1:
                            ins.then_inc(sA.h, 1)
                sA.n += 1
                bb_a = sA.n

                for jj in range(GG):
                    dc = g * GG + jj
                    ii = n * GG + jj
                    pp = ii % 2

                    @blk.scalar
                    def _(s, dc=dc, jj=jj, n=n, pp=pp, ii=ii, vat=sV.n, pat=sP.n,
                          ct_l=ct_l):
                        s.wait_ge(qt.sems[0], ct_l)
                        if n == 0 and jj == 0:
                            s.wait_ge(sA.h, PREP_A)
                        if ii >= 2:
                            s.wait_ge(sV.h, vat)
                            s.wait_ge(sP.h, pat)
                        col = dc * NST + n
                        s.activation(out=corr[pp][:],
                                     in_=ct_g[:, jj * T:(jj + 1) * T],
                                     func=AF.Exp, scale=a_sb[:, col:col + 1],
                                     bias=lnS[:, col:col + 1]).then_inc(sA.h, 1)
                    sA.n += 1
                    a_a = sA.n

                    @blk.vector
                    def _(v, dc=dc, n=n, pp=pp, ii=ii, a_a=a_a, bb_a=bb_a,
                          pat=sP.n):
                        v.wait_ge(sA.h, a_a)
                        v.wait_ge(sA.h, bb_a)
                        if ii >= 2:
                            v.wait_ge(sP.h, pat - 1)
                        col = dc * NST + n
                        v.scalar_tensor_tensor(out=tmpb[pp][:], in0=corr[pp][:],
                                               scalar=sgn[:, col:col + 1],
                                               in1=cb_s[:], op0=ALU.mult,
                                               op1=ALU.mult).then_inc(sV.h, 1)
                    sV.n += 1
                    t_v = sV.n

                    @blk.gpsimd
                    def _(gp, jj=jj, n=n, pp=pp, t_v=t_v, ym=yc_marks.get(jj, 0)):
                        gp.wait_ge(sV.h, t_v)
                        if ym:
                            gp.wait_ge(sP.h, ym)
                        if n == 0:
                            gp.tensor_copy(out=yc_g[:, jj * T:(jj + 1) * T],
                                           in_=tmpb[pp][:]).then_inc(sP.h, 1)
                        else:
                            gp.tensor_add(out=yc_g[:, jj * T:(jj + 1) * T],
                                          in0=yc_g[:, jj * T:(jj + 1) * T],
                                          in1=tmpb[pp][:]).then_inc(sP.h, 1)
                    sP.n += 1
                    yc_marks[jj] = sP.n

            for jj in range(GG):
                dc = g * GG + jj
                pp = jj % 2

                @blk.sync
                def _(sync, dc=dc, jj=jj, pp=pp, at=sV.n,
                      zst=z_st.get(g * GG + jj - 2, 0)):
                    if jj >= 2:
                        sync.wait_ge(sV.h, at)
                    if dc >= 2:
                        sync.wait_ge(qz.sems[pp], zst)
                    qe.inc(sync.dma_start(out=sgt[pp][:],
                                          in_=din["sg"][dc * 128:(dc + 1) * 128, :]),
                           pp)
                    qe.inc(sync.dma_start(out=z1t[pp][:],
                                          in_=din["z1"][dc * 128:(dc + 1) * 128, :]),
                           pp)
                ul = qe.cnt[pp]
                yp = sP.n

                @blk.vector
                def _(v, dc=dc, jj=jj, pp=pp, ul=ul, yp=yp):
                    v.wait_ge(qe.sems[pp], ul)
                    v.wait_ge(sP.h, yp)
                    v.tensor_tensor(out=z2b[:], in0=yc_g[:, jj * T:(jj + 1) * T],
                                    in1=sgt[pp][:], op=ALU.mult).then_inc(sV.h, 1)
                    sV.n += 1
                    v.wait_ge(sV.h, sV.n)
                    v.tensor_add(out=z1t[pp][:], in0=z2b[:], in1=z1t[pp][:]
                                 ).then_inc(sV.h, 1)
                sV.n += 1
                z_v = sV.n

                @blk.sync
                def _(sync, dc=dc, pp=pp, z_v=z_v):
                    sync.wait_ge(sV.h, z_v)
                    qz.inc(sync.dma_start(out=z_dram[dc * 128:(dc + 1) * 128, :],
                                          in_=z1t[pp][:]), pp)
                z_st[dc] = qz.cnt[pp]

    engine_barrier(blk, sBar, 2)
    mid.close()

    with ExitStack() as pes:
        psAT = [_ps(pes, nc, f"psAT{i}", [128, 512]) for i in range(2)]
        psF = [_ps(pes, nc, f"psF{i}", [128, 512]) for i in range(2)]
        z_sb = _sb(pes, nc, "z_sb", [128, DC * T], BF16)
        wof = [_sb(pes, nc, f"wof{i}", [128, 1024], F32) for i in range(2)]
        wo = [_sb(pes, nc, f"wo{i}", [128, DC * 128], BF16) for i in range(2)]
        fbf = _sb(pes, nc, "fbf", [128, 8 * 128], F32)
        fbb = [_sb(pes, nc, f"fbb{i}", [128, 8 * 128], BF16) for i in range(2)]
        hgt = [_sb(pes, nc, f"hgt{i}", [128, 8 * 512], BF16) for i in range(2)]
        xres = _sb(pes, nc, "xres", [128, T], F32)
        o1 = _sb(pes, nc, "o1", [128, 512], F32)
        o2 = _sb(pes, nc, "o2", [128, 512], F32)
        out_s = _sb(pes, nc, "out_s", [128, T], F32)

        @blk.sync
        def _(sync):
            qz.wait_all(sync)
            qZ.inc(sync.dma_start(
                out=z_sb[:].rearrange("p (c t) -> p c t", c=DC),
                in_=z_dram.rearrange("(c p) t -> p c t", p=128)))
            for th in range(2):
                qZ.inc(sync.dma_start(
                    out=hgt[th][:].rearrange("p (k c) -> p k c", k=8),
                    in_=din["hg"].rearrange("(k p) t -> p k t", p=128)[
                        :, :, th * 512:(th + 1) * 512]))

        wo_marks = {}
        mm_marks = {}
        st_marks = {}
        o1_mark = [0]
        for hmt in range(32):
            for q in range(8):
                qi = hmt * 8 + q

                @blk.sync
                def _(sync, hmt=hmt, q=q, qi=qi, at=wo_marks.get(qi - 2, 0)):
                    if qi >= 2:
                        sync.wait_ge(sP.h, at)
                    qw.inc(sync.dma_start(
                        out=wof[q % 2][:],
                        in_=din["woT"][hmt][:, q * 1024:(q + 1) * 1024]), q % 2)

                @blk.gpsimd
                def _(g, hmt=hmt, q=q, at=qw.cnt[q % 2],
                      pat=mm_marks.get(hmt - 2, 0)):
                    g.wait_ge(qw.sems[q % 2], at)
                    if hmt >= 2 and q == 0:
                        g.wait_ge(sPE.h, pat)
                    g.tensor_copy(out=wo[hmt % 2][:, q * 1024:(q + 1) * 1024],
                                  in_=wof[q % 2][:]).then_inc(sP.h, 1)
                sP.n += 1
                wo_marks[qi] = sP.n
            wo_p = sP.n

            @blk.sync
            def _(sync, hmt=hmt, pat=mm_marks.get(hmt - 2, 0), vat=sV.n,
                  fat=sP.n):
                if hmt >= 2:
                    sync.wait_ge(sPE.h, pat)
                if hmt >= 1:
                    sync.wait_ge(sV.h, vat)
                    sync.wait_ge(sP.h, fat)
                qf.inc(sync.dma_start(out=fbf[:], in_=din["fbT"][hmt]))
                qf.inc(sync.dma_start(
                    out=xres[:], in_=din["hsT"][hmt * 128:(hmt + 1) * 128, :]))

            @blk.gpsimd
            def _(g, hmt=hmt, at=qf.cnt[0]):
                g.wait_ge(qf.sems[0], at)
                g.tensor_copy(out=fbb[hmt % 2][:], in_=fbf[:]
                              ).then_inc(sP.h, 1)
            sP.n += 1
            fb_p = sP.n

            for th in range(2):
                @blk.tensor
                def _(te, hmt=hmt, th=th, wo_p=wo_p, fb_p=fb_p,
                      aat=sA.n, vat=sV.n):
                    te.wait_ge(sP.h, fb_p)
                    qZ.wait_all(te)
                    if hmt >= 1:
                        te.wait_ge(sV.h, vat)
                    for dc in range(DC):
                        te.matmul(out=psAT[th][:, :],
                                  lhsT=wo[hmt % 2][:, dc * 128:(dc + 1) * 128],
                                  rhs=z_sb[:, dc * T + th * 512:
                                           dc * T + th * 512 + 512],
                                  start=(dc == 0), stop=(dc == DC - 1))
                    for kr in range(8):
                        mm = te.matmul(out=psF[th][:, :],
                                       lhsT=fbb[hmt % 2][:, kr * 128:(kr + 1) * 128],
                                       rhs=hgt[th][:, kr * 512:(kr + 1) * 512],
                                       start=(kr == 0), stop=(kr == 7))
                        if kr == 7:
                            mm.then_inc(sPE.h, 1)
                sPE.n += 1
            mm_marks[hmt] = sPE.n

            for th in range(2):
                mm_pe = sPE.n - 1 + th if False else mm_marks[hmt]

                @blk.scalar
                def _(s, hmt=hmt, th=th, mm_pe=mm_pe, xat=qf.cnt[0],
                      om=o1_mark[0]):
                    s.wait_ge(sPE.h, mm_pe - (1 if th == 0 else 0))
                    s.wait_ge(qf.sems[0], xat)
                    if om:
                        s.wait_ge(sV.h, om)
                    s.activation(out=o1[:], in_=psF[th][:, :], func=AF.Identity,
                                 scale=2.0).then_inc(sA.h, 1)
                sA.n += 1
                ev_a = sA.n

                @blk.vector
                def _(v, hmt=hmt, th=th, ev_a=ev_a, stm=st_marks.get(hmt - 1, 0)):
                    v.wait_ge(sV.h, sV.n)
                    v.wait_ge(sA.h, ev_a)
                    if hmt >= 1 and th == 0:
                        v.wait_ge(qo.sems[0], stm)
                    v.tensor_add(out=o2[:], in0=o1[:], in1=psAT[th][:, :]
                                 ).then_inc(sV.h, 1)
                    sV.n += 1
                    o1_mark[0] = sV.n
                    v.wait_ge(sV.h, sV.n)
                    v.tensor_add(out=out_s[:, th * 512:(th + 1) * 512],
                                 in0=o2[:],
                                 in1=xres[:, th * 512:(th + 1) * 512]
                                 ).then_inc(sV.h, 1)
                sV.n += 1
            o_v = sV.n

            @blk.sync
            def _(sync, hmt=hmt, o_v=o_v):
                sync.wait_ge(sV.h, o_v)
                qo.inc(sync.dma_start(out=out_t[hmt * 128:(hmt + 1) * 128, :],
                                      in_=out_s[:]))
            st_marks[hmt] = qo.cnt[0]

    @blk.sync
    def _(sync):
        qo.wait_all(sync)
        if DEV_TAPS:
            qtap.wait_all(sync)

    es.close()
    return nc



def _stage_l1(inputs):
    x = np.asarray(inputs["L0_attn_hidden_states"], np.float32)[0]
    W_in = np.asarray(inputs["L0_attn_W_in"], np.float32)
    conv_w = np.asarray(inputs["L0_attn_conv_w"], np.float32)
    conv_b = np.asarray(inputs["L0_attn_conv_b"], np.float32)
    Wx = np.asarray(inputs["L0_attn_Wx"], np.float32)
    dt_ln_w = np.asarray(inputs["L0_attn_dt_ln_w"], np.float32)
    b_ln_w = np.asarray(inputs["L0_attn_b_ln_w"], np.float32)
    c_ln_w = np.asarray(inputs["L0_attn_c_ln_w"], np.float32)
    Wdt = np.asarray(inputs["L0_attn_Wdt"], np.float32)
    dt_bias = np.asarray(inputs["L0_attn_dt_bias"], np.float32)
    A = np.asarray(inputs["L0_attn_A"], np.float32)
    D_param = np.asarray(inputs["L0_attn_D_param"], np.float32)
    fA = np.asarray(inputs["L0_ffn_A"], np.float32)
    Wr = np.asarray(inputs["L0_ffn_Wr"], np.float32)

    hsT = np.ascontiguousarray(x.T)
    winT = np.ascontiguousarray(
        W_in.T.reshape(HC, 128, 128, 128).transpose(2, 1, 0, 3)).reshape(
        128, 128, HC * 128)
    wxT = np.ascontiguousarray(Wx.T.reshape(DC, 128, 288))
    wdtT = np.ascontiguousarray(
        Wdt.T.reshape(2, 128, DC, 128).transpose(2, 1, 0, 3)).reshape(DC, 128, 256)
    a_st = np.ascontiguousarray(
        A.reshape(DC, 128, NST).transpose(1, 0, 2)).reshape(128, DC * NST)
    convw = np.ascontiguousarray(
        conv_w.reshape(DC, 128, 4).transpose(1, 0, 2)).reshape(128, DC * 4)
    def pad128(a):
        out = np.zeros((a.shape[0], 128), np.float32)
        out[:, :a.shape[1]] = a
        return out
    convb = pad128(np.ascontiguousarray(conv_b.reshape(DC, 128).T))
    dtb = pad128(np.ascontiguousarray(dt_bias.reshape(DC, 128).T))
    dparam = pad128(np.ascontiguousarray(D_param.reshape(DC, 128).T))
    dtlnw = pad128(np.ascontiguousarray(dt_ln_w.reshape(2, 128).T))
    bclnw = pad128(np.concatenate([b_ln_w, c_ln_w]).reshape(32, 1).astype(np.float32))
    bc_ones = np.zeros((32, 128), np.float32)
    bc_ones[0:16, 0] = 1.0
    bc_ones[16:32, 1] = 1.0
    bc_exp = np.zeros((2, 128), np.float32)
    bc_exp[0, 0:16] = 1.0
    bc_exp[1, 16:32] = 1.0
    eye128 = np.eye(128, dtype=np.float32)
    wr_st = np.ascontiguousarray(
        Wr.T.reshape(HC, 128, 64).transpose(1, 0, 2)).reshape(128, HC * 64)
    fAT = np.ascontiguousarray(fA.reshape(1024, H).T)
    ffnA = np.ascontiguousarray(
        fAT.reshape(HC, 128, 8, 128).transpose(2, 1, 0, 3)).reshape(
        8, 128, HC * 128)
    exp16 = np.zeros((8, 64, 128), np.float32)
    for c in range(8):
        for m in range(128):
            exp16[c, (c * 128 + m) // 16, m] = 1.0
    seln = np.zeros((32, 32 * 128), np.float32)
    for n in range(32):
        seln[n, n * 128:(n + 1) * 128] = 1.0

    shared = dict(winT=winT, wxT=wxT, wdtT=wdtT, a_st=a_st, convw=convw,
                  convb=convb, dtb=dtb, dparam=dparam, dtlnw=dtlnw, bclnw=bclnw,
                  bc_ones=bc_ones, bc_exp=bc_exp, eye128=eye128, wr_st=wr_st,
                  ffnA=ffnA, exp16=exp16, seln=seln)
    in_maps = []
    for c in range(NCORES):
        c0 = c * T
        halo = (np.zeros((H, 3), np.float32) if c == 0
                else np.ascontiguousarray(hsT[:, c0 - 3:c0]))
        m = dict(shared)
        m["hsT"] = np.ascontiguousarray(
            np.concatenate([halo, hsT[:, c0:c0 + T]], axis=1))
        in_maps.append(m)
    return in_maps


def _stage_l2(inputs, res1):
    x = np.asarray(inputs["L0_attn_hidden_states"], np.float32)[0]
    W_out = np.asarray(inputs["L0_attn_W_out"], np.float32)
    A = np.asarray(inputs["L0_attn_A"], np.float32)
    fB = np.asarray(inputs["L0_ffn_B"], np.float32)

    hsT = np.ascontiguousarray(x.T)
    a_st = np.ascontiguousarray(
        A.reshape(DC, 128, NST).transpose(1, 0, 2)).reshape(128, DC * NST)
    seln = np.zeros((32, 32 * 128), np.float32)
    for n in range(32):
        seln[n, n * 128:(n + 1) * 128] = 1.0
    woT = np.ascontiguousarray(
        W_out.T.reshape(DC, 128, 32, 128).transpose(2, 1, 0, 3)).reshape(
        32, 128, DC * 128)
    fbT = np.ascontiguousarray(
        fB.transpose(0, 2, 1).reshape(8, 128, 32, 128).transpose(2, 1, 0, 3)
    ).reshape(32, 128, 8 * 128)
    s_all = np.stack([np.asarray(res1.results[c]["s_out"], np.float32)
                      for c in range(NCORES)])
    ctot_all = np.stack([np.asarray(res1.results[c]["ctot_out"], np.float32)
                         for c in range(NCORES)])
    in_maps = []
    for c in range(NCORES):
        mA = np.zeros((128, 128), np.float32)
        mB = np.full((128, 128), 1.0e5, np.float32)
        mA[:, 0:c] = 1.0
        mB[:, 0:c] = 0.0
        r = res1.results[c]
        m = dict(
            z1=np.asarray(r["z1"]), sg=np.asarray(r["sg_out"], np.float32),
            ct=np.asarray(r["ct_out"], np.float32),
            cn=np.asarray(r["cn_out"], np.float32), hg=np.asarray(r["hg_out"]),
            s_all=s_all, ctot_all=ctot_all, a_st=a_st, maskA=mA, maskB=mB,
            seln=seln, woT=woT, fbT=fbT,
            hsT=np.ascontiguousarray(hsT[:, c * T:(c + 1) * T]),
        )
        in_maps.append(m)
    return in_maps


def kernel(**inputs):
    in_maps1 = _stage_l1(inputs)
    nc1 = build_l1()
    res1 = run_bass_kernel_spmd(nc1, in_maps1, core_ids=list(range(NCORES)))
    in_maps2 = _stage_l2(inputs, res1)
    nc2 = build_l2()
    res2 = run_bass_kernel_spmd(nc2, in_maps2, core_ids=list(range(NCORES)))
    out = np.concatenate(
        [np.asarray(res2.results[c]["out"], np.float32).T for c in range(NCORES)],
        axis=0)
    return out.reshape(1, 8192, H)


if __name__ == "__main__":
    pass


# version 02
# speedup vs baseline: 1.1973x; 1.1973x over previous
import numpy as np
from contextlib import ExitStack

import concourse.bass as bass
import concourse.mybir as mybir
from concourse.bass_utils import run_bass_kernel_spmd

F32 = mybir.dt.float32
BF16 = mybir.dt.bfloat16
AF = mybir.ActivationFunctionType
ALU = mybir.AluOpType
AX = mybir.AxisListType

NCORES = 8
H = 4096
DD = 8192
NST = 16
RR = 256
T = 1024
TH = T + 3
HC = H // 128
DC = DD // 128
GG = 8
NEG = -1.0e30

DEV_TAPS = False


def _sb(es, nc, name, shape, dt):
    return es.enter_context(nc.sbuf_tensor(name, shape, dt))


def _ps(es, nc, name, shape, dt=F32):
    return es.enter_context(nc.psum_tensor(name, shape, dt))


class Ctr:
    def __init__(self, es, nc, name):
        self.h = es.enter_context(nc.semaphore(name))
        self.n = 0


class DRing:

    def __init__(self, es, nc, name, slots):
        self.sems = [es.enter_context(nc.semaphore(f"{name}{i}"))
                     for i in range(slots)]
        self.cnt = [0] * slots

    def inc(self, instr, slot=0, k=16):
        instr.then_inc(self.sems[slot], k)
        self.cnt[slot] += k
        return self.cnt[slot]

    def wait(self, eng, slot=0, val=None):
        eng.wait_ge(self.sems[slot], self.cnt[slot] if val is None else val)

    def wait_all(self, eng):
        for i, s in enumerate(self.sems):
            eng.wait_ge(s, self.cnt[i])


def load_tbl(eng, set_id):
    eng.add_instruction(mybir.InstLoadActFuncSet(
        name=eng.bass.get_next_instruction_name(),
        act_func_set_id=set_id, ins=[], outs=[]))


def engine_barrier(blk, bar, k):
    for name in ["sync", "tensor", "scalar", "vector", "gpsimd"]:
        dec = getattr(blk, name)

        def _f(eng):
            eng.nop().then_inc(bar.h, 1)
        dec(_f)
    bar.n += 5
    tgt = bar.n
    for name in ["sync", "tensor", "scalar", "vector", "gpsimd"]:
        dec = getattr(blk, name)

        def _g(eng, tgt=tgt):
            eng.wait_ge(bar.h, tgt)
        dec(_g)



def build_l1():
    nc = bass.Bass("TRN2", target_bir_lowering=False)
    din = {}
    for name, shape in [
        ("hsT", [H, TH]),
        ("winT", [128, 128, HC * 128]),
        ("wxT", [DC, 128, 288]),
        ("wdtT", [DC, 128, 256]),
        ("a_st", [128, DC * NST]),
        ("convw", [128, DC * 4]),
        ("convb", [128, 128]),
        ("dtb", [128, 128]),
        ("dparam", [128, 128]),
        ("dtlnw", [128, 128]),
        ("bclnw", [32, 128]),
        ("bc_ones", [32, 128]),
        ("bc_exp", [2, 128]),
        ("eye128", [128, 128]),
        ("wr_st", [128, HC * 64]),
        ("ffnA", [8, 128, HC * 128]),
        ("exp16", [8, 64, 128]),
        ("seln", [32, 32 * 128]),
    ]:
        din[name] = nc.dram_tensor(name, shape, F32, kind="ExternalInput").ap()

    dout = {}
    for name, shape, dt in [
        ("z1", [DD, T], BF16),
        ("sg_out", [DD, T], F32),
        ("ct_out", [DD, T], F32),
        ("s_out", [128, DC * NST], F32),
        ("ctot_out", [128, DC], F32),
        ("cn_out", [16, T], F32),
        ("hg_out", [1024, T], BF16),
    ]:
        dout[name] = nc.dram_tensor(name, shape, dt, kind="ExternalOutput").ap()
    if DEV_TAPS:
        for name, shape, dt in [
            ("xn_tap", [128, HC * TH], BF16),
            ("u_tap", [DD, T], F32),
            ("ssm_tap", [288, T], F32),
            ("dtn_tap", [RR, T], BF16),
            ("bcn_tap", [32, T], F32),
            ("g_tap", [T, 64], F32),
            ("dl_tap", [DD, T], F32),
        ]:
            dout[name] = nc.dram_tensor(name, shape, dt, kind="ExternalOutput").ap()

    u_dram = nc.dram_tensor("u_dram", [DD, T], F32, kind="Internal").ap()
    sg_dram = nc.dram_tensor("sg_dram", [DD, T], F32, kind="Internal").ap()

    es = ExitStack()
    dtn0 = _sb(es, nc, "dtn0", [128, T], BF16)
    dtn1 = _sb(es, nc, "dtn1", [128, T], BF16)
    bcn = _sb(es, nc, "bcn", [32, T], F32)
    bchi = _sb(es, nc, "bchi", [32, T], BF16)
    bclo = _sb(es, nc, "bclo", [32, T], BF16)
    seln_sb = _sb(es, nc, "seln_sb", [32, 32 * 128], BF16)
    a_sb = _sb(es, nc, "a_sb", [128, DC * NST], F32)
    convw_sb = _sb(es, nc, "convw_sb", [128, DC * 4], F32)
    convb_sb = _sb(es, nc, "convb_sb", [128, 128], F32)
    dtb_sb = _sb(es, nc, "dtb_sb", [128, 128], F32)
    dparam_sb = _sb(es, nc, "dparam_sb", [128, 128], F32)
    dtlnw_sb = _sb(es, nc, "dtlnw_sb", [128, 128], F32)
    bclnw_sb = _sb(es, nc, "bclnw_sb", [32, 128], F32)
    bco_sb = _sb(es, nc, "bco_sb", [32, 2], BF16)
    bce_sb = _sb(es, nc, "bce_sb", [2, 32], BF16)
    eye_sb = _sb(es, nc, "eye_sb", [128, 128], F32)
    ones_col = _sb(es, nc, "ones_col", [128, 1], BF16)
    ones_row = _sb(es, nc, "ones_row", [1, 128], BF16)
    ctot_sb = _sb(es, nc, "ctot_sb", [128, DC], F32)
    s_cap = _sb(es, nc, "s_cap", [128, DC * NST], F32)
    cpool = _sb(es, nc, "cpool", [128, 4], F32)

    sPE = Ctr(es, nc, "sPE")
    sA = Ctr(es, nc, "sA")
    sV = Ctr(es, nc, "sV")
    sP = Ctr(es, nc, "sP")
    rc = DRing(es, nc, "rc", 1)
    rx = DRing(es, nc, "rx", 2)
    rw = DRing(es, nc, "rw", 2)
    ru = DRing(es, nc, "ru", 2)
    rs = DRing(es, nc, "rs", 2)
    rt = DRing(es, nc, "rt", 2)
    rd = DRing(es, nc, "rd", 2)
    rct = DRing(es, nc, "rct", 2)
    rzr = DRing(es, nc, "rz", 2)
    rf = DRing(es, nc, "rf", 2)
    rfw = DRing(es, nc, "rfw", 1)
    rh = DRing(es, nc, "rh", 2)
    rfin = DRing(es, nc, "rfin", 1)
    rtap = DRing(es, nc, "rtap", 1)
    sBar = Ctr(es, nc, "sBar")

    xn_es = ExitStack()
    xn = _sb(xn_es, nc, "xn", [128, HC * TH], BF16)
    ssm_es = ExitStack()
    ssm0 = _sb(ssm_es, nc, "ssm0", [128, T], F32)
    ssm1 = _sb(ssm_es, nc, "ssm1", [128, T], F32)
    ssmBC = _sb(ssm_es, nc, "ssmBC", [32, T], F32)

    blk = es.enter_context(nc.Block())

    cst_es = ExitStack()
    bcof = _sb(cst_es, nc, "bcof", [32, 128], F32)
    bcef = _sb(cst_es, nc, "bcef", [2, 128], F32)
    selnf = _sb(cst_es, nc, "selnf", [32, 32 * 128], F32)

    @blk.sync
    def _(sync):
        for t_, src in [
            (a_sb, din["a_st"]), (convw_sb, din["convw"]), (convb_sb, din["convb"]),
            (dtb_sb, din["dtb"]), (dparam_sb, din["dparam"]), (dtlnw_sb, din["dtlnw"]),
            (bclnw_sb, din["bclnw"]), (eye_sb, din["eye128"]),
            (bcof, din["bc_ones"]), (bcef, din["bc_exp"]), (selnf, din["seln"]),
        ]:
            rc.inc(sync.dma_start(out=t_[:], in_=src))

    for eng_name in ["tensor", "scalar", "vector", "gpsimd"]:
        getattr(blk, eng_name)(lambda eng: rc.wait_all(eng))

    @blk.vector
    def _(v):
        v.tensor_copy(out=bco_sb[:], in_=bcof[:, 0:2])
        v.tensor_copy(out=bce_sb[:], in_=bcef[:, 0:32])
        v.tensor_copy(out=seln_sb[:], in_=selnf[:])
        for i, val in enumerate([1e-5, 1e-6, 1.0]):
            v.memset(cpool[:, i:i + 1], val)
            nc.const_aps.register(F32, val, cpool[:, i:i + 1])
        v.memset(ones_col[:], 1.0)
        v.memset(ones_row[:], 1.0).then_inc(sV.h, 1)
        sV.n += 1

    CONST_V = sV.n


    @blk.scalar
    def _(s):
        load_tbl(s, 6)
    LNG = [(0, 512), (512, 512), (899, 128)]
    with ExitStack() as pes:
        psA_ = _ps(pes, nc, "psA", [1, 512])
        psB_ = _ps(pes, nc, "psB", [1, 512])
        psMU = _ps(pes, nc, "psMU", [128, 512])
        psRS = _ps(pes, nc, "psRS", [128, 512])
        xbuf = [_sb(pes, nc, f"xbuf{i}", [128, 512], F32) for i in range(2)]
        xbf = [_sb(pes, nc, f"xbf{i}", [128, 512], BF16) for i in range(2)]
        x2bf = [_sb(pes, nc, f"x2bf{i}", [128, 512], BF16) for i in range(2)]
        murow = _sb(pes, nc, "murow", [1, 512], F32)
        m2row = _sb(pes, nc, "m2row", [1, 512], F32)
        musq = _sb(pes, nc, "musq", [1, 512], F32)
        varrow = _sb(pes, nc, "varrow", [1, 512], F32)
        lnrow = _sb(pes, nc, "lnrow", [1, 512], F32)
        rstdrow = _sb(pes, nc, "rstdrow", [1, 512], F32)
        mu_hi = _sb(pes, nc, "mu_hi", [1, 512], BF16)
        mu_lo = _sb(pes, nc, "mu_lo", [1, 512], BF16)
        rs_hi = _sb(pes, nc, "rs_hi", [1, 512], BF16)
        rs_lo = _sb(pes, nc, "rs_lo", [1, 512], BF16)
        t1b = [_sb(pes, nc, f"t1b{i}", [128, 512], F32) for i in range(2)]

        for (c0, gw) in LNG:
            gV, gA, gPE = sV.n, sA.n, sPE.n
            src_g = din["hsT"][:, c0:c0 + gw].rearrange("(hc p) c -> hc p c", p=128)
            xv = []

            @blk.sync
            def _(sync, gV=gV, gA=gA, src_g=src_g, gw=gw, xv=xv):
                for k in range(HC):
                    if k >= 2:
                        sync.wait_ge(sV.h, gV + (k - 1))
                        sync.wait_ge(sA.h, gA + (k - 1))
                    else:
                        sync.wait_ge(sV.h, gV)
                    xv.append(rx.inc(sync.dma_start(out=xbuf[k % 2][:, :gw],
                                                    in_=src_g[k]), k % 2))

            @blk.vector
            def _(v, gPE=gPE, gw=gw, xv=xv):
                for k in range(HC):
                    v.wait_ge(rx.sems[k % 2], xv[k])
                    if k >= 2:
                        v.wait_ge(sPE.h, gPE + (k - 1))
                    else:
                        v.wait_ge(sPE.h, gPE)
                    v.tensor_copy(out=xbf[k % 2][:, :gw], in_=xbuf[k % 2][:, :gw]
                                  ).then_inc(sV.h, 1)
            sV.n += HC

            @blk.scalar
            def _(s, gPE=gPE, gw=gw, xv=xv):
                s.wait_ge(sA.h, sA.n)
                for k in range(HC):
                    s.wait_ge(rx.sems[k % 2], xv[k])
                    if k >= 2:
                        s.wait_ge(sPE.h, gPE + (k - 1))
                    else:
                        s.wait_ge(sPE.h, gPE)
                    s.activation(out=x2bf[k % 2][:, :gw], in_=xbuf[k % 2][:, :gw],
                                 func=AF.Square).then_inc(sA.h, 1)
            sA.n += HC

            @blk.tensor
            def _(te, gV=gV, gA=gA, gw=gw):
                te.wait_ge(sV.h, CONST_V)
                for k in range(HC):
                    te.wait_ge(sV.h, gV + (k + 1))
                    te.wait_ge(sA.h, gA + (k + 1))
                    te.matmul(out=psA_[:, :gw], lhsT=ones_col[:],
                              rhs=xbf[k % 2][:, :gw],
                              start=(k == 0), stop=(k == HC - 1))
                    te.matmul(out=psB_[:, :gw], lhsT=ones_col[:],
                              rhs=x2bf[k % 2][:, :gw],
                              start=(k == 0), stop=(k == HC - 1)).then_inc(sPE.h, 1)
            sPE.n += HC

            eV, eA, ePE = sV.n, sA.n, sPE.n

            @blk.vector
            def _(v, gw=gw, ePE=ePE, eA=eA):
                def edge(ins):
                    ins.then_inc(sV.h, 1)
                    sV.n += 1
                    v.wait_ge(sV.h, sV.n)
                v.wait_ge(sPE.h, ePE)
                edge(v.tensor_scalar_mul(out=murow[:, :gw], in0=psA_[:, :gw],
                                         scalar1=1.0 / H))
                edge(v.tensor_scalar_mul(out=m2row[:, :gw], in0=psB_[:, :gw],
                                         scalar1=1.0 / H))
                edge(v.tensor_mul(out=musq[:, :gw], in0=murow[:, :gw],
                                  in1=murow[:, :gw]))
                v.tensor_sub(out=varrow[:, :gw], in0=m2row[:, :gw],
                             in1=musq[:, :gw]).then_inc(sV.h, 1)
                sV.n += 1
                v.wait_ge(sA.h, eA + 2)
                edge(v.tensor_copy(out=mu_hi[:, :gw], in_=murow[:, :gw]))
                edge(v.tensor_copy(out=rs_hi[:, :gw], in_=rstdrow[:, :gw]))
                v.tensor_sub(out=mu_lo[:, :gw], in0=murow[:, :gw], in1=mu_hi[:, :gw])
                v.tensor_sub(out=rs_lo[:, :gw], in0=rstdrow[:, :gw],
                             in1=rs_hi[:, :gw]).then_inc(sV.h, 1)
                sV.n += 1
            pass

            @blk.scalar
            def _(s, gw=gw, eV=eV):
                s.wait_ge(sV.h, eV + 4)
                s.activation(out=lnrow[:, :gw], in_=varrow[:, :gw], func=AF.Ln,
                             bias=1e-5).then_inc(sA.h, 1)
                sA.n += 1
                s.wait_ge(sA.h, sA.n)
                s.activation(out=rstdrow[:, :gw], in_=lnrow[:, :gw], func=AF.Exp,
                             scale=-0.5).then_inc(sA.h, 1)
            sA.n += 1

            @blk.tensor
            def _(te, gw=gw, eV=eV):
                te.wait_ge(sV.h, eV + 7)
                te.matmul(out=psMU[:, :gw], lhsT=ones_row[:], rhs=mu_hi[:, :gw],
                          start=True, stop=False)
                te.matmul(out=psMU[:, :gw], lhsT=ones_row[:], rhs=mu_lo[:, :gw],
                          start=False, stop=True)
                te.matmul(out=psRS[:, :gw], lhsT=ones_row[:], rhs=rs_hi[:, :gw],
                          start=True, stop=False)
                te.matmul(out=psRS[:, :gw], lhsT=ones_row[:], rhs=rs_lo[:, :gw],
                          start=False, stop=True).then_inc(sPE.h, 1)
            sPE.n += 1

            p2V, p2PE = sV.n, sPE.n
            xv2 = []

            @blk.sync
            def _(sync, p2V=p2V, p2PE=p2PE, src_g=src_g, gw=gw, xv2=xv2):
                sync.wait_ge(sPE.h, p2PE)
                for k in range(HC):
                    if k >= 2:
                        sync.wait_ge(sV.h, p2V + 2 * (k - 1))
                    xv2.append(rx.inc(sync.dma_start(out=xbuf[k % 2][:, :gw],
                                                     in_=src_g[k]), k % 2))

            @blk.vector
            def _(v, c0=c0, gw=gw, p2PE=p2PE, xv2=xv2):
                v.wait_ge(sPE.h, p2PE)
                for k in range(HC):
                    v.wait_ge(rx.sems[k % 2], xv2[k])
                    v.tensor_sub(out=t1b[k % 2][:, :gw], in0=xbuf[k % 2][:, :gw],
                                 in1=psMU[:, :gw]).then_inc(sV.h, 1)
                    sV.n += 1
                    v.wait_ge(sV.h, sV.n)
                    v.tensor_mul(out=xn[:, k * TH + c0:k * TH + c0 + gw],
                                 in0=t1b[k % 2][:, :gw], in1=psRS[:, :gw]
                                 ).then_inc(sV.h, 1)
                    sV.n += 1
            pass

    LN_V = sV.n
    cst_es.close()
    engine_barrier(blk, sBar, 0)

    if DEV_TAPS:
        @blk.sync
        def _(sync):
            sync.wait_ge(sV.h, LN_V)
            rtap.inc(sync.dma_start(out=dout["xn_tap"], in_=xn[:]))


    @blk.scalar
    def _(s):
        load_tbl(s, 18)
    with ExitStack() as pes:
        psP = [_ps(pes, nc, f"psP{i}", [128, 512]) for i in range(2)]
        psS0 = [_ps(pes, nc, f"psS0{i}", [128, 512]) for i in range(2)]
        psS1 = [_ps(pes, nc, f"psS1{i}", [128, 512]) for i in range(2)]
        psS2 = [_ps(pes, nc, f"psS2{i}", [32, 512]) for i in range(2)]
        wch = [_sb(pes, nc, f"wch{i}", [128, HC * 128], BF16) for i in range(2)]
        wxch = [_sb(pes, nc, f"wxch{i}", [128, 288], BF16) for i in range(2)]
        wchf = [_sb(pes, nc, f"wchf{i}", [128, HC * 128], F32) for i in range(2)]
        wxchf = [_sb(pes, nc, f"wxchf{i}", [128, 288], F32) for i in range(2)]
        upre = _sb(pes, nc, "upre", [128, TH], F32)
        conv1 = _sb(pes, nc, "conv1", [128, T], F32)
        ubuf = [_sb(pes, nc, f"ubuf{i}", [128, T], F32) for i in range(2)]
        ubf = [_sb(pes, nc, f"ubf{i}", [128, T], BF16) for i in range(2)]
        sgb = [_sb(pes, nc, f"sgb{i}", [128, T], F32) for i in range(2)]
        @blk.vector
        def _(v):
            v.memset(ssm0[:], 0.0)
            v.memset(ssm1[:], 0.0)
            v.memset(ssmBC[:], 0.0).then_inc(sV.h, 1)
        sV.n += 1

        MEMSET_V = sV.n
        pe_chunk = []
        evac_marks = {}
        silu_marks = {}
        store_marks = {}
        conv_marks = {}
        wx_marks = {}
        cast_marks = {}
        ssm_add_marks = []

        for j in range(128):
            is_u = j < 64
            jj = j if is_u else j - 64

            @blk.sync
            def _(sync, j=j, is_u=is_u):
                if j >= 2:
                    sync.wait_ge(sP.h, cast_marks[j - 2])
                rw.inc(sync.dma_start(out=wchf[j % 2][:], in_=din["winT"][j]),
                       j % 2)
                if is_u:
                    rw.inc(sync.dma_start(out=wxchf[j % 2][:], in_=din["wxT"][j]),
                           j % 2)
            wLd = rw.cnt[j % 2]

            @blk.gpsimd
            def _(g, j=j, is_u=is_u, wLd=wLd):
                g.wait_ge(rw.sems[j % 2], wLd)
                if j >= 2:
                    g.wait_ge(sPE.h, pe_chunk[j - 2])
                g.tensor_copy(out=wch[j % 2][:], in_=wchf[j % 2][:]).then_inc(sP.h, 1)
                sP.n += 1
                if is_u:
                    g.tensor_copy(out=wxch[j % 2][:], in_=wxchf[j % 2][:]
                                  ).then_inc(sP.h, 1)
                    sP.n += 1
            cast_marks[j] = sP.n
            wG = sP.n

            if is_u:
                groups = [(0, 512, 0), (512, 512, 1), (899, 128, 0)]
            else:
                groups = [(3, 512, 0), (515, 512, 1)]

            g_pe_marks = []
            for gi, (c0, gw, pb) in enumerate(groups):
                @blk.tensor
                def _(te, j=j, gi=gi, c0=c0, gw=gw, pb=pb, wG=wG):
                    if j == 0 and gi == 0:
                        te.wait_ge(sV.h, MEMSET_V)
                    te.wait_ge(sP.h, wG)
                    if gi >= 2:
                        te.wait_ge(sA.h, evac_marks[(j, 0)])
                    elif j >= 1:
                        prev_groups = 3 if j - 1 < 64 else 2
                        for pgi in range(prev_groups - 1, -1, -1):
                            ppb = [0, 1, 0][pgi] if j - 1 < 64 else [0, 1][pgi]
                            if ppb == pb:
                                te.wait_ge(sA.h, evac_marks[(j - 1, pgi)])
                                break
                    for hc in range(HC):
                        mm = te.matmul(out=psP[pb][:, :gw],
                                       lhsT=wch[j % 2][:, hc * 128:(hc + 1) * 128],
                                       rhs=xn[:, hc * TH + c0:hc * TH + c0 + gw],
                                       start=(hc == 0), stop=(hc == HC - 1))
                        if hc == HC - 1:
                            mm.then_inc(sPE.h, 1)
                sPE.n += 1
                g_pe_marks.append(sPE.n)

                if is_u:
                    dsts = [(0, 512), (512, 512), (899, 128)]
                    d0, dw = dsts[gi]

                    @blk.scalar
                    def _(s, j=j, gi=gi, pb=pb, d0=d0, dw=dw, mark=sPE.n):
                        s.wait_ge(sA.h, sA.n)
                        s.wait_ge(sPE.h, mark)
                        if gi == 0 and j >= 1:
                            s.wait_ge(sV.h, conv_marks[j - 1])
                        s.copy(out=upre[:, d0:d0 + dw], in_=psP[pb][:, :dw]
                               ).then_inc(sA.h, 1)
                    sA.n += 1
                    evac_marks[(j, gi)] = sA.n
                else:
                    dsts = [(0, 512), (512, 512)]
                    d0, dw = dsts[gi]

                    @blk.scalar
                    def _(s, j=j, jj=jj, gi=gi, pb=pb, d0=d0, dw=dw, mark=sPE.n):
                        s.wait_ge(sA.h, sA.n)
                        s.wait_ge(sPE.h, mark)
                        if gi == 0 and jj >= 2:
                            s.wait_ge(rs.sems[jj % 2], store_marks[64 + jj - 2])
                        s.activation(out=sgb[jj % 2][:, d0:d0 + dw],
                                     in_=psP[pb][:, :dw], func=AF.Silu
                                     ).then_inc(sA.h, 1)
                    sA.n += 1
                    evac_marks[(j, gi)] = sA.n

            pe_chunk.append(sPE.n)

            if is_u:
                up_done = evac_marks[(j, 2)]

                @blk.vector
                def _(v, jj=jj, up_done=up_done):
                    v.wait_ge(sV.h, sV.n)
                    v.wait_ge(sA.h, up_done)
                    if jj >= 1:
                        v.wait_ge(sA.h, silu_marks[jj - 1])
                    v.tensor_scalar(out=conv1[:], in0=upre[:, 0:T],
                                    scalar1=convw_sb[:, 4 * jj:4 * jj + 1],
                                    scalar2=None, op0=ALU.mult).then_inc(sV.h, 1)
                    sV.n += 1
                    for kk in range(1, 4):
                        v.wait_ge(sV.h, sV.n)
                        v.scalar_tensor_tensor(
                            out=conv1[:], in0=upre[:, kk:T + kk],
                            scalar=convw_sb[:, 4 * jj + kk:4 * jj + kk + 1],
                            in1=conv1[:], op0=ALU.mult, op1=ALU.add
                        ).then_inc(sV.h, 1)
                        sV.n += 1
                pass
                conv_done = sV.n
                conv_marks[jj] = conv_done

                @blk.scalar
                def _(s, jj=jj, conv_done=conv_done):
                    s.wait_ge(sA.h, sA.n)
                    s.wait_ge(sV.h, conv_done)
                    if jj >= 2:
                        s.wait_ge(ru.sems[jj % 2], store_marks[jj - 2])
                    s.activation(out=ubuf[jj % 2][:], in_=conv1[:], func=AF.Silu,
                                 bias=convb_sb[:, jj:jj + 1]).then_inc(sA.h, 1)
                sA.n += 1
                silu_marks[jj] = sA.n
                u_done = sA.n

                @blk.vector
                def _(v, jj=jj, u_done=u_done):
                    v.wait_ge(sV.h, sV.n)
                    v.wait_ge(sA.h, u_done)
                    if jj >= 2:
                        v.wait_ge(sPE.h, wx_marks[jj - 2])
                    v.tensor_copy(out=ubf[jj % 2][:], in_=ubuf[jj % 2][:]
                                  ).then_inc(sV.h, 1)
                sV.n += 1
                ubf_done = sV.n

                @blk.tensor
                def _(te, j=j, jj=jj, ubf_done=ubf_done):
                    te.wait_ge(sV.h, ubf_done)
                    st = (jj % 16 == 0)
                    sp = (jj % 16 == 15)
                    for th in range(2):
                        rhs = ubf[jj % 2][:, th * 512:(th + 1) * 512]
                        te.matmul(out=psS0[th][:, :], lhsT=wxch[j % 2][:, 0:128],
                                  rhs=rhs, start=st, stop=sp)
                        te.matmul(out=psS1[th][:, :], lhsT=wxch[j % 2][:, 128:256],
                                  rhs=rhs, start=st, stop=sp)
                        mm = te.matmul(out=psS2[th][:, :],
                                       lhsT=wxch[j % 2][:, 256:288],
                                       rhs=rhs, start=st, stop=sp)
                        if th == 1:
                            mm.then_inc(sPE.h, 1)
                sPE.n += 1
                wx_pe = sPE.n
                wx_marks[jj] = sPE.n
                pe_chunk[-1] = sPE.n

                if jj % 16 == 15:
                    @blk.vector
                    def _(v, wx_pe=wx_pe, prevm=ssm_add_marks[-1] if ssm_add_marks
                          else 0):
                        v.wait_ge(sPE.h, wx_pe)
                        if prevm:
                            v.wait_ge(sV.h, prevm)
                        for th in range(2):
                            sl = slice(th * 512, (th + 1) * 512)
                            v.tensor_add(out=ssm0[:, sl], in0=ssm0[:, sl],
                                         in1=psS0[th][:, :])
                            v.tensor_add(out=ssm1[:, sl], in0=ssm1[:, sl],
                                         in1=psS1[th][:, :])
                            ins = v.tensor_add(out=ssmBC[:, sl], in0=ssmBC[:, sl],
                                               in1=psS2[th][:, :])
                            if th == 1:
                                ins.then_inc(sV.h, 1)
                    sV.n += 1
                    ssm_add_marks.append(sV.n)

                @blk.sync
                def _(sync, jj=jj, u_done=u_done, ubf_done=ubf_done):
                    sync.wait_ge(sA.h, u_done)
                    sync.wait_ge(sV.h, ubf_done)
                    ru.inc(sync.dma_start(out=u_dram[jj * 128:(jj + 1) * 128, :],
                                          in_=ubuf[jj % 2][:]), jj % 2)
                    if DEV_TAPS:
                        ru.inc(sync.dma_start(
                            out=dout["u_tap"][jj * 128:(jj + 1) * 128, :],
                            in_=ubuf[jj % 2][:]), jj % 2)
                store_marks[jj] = ru.cnt[jj % 2]
            else:
                sg_done = evac_marks[(j, 1)]

                @blk.sync
                def _(sync, jj=jj, sg_done=sg_done):
                    sync.wait_ge(sA.h, sg_done)
                    rs.inc(sync.dma_start(out=sg_dram[jj * 128:(jj + 1) * 128, :],
                                          in_=sgb[jj % 2][:]), jj % 2)
                    rs.inc(sync.dma_start(
                        out=dout["sg_out"][jj * 128:(jj + 1) * 128, :],
                        in_=sgb[jj % 2][:]), jj % 2)
                store_marks[64 + jj] = rs.cnt[jj % 2]

    PROJ_V, PROJ_A, PROJ_PE = sV.n, sA.n, sPE.n

    if DEV_TAPS:
        @blk.sync
        def _(sync):
            sync.wait_ge(sV.h, PROJ_V)
            rtap.inc(sync.dma_start(out=dout["ssm_tap"][0:128, :], in_=ssm0[:]))
            rtap.inc(sync.dma_start(out=dout["ssm_tap"][128:256, :], in_=ssm1[:]))
            rtap.inc(sync.dma_start(out=dout["ssm_tap"][256:288, :], in_=ssmBC[:]))

    @blk.sync
    def _(sync):
        ru.wait_all(sync)
        rs.wait_all(sync)
        if DEV_TAPS:
            rtap.wait_all(sync)

    engine_barrier(blk, sBar, 0)


    @blk.scalar
    def _(s):
        load_tbl(s, 6)
    with ExitStack() as pes:
        psq = [_ps(pes, nc, f"psq{i}", [1, 512]) for i in range(2)]
        psq2 = [_ps(pes, nc, f"psq2{i}", [2, 512]) for i in range(2)]
        psRD = [_ps(pes, nc, f"psRD{i}", [128, 512]) for i in range(2)]
        psRB = [_ps(pes, nc, f"psRB{i}", [32, 512]) for i in range(2)]
        dsq0 = _sb(pes, nc, "dsq0", [128, T], BF16)
        dsq1 = _sb(pes, nc, "dsq1", [128, T], BF16)
        bsq = _sb(pes, nc, "bsq", [32, T], BF16)
        r_dt = _sb(pes, nc, "r_dt", [1, T], F32)
        r_bc = _sb(pes, nc, "r_bc", [2, T], F32)
        lnt = _sb(pes, nc, "lnt", [2, T], F32)
        rdt_hi = _sb(pes, nc, "rdt_hi", [1, T], BF16)
        rdt_lo = _sb(pes, nc, "rdt_lo", [1, T], BF16)
        rbc_hi = _sb(pes, nc, "rbc_hi", [2, T], BF16)
        rbc_lo = _sb(pes, nc, "rbc_lo", [2, T], BF16)

        @blk.scalar
        def _(s):
            s.wait_ge(sV.h, PROJ_V)
            s.activation(out=dsq0[:], in_=ssm0[:], func=AF.Square)
            s.activation(out=dsq1[:], in_=ssm1[:], func=AF.Square)
            s.activation(out=bsq[:], in_=ssmBC[:], func=AF.Square).then_inc(sA.h, 1)
        sA.n += 1
        sq_done = sA.n

        @blk.tensor
        def _(te, sq_done=sq_done):
            te.wait_ge(sA.h, sq_done)
            te.wait_ge(sV.h, CONST_V)
            for th in range(2):
                sl = slice(th * 512, (th + 1) * 512)
                te.matmul(out=psq[th][:, :], lhsT=ones_col[:], rhs=dsq0[:, sl],
                          start=True, stop=False)
                te.matmul(out=psq[th][:, :], lhsT=ones_col[:], rhs=dsq1[:, sl],
                          start=False, stop=True)
                mm = te.matmul(out=psq2[th][:, :], lhsT=bco_sb[:], rhs=bsq[:, sl],
                               start=True, stop=True)
                if th == 1:
                    mm.then_inc(sPE.h, 1)
        sPE.n += 1
        q_done = sPE.n

        @blk.scalar
        def _(s, q_done=q_done):
            s.wait_ge(sPE.h, q_done)
            for th in range(2):
                sl = slice(th * 512, (th + 1) * 512)
                s.activation(out=r_dt[:, sl], in_=psq[th][:, :], func=AF.Ln,
                             scale=1.0 / RR, bias=1e-6)
                s.activation(out=lnt[:, sl], in_=psq2[th][:, :], func=AF.Ln,
                             scale=1.0 / NST, bias=1e-6).then_inc(sA.h, 1)
                sA.n += 1
            s.wait_ge(sA.h, sA.n)
            s.activation(out=r_dt[:], in_=r_dt[:], func=AF.Exp, scale=-0.5)
            s.activation(out=r_bc[:], in_=lnt[:], func=AF.Exp, scale=-0.5
                         ).then_inc(sA.h, 1)
        sA.n += 1
        r_done = sA.n

        @blk.vector
        def _(v, r_done=r_done):
            v.wait_ge(sA.h, r_done)
            v.tensor_copy(out=rdt_hi[:], in_=r_dt[:]).then_inc(sV.h, 1)
            sV.n += 1
            v.wait_ge(sV.h, sV.n)
            v.tensor_sub(out=rdt_lo[:], in0=r_dt[:], in1=rdt_hi[:])
            v.tensor_copy(out=rbc_hi[:], in_=r_bc[:]).then_inc(sV.h, 1)
            sV.n += 1
            v.wait_ge(sV.h, sV.n)
            v.tensor_sub(out=rbc_lo[:], in0=r_bc[:], in1=rbc_hi[:]).then_inc(sV.h, 1)
        sV.n += 1
        hl_done = sV.n

        @blk.tensor
        def _(te, hl_done=hl_done):
            te.wait_ge(sV.h, hl_done)
            for th in range(2):
                sl = slice(th * 512, (th + 1) * 512)
                te.matmul(out=psRD[th][:, :], lhsT=ones_row[:], rhs=rdt_hi[:, sl],
                          start=True, stop=False)
                te.matmul(out=psRD[th][:, :], lhsT=ones_row[:], rhs=rdt_lo[:, sl],
                          start=False, stop=True)
                te.matmul(out=psRB[th][:, :], lhsT=bce_sb[:], rhs=rbc_hi[:, sl],
                          start=True, stop=False)
                mm = te.matmul(out=psRB[th][:, :], lhsT=bce_sb[:], rhs=rbc_lo[:, sl],
                               start=False, stop=True)
                if th == 1:
                    mm.then_inc(sPE.h, 1)
        sPE.n += 1
        bb_done = sPE.n

        @blk.vector
        def _(v, bb_done=bb_done):
            v.wait_ge(sPE.h, bb_done)
            for th in range(2):
                sl = slice(th * 512, (th + 1) * 512)
                v.scalar_tensor_tensor(out=dtn0[:, sl], in0=ssm0[:, sl],
                                       scalar=dtlnw_sb[:, 0:1], in1=psRD[th][:, :],
                                       op0=ALU.mult, op1=ALU.mult)
                v.scalar_tensor_tensor(out=dtn1[:, sl], in0=ssm1[:, sl],
                                       scalar=dtlnw_sb[:, 1:2], in1=psRD[th][:, :],
                                       op0=ALU.mult, op1=ALU.mult)
                v.scalar_tensor_tensor(out=bcn[:, sl], in0=ssmBC[:, sl],
                                       scalar=bclnw_sb[:, 0:1], in1=psRB[th][:, :],
                                       op0=ALU.mult, op1=ALU.mult).then_inc(sV.h, 1)
                sV.n += 1
            v.wait_ge(sV.h, sV.n)
            v.tensor_copy(out=bchi[:], in_=bcn[:]).then_inc(sV.h, 1)
            sV.n += 1
            v.wait_ge(sV.h, sV.n)
            v.tensor_sub(out=bclo[:], in0=bcn[:], in1=bchi[:]).then_inc(sV.h, 1)
        sV.n += 1
        bcn_done = sV.n

        @blk.sync
        def _(sync, bcn_done=bcn_done):
            sync.wait_ge(sV.h, bcn_done)
            rfin.inc(sync.dma_start(out=dout["cn_out"], in_=bcn[16:32, :]))
            if DEV_TAPS:
                rtap.inc(sync.dma_start(out=dout["dtn_tap"][0:128, :], in_=dtn0[:]))
                rtap.inc(sync.dma_start(out=dout["dtn_tap"][128:256, :], in_=dtn1[:]))
                rtap.inc(sync.dma_start(out=dout["bcn_tap"], in_=bcn[:]))

    B_V = sV.n

    engine_barrier(blk, sBar, 0)

    with ExitStack() as pes:
        psSC = _ps(pes, nc, "psSC", [128, 64])
        psGT = _ps(pes, nc, "psGT", [64, 128])
        psH = [_ps(pes, nc, f"psH{i}", [128, 512]) for i in range(2)]
        psGB = [_ps(pes, nc, f"psGB{i}", [128, 512]) for i in range(2)]
        wrb = _sb(pes, nc, "wrb", [128, HC * 64], BF16)
        ach = [_sb(pes, nc, f"ach{i}", [128, HC * 128], BF16) for i in range(2)]
        e16b = [_sb(pes, nc, f"e16b{i}", [64, 128], BF16) for i in range(2)]
        sc = _sb(pes, nc, "sc", [128, 64], F32)
        wk = _sb(pes, nc, "wk", [128, 64], F32)
        wk2 = _sb(pes, nc, "wk2", [128, 64], F32)
        wk3 = _sb(pes, nc, "wk3", [128, 64], F32)
        vals = _sb(pes, nc, "vals", [128, 16], F32)
        nmax = _sb(pes, nc, "nmax", [128, 1], F32)
        e16v = _sb(pes, nc, "e16v", [128, 16], F32)
        zrow = _sb(pes, nc, "zrow", [128, 1], F32)
        rz = _sb(pes, nc, "rz", [128, 1], F32)
        mask = _sb(pes, nc, "mask", [128, 64], F32)
        eo = _sb(pes, nc, "eo", [128, 64], F32)
        gmat = _sb(pes, nc, "gmat", [128, 64], F32)
        gt_s = _sb(pes, nc, "gt_s", [64, T], F32)
        gt_hi = _sb(pes, nc, "gt_hi", [64, T], BF16)
        gt_lo = _sb(pes, nc, "gt_lo", [64, T], BF16)
        h_s = _sb(pes, nc, "h_s", [128, T], F32)
        hg_b = [_sb(pes, nc, f"hg_b{i}", [128, T], BF16) for i in range(2)]

        wrbf = _sb(pes, nc, "wrbf", [128, HC * 64], F32)
        achf = [_sb(pes, nc, f"achf{i}", [128, HC * 128], F32) for i in range(2)]
        e16f = [_sb(pes, nc, f"e16f{i}", [64, 128], F32) for i in range(2)]

        @blk.sync
        def _(sync):
            rfw.inc(sync.dma_start(out=wrbf[:], in_=din["wr_st"]))

        @blk.gpsimd
        def _(g):
            rfw.wait(g)
            g.tensor_copy(out=wrb[:], in_=wrbf[:]).then_inc(sP.h, 1)
            sP.n += 1
        wr_g = sP.n

        for i in range(8):
            @blk.tensor
            def _(te, i=i, wr_g=wr_g):
                te.wait_ge(sP.h, wr_g)
                te.wait_ge(sV.h, B_V if i == 0 else sV.n)
                for hc in range(HC):
                    mm = te.matmul(
                        out=psSC[:, :],
                        lhsT=xn[:, hc * TH + 3 + i * 128: hc * TH + 3 + (i + 1) * 128],
                        rhs=wrb[:, hc * 64:(hc + 1) * 64],
                        start=(hc == 0), stop=(hc == HC - 1))
                    if hc == HC - 1:
                        mm.then_inc(sPE.h, 1)
            sPE.n += 1
            sc_done = sPE.n

            @blk.vector
            def _(v, sc_done=sc_done):
                v.wait_ge(sV.h, sV.n)
                v.wait_ge(sPE.h, sc_done)
                v.tensor_copy(out=sc[:], in_=psSC[:, :]).then_inc(sV.h, 1)
                sV.n += 1
                v.wait_ge(sV.h, sV.n)
                v.tensor_copy(out=wk[:], in_=sc[:]).then_inc(sV.h, 1)
                sV.n += 1
                v.wait_ge(sV.h, sV.n)
                v.max(out=vals[:, 0:8], in_=wk[:]).then_inc(sV.h, 1)
                sV.n += 1
                v.wait_ge(sV.h, sV.n)
                v.match_replace(out=wk2[:], in_to_replace=vals[:, 0:8],
                                in_values=wk[:], imm_value=NEG).then_inc(sV.h, 1)
                sV.n += 1
                v.wait_ge(sV.h, sV.n)
                v.max(out=vals[:, 8:16], in_=wk2[:]).then_inc(sV.h, 1)
                sV.n += 1
                v.wait_ge(sV.h, sV.n)
                v.match_replace(out=wk3[:], in_to_replace=vals[:, 8:16],
                                in_values=wk2[:], imm_value=NEG).then_inc(sV.h, 1)
                sV.n += 1
                v.wait_ge(sV.h, sV.n)
                v.tensor_scalar_mul(out=nmax[:], in0=vals[:, 0:1], scalar1=-1.0
                                    ).then_inc(sV.h, 1)
            sV.n += 1
            tk_done = sV.n

            @blk.scalar
            def _(s, tk_done=tk_done):
                s.wait_ge(sA.h, sA.n)
                s.wait_ge(sV.h, tk_done)
                s.activation(out=e16v[:], in_=vals[:], func=AF.Exp,
                             bias=nmax[:, 0:1])
                s.activation(out=eo[:], in_=sc[:], func=AF.Exp,
                             bias=nmax[:, 0:1]).then_inc(sA.h, 1)
            sA.n += 1
            exp_done = sA.n

            @blk.vector
            def _(v, exp_done=exp_done):
                v.wait_ge(sV.h, sV.n)
                v.wait_ge(sA.h, exp_done)
                v.reduce_sum(out=zrow[:], in_=e16v[:], axis=AX.X).then_inc(sV.h, 1)
                sV.n += 1
                v.wait_ge(sV.h, sV.n)
                v.reciprocal(out=rz[:], in_=zrow[:]).then_inc(sV.h, 1)
                sV.n += 1
                v.wait_ge(sV.h, sV.n)
                v.tensor_tensor(out=mask[:], in0=sc[:], in1=wk3[:], op=ALU.is_gt
                                ).then_inc(sV.h, 1)
                sV.n += 1
                v.wait_ge(sV.h, sV.n)
                v.scalar_tensor_tensor(out=gmat[:], in0=eo[:], scalar=rz[:, 0:1],
                                       in1=mask[:], op0=ALU.mult, op1=ALU.mult
                                       ).then_inc(sV.h, 1)
            sV.n += 1
            g_done = sV.n

            @blk.tensor
            def _(te, g_done=g_done):
                te.wait_ge(sV.h, g_done)
                te.transpose(out=psGT[:, :], in_=gmat[:], identity=eye_sb[:]
                             ).then_inc(sPE.h, 1)
            sPE.n += 1
            t_done = sPE.n

            @blk.vector
            def _(v, i=i, t_done=t_done):
                v.wait_ge(sV.h, sV.n)
                v.wait_ge(sPE.h, t_done)
                v.tensor_copy(out=gt_s[:, i * 128:(i + 1) * 128], in_=psGT[:, :]
                              ).then_inc(sV.h, 1)
            sV.n += 1

            if DEV_TAPS:
                gm_done = sV.n

                @blk.sync
                def _(sync, i=i, gm_done=gm_done):
                    sync.wait_ge(sV.h, gm_done)
                    rtap.inc(sync.dma_start(
                        out=dout["g_tap"][i * 128:(i + 1) * 128, :], in_=gmat[:]))

        @blk.vector
        def _(v, at=sV.n):
            v.wait_ge(sV.h, at)
            v.tensor_copy(out=gt_hi[:], in_=gt_s[:]).then_inc(sV.h, 1)
            sV.n += 1
            v.wait_ge(sV.h, sV.n)
            v.tensor_sub(out=gt_lo[:], in0=gt_s[:], in1=gt_hi[:]).then_inc(sV.h, 1)
        sV.n += 1
        gt_done = sV.n

        hg_marks = {}
        for c in range(8):
            @blk.sync
            def _(sync, c=c, at=sP.n):
                if c >= 2:
                    sync.wait_ge(sP.h, at)
                rf.inc(sync.dma_start(out=achf[c % 2][:], in_=din["ffnA"][c]),
                       c % 2)
                rf.inc(sync.dma_start(out=e16f[c % 2][:], in_=din["exp16"][c]),
                       c % 2)

            @blk.gpsimd
            def _(g, c=c, at=rf.cnt[c % 2], pat=sPE.n):
                g.wait_ge(rf.sems[c % 2], at)
                if c >= 2:
                    g.wait_ge(sPE.h, pat)
                g.tensor_copy(out=ach[c % 2][:], in_=achf[c % 2][:])
                g.tensor_copy(out=e16b[c % 2][:], in_=e16f[c % 2][:]
                              ).then_inc(sP.h, 1)
                sP.n += 1
            a_g = sP.n

            @blk.tensor
            def _(te, c=c, a_g=a_g, gt_done=gt_done):
                te.wait_ge(sP.h, a_g)
                te.wait_ge(sV.h, gt_done)
                if c >= 1:
                    te.wait_ge(sV.h, hg_marks[c - 1])
                for th in range(2):
                    for hc in range(HC):
                        te.matmul(
                            out=psH[th][:, :],
                            lhsT=ach[c % 2][:, hc * 128:(hc + 1) * 128],
                            rhs=xn[:, hc * TH + 3 + th * 512:
                                    hc * TH + 3 + th * 512 + 512],
                            start=(hc == 0), stop=(hc == HC - 1))
                for th in range(2):
                    sl = slice(th * 512, (th + 1) * 512)
                    te.matmul(out=psGB[th][:, :], lhsT=e16b[c % 2][:],
                              rhs=gt_hi[:, sl], start=True, stop=False)
                    mm = te.matmul(out=psGB[th][:, :], lhsT=e16b[c % 2][:],
                                   rhs=gt_lo[:, sl], start=False, stop=True)
                    if th == 1:
                        mm.then_inc(sPE.h, 1)
            sPE.n += 1
            h_done = sPE.n

            @blk.scalar
            def _(s, h_done=h_done):
                s.wait_ge(sA.h, sA.n)
                s.wait_ge(sPE.h, h_done)
                s.copy(out=h_s[:, 0:512], in_=psH[0][:, :])
                s.copy(out=h_s[:, 512:1024], in_=psH[1][:, :]).then_inc(sA.h, 1)
            sA.n += 1
            he_done = sA.n

            @blk.vector
            def _(v, c=c, he_done=he_done):
                v.wait_ge(sV.h, sV.n)
                v.wait_ge(sA.h, he_done)
                if c >= 2:
                    v.wait_ge(rh.sems[c % 2], hg_marks[(c - 2, "st")])
                v.tensor_tensor(out=hg_b[c % 2][:, 0:512], in0=h_s[:, 0:512],
                                in1=psGB[0][:, :], op=ALU.mult)
                v.tensor_tensor(out=hg_b[c % 2][:, 512:1024], in0=h_s[:, 512:1024],
                                in1=psGB[1][:, :], op=ALU.mult).then_inc(sV.h, 1)
            sV.n += 1
            hg_marks[c] = sV.n

            @blk.sync
            def _(sync, c=c, at=sV.n):
                sync.wait_ge(sV.h, at)
                rh.inc(sync.dma_start(out=dout["hg_out"][c * 128:(c + 1) * 128, :],
                                      in_=hg_b[c % 2][:]), c % 2)
            hg_marks[(c, "st")] = rh.cnt[c % 2]

    ssm_es.close()
    xn_es.close()

    @blk.sync
    def _(sync):
        rh.wait_all(sync)
        if DEV_TAPS:
            rtap.wait_all(sync)

    engine_barrier(blk, sBar, 0)

    with ExitStack() as pes:
        psD = [_ps(pes, nc, f"psD{i}", [128, 512]) for i in range(2)]
        psBB = [_ps(pes, nc, f"psBB{i}", [128, 512]) for i in range(2)]
        psCB = [_ps(pes, nc, f"psCB{i}", [128, 512]) for i in range(2)]
        wdch = [_sb(pes, nc, f"wdch{i}", [128, 256], BF16) for i in range(2)]
        wdchf = [_sb(pes, nc, f"wdchf{i}", [128, 256], F32) for i in range(2)]
        dl_g = _sb(pes, nc, "dl_g", [128, GG * T], F32)
        du_g = _sb(pes, nc, "du_g", [128, GG * T], F32)
        y_g = _sb(pes, nc, "y_g", [128, GG * T], F32)
        ab1 = _sb(pes, nc, "ab1", [128, 512], F32)
        exn = _sb(pes, nc, "exn", [128, 512], F32)
        l1p = _sb(pes, nc, "l1p", [128, 512], F32)
        rlu = _sb(pes, nc, "rlu", [128, 512], F32)
        utmp = [_sb(pes, nc, f"utmp{i}", [128, T], F32) for i in range(2)]
        sgt = [_sb(pes, nc, f"sgt{i}", [128, T], F32) for i in range(2)]
        z1b = [_sb(pes, nc, f"z1b{i}", [128, T], BF16) for i in range(2)]
        ctt = [_sb(pes, nc, f"ctt{i}", [128, T], F32) for i in range(2)]
        bb_s = _sb(pes, nc, "bb_s", [128, T], F32)
        cb_s = _sb(pes, nc, "cb_s", [128, T], F32)
        abuf = [_sb(pes, nc, f"abuf{i}", [128, T], F32) for i in range(2)]
        bbuf = [_sb(pes, nc, f"bbuf{i}", [128, T], F32) for i in range(2)]
        sbx = [_sb(pes, nc, f"sbx{i}", [128, T], F32) for i in range(2)]
        ymb = [_sb(pes, nc, f"ymb{i}", [128, T], F32) for i in range(2)]

        ct_store = {}
        z1_store = {}
        dve_marks = {}
        ya_marks = {}
        pend = []
        for g in range(DC // GG):
            dl_marks = {}
            yg_marks = {}
            for jj in range(GG):
                dc = g * GG + jj

                @blk.sync
                def _(sync, dc=dc, at=sA.n):
                    if dc >= 2:
                        sync.wait_ge(sA.h, at)
                    rd.inc(sync.dma_start(out=wdchf[dc % 2][:],
                                          in_=din["wdtT"][dc]), dc % 2)

                @blk.scalar
                def _(s, dc=dc, at=rd.cnt[dc % 2], pat=sPE.n):
                    s.wait_ge(sA.h, sA.n)
                    s.wait_ge(rd.sems[dc % 2], at)
                    if dc >= 2:
                        s.wait_ge(sPE.h, pat)
                    s.copy(out=wdch[dc % 2][:], in_=wdchf[dc % 2][:]
                           ).then_inc(sA.h, 1)
                    sA.n += 1
                wd_g = sA.n

                @blk.sync
                def _(sync, dc=dc, at=sV.n):
                    if dc == 0:
                        ru.wait_all(sync)
                        rs.wait_all(sync)
                    if dc >= 2:
                        sync.wait_ge(sV.h, at)
                    rt.inc(sync.dma_start(out=utmp[dc % 2][:],
                                          in_=u_dram[dc * 128:(dc + 1) * 128, :]),
                           dc % 2)
                u_l = rt.cnt[dc % 2]

                @blk.tensor
                def _(te, dc=dc, wd_g=wd_g, at=sA.n):
                    te.wait_ge(sA.h, wd_g)
                    te.wait_ge(sV.h, B_V)
                    if dc >= 1:
                        te.wait_ge(sA.h, at)
                    for th in range(2):
                        sl = slice(th * 512, (th + 1) * 512)
                        te.matmul(out=psD[th][:, :], lhsT=wdch[dc % 2][:, 0:128],
                                  rhs=dtn0[:, sl], start=True, stop=False)
                        mm = te.matmul(out=psD[th][:, :],
                                       lhsT=wdch[dc % 2][:, 128:256],
                                       rhs=dtn1[:, sl], start=False, stop=True)
                        if th == 1:
                            mm.then_inc(sPE.h, 1)
                sPE.n += 1
                d_pe = sPE.n

                for th in range(2):
                    @blk.scalar
                    def _(s, dc=dc, th=th, d_pe=d_pe, at=sV.n):
                        s.wait_ge(sPE.h, d_pe)
                        if dc == 0 and th == 0:
                            s.wait_ge(sV.h, CONST_V)
                        else:
                            s.wait_ge(sV.h, at)
                        s.activation(out=ab1[:], in_=psD[th][:, :], func=AF.Abs,
                                     bias=dtb_sb[:, dc:dc + 1]).then_inc(sA.h, 1)
                        sA.n += 1
                        s.wait_ge(sA.h, sA.n)
                        s.activation(out=exn[:], in_=ab1[:], func=AF.Exp,
                                     scale=-1.0).then_inc(sA.h, 1)
                        sA.n += 1
                        s.wait_ge(sA.h, sA.n)
                        s.activation(out=l1p[:], in_=exn[:], func=AF.Ln, bias=1.0)
                        s.activation(out=rlu[:], in_=psD[th][:, :], func=AF.Relu,
                                     bias=dtb_sb[:, dc:dc + 1]).then_inc(sA.h, 1)
                    sA.n += 1

                    @blk.vector
                    def _(v, dc=dc, jj=jj, th=th, at=sA.n, u_l=u_l):
                        v.wait_ge(sA.h, at)
                        sl = slice(jj * T + th * 512, jj * T + (th + 1) * 512)
                        ins = v.tensor_add(out=dl_g[:, sl], in0=rlu[:], in1=l1p[:])
                        ins.then_inc(sV.h, 1)
                        sV.n += 1
                        if th == 1:
                            v.wait_ge(sV.h, sV.n)
                            v.wait_ge(rt.sems[dc % 2], u_l)
                            if dc >= 2:
                                v.wait_ge(rct.sems[dc % 2], ct_store[dc - 2])
                            v.tensor_tensor_scan(
                                out=ctt[dc % 2][:],
                                data0=dl_g[:, jj * T:(jj + 1) * T],
                                data1=dl_g[:, jj * T:(jj + 1) * T],
                                initial=0.0, op0=ALU.add, op1=ALU.bypass
                            ).then_inc(sV.h, 1)
                            sV.n += 1
                            v.tensor_tensor(out=du_g[:, jj * T:(jj + 1) * T],
                                            in0=dl_g[:, jj * T:(jj + 1) * T],
                                            in1=utmp[dc % 2][:], op=ALU.mult)
                            v.wait_ge(sV.h, sV.n)
                            v.tensor_copy(out=ctot_sb[:, dc:dc + 1],
                                          in_=ctt[dc % 2][:, T - 1:T]
                                          ).then_inc(sV.h, 1)
                            sV.n += 1

                ct_v = sV.n
                dl_marks[jj] = sV.n

                @blk.sync
                def _(sync, dc=dc, ct_v=ct_v):
                    sync.wait_ge(sV.h, ct_v)
                    rct.inc(sync.dma_start(
                        out=dout["ct_out"][dc * 128:(dc + 1) * 128, :],
                        in_=ctt[dc % 2][:]), dc % 2)
                    if DEV_TAPS:
                        rtap.inc(sync.dma_start(
                            out=dout["dl_tap"][dc * 128:(dc + 1) * 128, :],
                            in_=dl_g[:, (dc % GG) * T:((dc % GG) + 1) * T]))
                ct_store[dc] = rct.cnt[dc % 2]

            for n in range(NST):
                @blk.tensor
                def _(te, g=g, n=n, aat=sA.n, vat=sV.n, pat=sP.n):
                    if g > 0 or n > 0:
                        te.wait_ge(sA.h, aat)
                        te.wait_ge(sV.h, vat)
                        te.wait_ge(sP.h, pat)
                    te.wait_ge(sV.h, B_V)
                    selB = seln_sb[:, n * 128:(n + 1) * 128]
                    selC = seln_sb[:, (16 + n) * 128:(16 + n + 1) * 128]
                    for th in range(2):
                        sl = slice(th * 512, (th + 1) * 512)
                        te.matmul(out=psBB[th][:, :], lhsT=selB,
                                  rhs=bchi[:, sl], start=True, stop=False)
                        te.matmul(out=psBB[th][:, :], lhsT=selB,
                                  rhs=bclo[:, sl], start=False, stop=True)
                        te.matmul(out=psCB[th][:, :], lhsT=selC,
                                  rhs=bchi[:, sl], start=True, stop=False)
                        mm = te.matmul(out=psCB[th][:, :], lhsT=selC,
                                       rhs=bclo[:, sl], start=False, stop=True)
                        if th == 1:
                            mm.then_inc(sPE.h, 1)
                sPE.n += 1
                bb_pe = sPE.n

                @blk.scalar
                def _(s, bb_pe=bb_pe):
                    s.wait_ge(sA.h, sA.n)
                    s.wait_ge(sPE.h, bb_pe)
                    for th in range(2):
                        sl = slice(th * 512, (th + 1) * 512)
                        s.copy(out=bb_s[:, sl], in_=psBB[th][:, :])
                        ins = s.copy(out=cb_s[:, sl], in_=psCB[th][:, :])
                        if th == 1:
                            ins.then_inc(sA.h, 1)
                sA.n += 1
                bb_a = sA.n

                def emit_ya(k, kj, kp, kv):
                    @blk.gpsimd
                    def _(gp, kj=kj, kp=kp, kv=kv, ym=yg_marks.get(kj, 0)):
                        gp.wait_ge(sP.h, sP.n)
                        gp.wait_ge(sV.h, kv)
                        if ym:
                            gp.wait_ge(sP.h, ym)
                        gp.tensor_tensor(out=y_g[:, kj * T:(kj + 1) * T],
                                         in0=y_g[:, kj * T:(kj + 1) * T],
                                         in1=ymb[kp][:], op=ALU.add
                                         ).then_inc(sP.h, 1)
                    sP.n += 1
                    ya_marks[k] = sP.n
                    yg_marks[kj] = sP.n

                for jj in range(GG):
                    dc = g * GG + jj
                    gii = (g * NST + n) * GG + jj
                    pp = gii % 2

                    @blk.scalar
                    def _(s, dc=dc, jj=jj, n=n, pp=pp,
                          war=dve_marks.get(gii - 2, 0)):
                        s.wait_ge(sA.h, sA.n)
                        if war:
                            s.wait_ge(sV.h, war)
                        s.activation(out=abuf[pp][:],
                                     in_=dl_g[:, jj * T:(jj + 1) * T],
                                     func=AF.Exp,
                                     scale=a_sb[:, dc * NST + n:dc * NST + n + 1]
                                     ).then_inc(sA.h, 1)
                    sA.n += 1
                    a_a = sA.n

                    @blk.gpsimd
                    def _(gp, jj=jj, pp=pp, bb_a=bb_a,
                          war=dve_marks.get(gii - 2, 0)):
                        gp.wait_ge(sP.h, sP.n)
                        if war:
                            gp.wait_ge(sV.h, war)
                        gp.wait_ge(sA.h, bb_a)
                        gp.tensor_tensor(out=bbuf[pp][:],
                                         in0=du_g[:, jj * T:(jj + 1) * T],
                                         in1=bb_s[:], op=ALU.mult).then_inc(sP.h, 1)
                    sP.n += 1
                    b_p = sP.n

                    @blk.vector
                    def _(v, dc=dc, jj=jj, n=n, pp=pp, a_a=a_a, b_p=b_p,
                          war=ya_marks.get(gii - 2, 0)):
                        v.wait_ge(sA.h, a_a)
                        v.wait_ge(sP.h, b_p)
                        if n > 0 and war:
                            v.wait_ge(sP.h, war)
                        v.tensor_tensor_scan(out=sbx[pp][:], data0=abuf[pp][:],
                                             data1=bbuf[pp][:], initial=0.0,
                                             op0=ALU.mult, op1=ALU.add
                                             ).then_inc(sV.h, 1)
                        sV.n += 1
                        v.wait_ge(sV.h, sV.n)
                        if n == 0:
                            v.tensor_tensor(out=y_g[:, jj * T:(jj + 1) * T],
                                            in0=sbx[pp][:], in1=cb_s[:],
                                            op=ALU.mult)
                        else:
                            v.tensor_tensor(out=ymb[pp][:], in0=sbx[pp][:],
                                            in1=cb_s[:], op=ALU.mult)
                        v.tensor_copy(out=s_cap[:, dc * NST + n:dc * NST + n + 1],
                                      in_=sbx[pp][:, T - 1:T]).then_inc(sV.h, 1)
                    sV.n += 1
                    dve_marks[gii] = sV.n

                    if n > 0:
                        pend.append((gii, jj, pp, sV.n))
                    while len(pend) > 1 or (pend and pend[0][0] < gii):
                        emit_ya(*pend.pop(0))

            while pend:
                emit_ya(*pend.pop(0))

            for jj in range(GG):
                dc = g * GG + jj
                pp = jj % 2

                @blk.sync
                def _(sync, dc=dc, pp=pp, at=sV.n):
                    if jj >= 2:
                        sync.wait_ge(sV.h, at)
                    rt.inc(sync.dma_start(out=utmp[pp][:],
                                          in_=u_dram[dc * 128:(dc + 1) * 128, :]),
                           pp)
                    rt.inc(sync.dma_start(out=sgt[pp][:],
                                          in_=sg_dram[dc * 128:(dc + 1) * 128, :]),
                           pp)
                ul = rt.cnt[pp]
                yp = sP.n

                @blk.vector
                def _(v, dc=dc, jj=jj, pp=pp, ul=ul, yp=yp):
                    v.wait_ge(sV.h, sV.n)
                    v.wait_ge(rt.sems[pp], ul)
                    v.wait_ge(sP.h, yp)
                    if dc >= 2:
                        v.wait_ge(rzr.sems[pp], z1_store.get(dc - 2, 0))
                    v.scalar_tensor_tensor(out=sbx[pp][:], in0=utmp[pp][:],
                                           scalar=dparam_sb[:, dc:dc + 1],
                                           in1=y_g[:, jj * T:(jj + 1) * T],
                                           op0=ALU.mult, op1=ALU.add
                                           ).then_inc(sV.h, 1)
                    sV.n += 1
                    v.wait_ge(sV.h, sV.n)
                    v.tensor_tensor(out=z1b[pp][:], in0=sbx[pp][:],
                                    in1=sgt[pp][:],
                                    op=ALU.mult).then_inc(sV.h, 1)
                sV.n += 1
                z_v = sV.n

                @blk.sync
                def _(sync, dc=dc, pp=pp, z_v=z_v):
                    sync.wait_ge(sV.h, z_v)
                    rzr.inc(sync.dma_start(
                        out=dout["z1"][dc * 128:(dc + 1) * 128, :],
                        in_=z1b[pp][:]), pp)
                z1_store[dc] = rzr.cnt[pp]

    @blk.sync
    def _(sync, at=sV.n):
        sync.wait_ge(sV.h, at)
        rfin.inc(sync.dma_start(out=dout["s_out"], in_=s_cap[:]))
        rfin.inc(sync.dma_start(out=dout["ctot_out"], in_=ctot_sb[:]))
        for r_ in [rfin, rct, rzr, rh, rs]:
            r_.wait_all(sync)
        if DEV_TAPS:
            rtap.wait_all(sync)

    es.close()
    return nc



def build_l2():
    nc = bass.Bass("TRN2", target_bir_lowering=False)
    din = {}
    for name, shape, dt in [
        ("z1", [DD, T], BF16),
        ("sg", [DD, T], F32),
        ("ct", [DD, T], F32),
        ("cn", [16, T], F32),
        ("hg", [1024, T], BF16),
        ("s_all", [8, 128, DC * NST], F32),
        ("ctot_all", [8, 128, DC], F32),
        ("a_st", [128, DC * NST], F32),
        ("maskA", [128, 128], F32),
        ("maskB", [128, 128], F32),
        ("seln", [32, 32 * 128], F32),
        ("woT", [32, 128, 64 * 128], F32),
        ("fbT", [32, 128, 8 * 128], F32),
        ("hsT", [H, T], F32),
    ]:
        din[name] = nc.dram_tensor(name, shape, dt, kind="ExternalInput").ap()
    out_t = nc.dram_tensor("out", [H, T], F32, kind="ExternalOutput").ap()
    z_dram = nc.dram_tensor("z_dram", [DD, T], BF16, kind="Internal").ap()
    if DEV_TAPS:
        sin_tap = nc.dram_tensor("sin_tap", [128, DC * NST], F32,
                                 kind="ExternalOutput").ap()
        hj_tap = nc.dram_tensor("hj_tap", [128, 8 * DC], F32,
                                 kind="ExternalOutput").ap()
        lns_tap = nc.dram_tensor("lns_tap", [128, DC * NST], F32,
                                 kind="ExternalOutput").ap()
        sgn_tap = nc.dram_tensor("sgn_tap", [128, DC * NST], F32,
                                 kind="ExternalOutput").ap()

    es = ExitStack()
    cpool = _sb(es, nc, "cpool", [128, 4], F32)
    sPE = Ctr(es, nc, "sPE")
    sA = Ctr(es, nc, "sA")
    sV = Ctr(es, nc, "sV")
    sP = Ctr(es, nc, "sP")
    sBar = Ctr(es, nc, "sBar")
    qc = DRing(es, nc, "qc", 1)
    qt = DRing(es, nc, "qt", 1)
    qe = DRing(es, nc, "qe", 2)
    qz = DRing(es, nc, "qz", 2)
    qZ = DRing(es, nc, "qZ", 1)
    qw = DRing(es, nc, "qw", 2)
    qf = DRing(es, nc, "qf", 1)
    qo = DRing(es, nc, "qo", 1)
    qtap = DRing(es, nc, "qtap", 1)

    mid = ExitStack()
    a_sb = _sb(mid, nc, "a_sb", [128, DC * NST], F32)
    lnS = _sb(mid, nc, "lnS", [128, DC * NST], F32)
    sgn = _sb(mid, nc, "sgn", [128, DC * NST], F32)
    cnn = _sb(mid, nc, "cnn", [16, T], F32)
    cnhi = _sb(mid, nc, "cnhi", [16, T], BF16)
    cnlo = _sb(mid, nc, "cnlo", [16, T], BF16)
    seln_sb = _sb(mid, nc, "seln_sb", [32, 32 * 128], BF16)

    blk = es.enter_context(nc.Block())


    @blk.scalar
    def _(s):
        load_tbl(s, 6)
    with ExitStack() as pes:
        selnf = _sb(pes, nc, "selnf", [32, 32 * 128], F32)
        sall = _sb(pes, nc, "sall", [128, 8 * DC * NST], F32)
        ctall = _sb(pes, nc, "ctall", [128, 8 * DC], F32)
        mA = _sb(pes, nc, "mA", [128, 128], F32)
        mB = _sb(pes, nc, "mB", [128, 128], F32)
        pfx = _sb(pes, nc, "pfx", [128, 8 * DC], F32)
        myp = _sb(pes, nc, "myp", [128, DC], F32)
        hj_sb = _sb(pes, nc, "hj_sb", [128, 8 * DC], F32)
        sin = _sb(pes, nc, "sin", [128, DC * NST], F32)
        tj = _sb(pes, nc, "tj", [128, DC], F32)

        @blk.sync
        def _(sync):
            for t_, srcn in [(sall, "s_all"), (ctall, "ctot_all")]:
                qc.inc(sync.dma_start(
                    out=t_[:].rearrange("p (j c) -> p j c", j=8),
                    in_=din[srcn].rearrange("j p c -> p j c")))
            for t_, srcn in [(a_sb, "a_st"), (mA, "maskA"), (mB, "maskB"),
                             (selnf, "seln")]:
                qc.inc(sync.dma_start(out=t_[:], in_=din[srcn]))
            qc.inc(sync.dma_start(out=cnn[:], in_=din["cn"]))

        def edge(v, ins):
            ins.then_inc(sV.h, 1)
            sV.n += 1
            v.wait_ge(sV.h, sV.n)

        @blk.vector
        def _(v):
            for i, val in enumerate([1.0]):
                v.memset(cpool[:, i:i + 1], val)
                nc.const_aps.register(F32, val, cpool[:, i:i + 1])
            qc.wait_all(v)
            v.tensor_copy(out=seln_sb[:], in_=selnf[:])
            edge(v, v.tensor_copy(out=cnhi[:], in_=cnn[:]))
            v.tensor_sub(out=cnlo[:], in0=cnn[:], in1=cnhi[:])
            edge(v, v.memset(pfx[:, 0:DC], 0.0))
            for j in range(1, 8):
                edge(v, v.tensor_add(out=pfx[:, j * DC:(j + 1) * DC],
                                     in0=pfx[:, (j - 1) * DC:j * DC],
                                     in1=ctall[:, (j - 1) * DC:j * DC]))
            edge(v, v.memset(myp[:], 0.0))
            for j in range(8):
                edge(v, v.scalar_tensor_tensor(
                    out=myp[:], in0=ctall[:, j * DC:(j + 1) * DC],
                    scalar=mA[:, j:j + 1], in1=myp[:],
                    op0=ALU.mult, op1=ALU.add))
            for j in range(8):
                sl = slice(j * DC, (j + 1) * DC)
                edge(v, v.tensor_sub(out=tj[:], in0=myp[:], in1=pfx[:, sl]))
                edge(v, v.tensor_sub(out=tj[:], in0=tj[:], in1=ctall[:, sl]))
                edge(v, v.tensor_scalar(out=hj_sb[:, sl], in0=tj[:],
                                        scalar1=mA[:, j:j + 1],
                                        scalar2=mB[:, j:j + 1],
                                        op0=ALU.mult, op1=ALU.add))
            v.memset(sin[:], 0.0).then_inc(sV.h, 1)
        sV.n += 1
        PREP_V = sV.n

        argb = [_sb(pes, nc, f"argb{i}", [128, DC * NST], F32) for i in range(2)]
        wb = [_sb(pes, nc, f"wb{i}", [128, DC * NST], F32) for i in range(2)]
        tb2 = _sb(pes, nc, "tb2", [128, DC * NST], F32)
        for j in range(8):
            pp = j % 2

            @blk.vector
            def _(v, j=j, pp=pp, aat=sA.n):
                if j == 0:
                    pass
                if j >= 2:
                    v.wait_ge(sA.h, aat)
                hb = hj_sb[:, j * DC:(j + 1) * DC].rearrange(
                    "p (c one) -> p c one", one=1).broadcast_to([128, DC, NST])
                v.tensor_tensor(out=argb[pp][:].rearrange("p (c n) -> p c n", n=NST),
                                in0=a_sb[:].rearrange("p (c n) -> p c n", n=NST),
                                in1=hb, op=ALU.mult).then_inc(sV.h, 1)
            sV.n += 1
            arg_v = sV.n

            @blk.scalar
            def _(s, j=j, pp=pp, arg_v=arg_v, vat=sV.n):
                s.wait_ge(sV.h, arg_v)
                s.activation(out=wb[pp][:], in_=argb[pp][:], func=AF.Exp
                             ).then_inc(sA.h, 1)
            sA.n += 1
            w_a = sA.n

            @blk.vector
            def _(v, j=j, pp=pp, w_a=w_a):
                v.wait_ge(sA.h, w_a)
                ins = v.tensor_mul(out=tb2[:], in0=wb[pp][:],
                                   in1=sall[:, j * DC * NST:(j + 1) * DC * NST])
                ins.then_inc(sV.h, 1)
                sV.n += 1
                v.wait_ge(sV.h, sV.n)
                v.tensor_add(out=sin[:], in0=sin[:], in1=tb2[:]).then_inc(sV.h, 1)
                sV.n += 1
                v.wait_ge(sV.h, sV.n)

        SIN_V = sV.n

        @blk.scalar
        def _(s, at=SIN_V):
            s.wait_ge(sV.h, at)
            s.activation(out=lnS[:], in_=sin[:], func=AF.Abs).then_inc(sA.h, 1)
            sA.n += 1
            s.wait_ge(sV.h, sV.n + 1)
            s.activation(out=lnS[:], in_=lnS[:], func=AF.Ln)
            s.activation(out=sgn[:], in_=sin[:], func=AF.Sign).then_inc(sA.h, 1)
            sA.n += 1

        @blk.vector
        def _(v, at=sA.n - 1):
            v.wait_ge(sA.h, at)
            v.tensor_scalar_max(out=lnS[:], in0=lnS[:], scalar1=1e-18
                                ).then_inc(sV.h, 1)
        sV.n += 1
        PREP_A = sA.n
        if DEV_TAPS:
            @blk.sync
            def _(sync, at=SIN_V):
                sync.wait_ge(sV.h, at)
                qtap.inc(sync.dma_start(out=sin_tap, in_=sin[:]))
                qtap.inc(sync.dma_start(out=hj_tap, in_=hj_sb[:]))

    if DEV_TAPS:
        @blk.sync
        def _(sync, at=sA.n, vt=sV.n):
            sync.wait_ge(sA.h, at)
            sync.wait_ge(sV.h, vt)
            qtap.inc(sync.dma_start(out=lns_tap, in_=lnS[:]))
            qtap.inc(sync.dma_start(out=sgn_tap, in_=sgn[:]))

    engine_barrier(blk, sBar, 1)

    with ExitStack() as pes:
        psCB = [_ps(pes, nc, f"psCB{i}", [128, 512]) for i in range(2)]
        ct_g = _sb(pes, nc, "ct_g", [128, GG * T], F32)
        yc_g = _sb(pes, nc, "yc_g", [128, GG * T], F32)
        sgt = [_sb(pes, nc, f"sgt{i}", [128, T], F32) for i in range(2)]
        z1t = [_sb(pes, nc, f"z1t{i}", [128, T], BF16) for i in range(2)]
        cb_s = _sb(pes, nc, "cb_s", [128, T], F32)
        corr = [_sb(pes, nc, f"corr{i}", [128, T], F32) for i in range(2)]
        tmpb = [_sb(pes, nc, f"tmpb{i}", [128, T], F32) for i in range(2)]
        z2b = _sb(pes, nc, "z2b", [128, T], F32)
        z_st = {}

        p2_dve = {}
        p2_ya = {}
        pend2 = []
        for g in range(DC // GG):
            yc_marks = {}
            @blk.sync
            def _(sync, g=g, at=sA.n):
                if g > 0:
                    sync.wait_ge(sA.h, at)
                for jj in range(GG):
                    dc = g * GG + jj
                    qt.inc(sync.dma_start(out=ct_g[:, jj * T:(jj + 1) * T],
                                          in_=din["ct"][dc * 128:(dc + 1) * 128, :]))
            ct_l = qt.cnt[0]

            for n in range(NST):
                @blk.tensor
                def _(te, g=g, n=n, aat=sA.n, vat=sV.n, pat=sP.n):
                    if g > 0 or n > 0:
                        te.wait_ge(sA.h, aat)
                        te.wait_ge(sV.h, vat)
                        te.wait_ge(sP.h, pat)
                    te.wait_ge(sV.h, PREP_V)
                    selC = seln_sb[:, n * 128:(n + 1) * 128]
                    for th in range(2):
                        sl = slice(th * 512, (th + 1) * 512)
                        te.matmul(out=psCB[th][:, :], lhsT=selC[0:16, :],
                                  rhs=cnhi[:, sl], start=True, stop=False)
                        mm = te.matmul(out=psCB[th][:, :], lhsT=selC[0:16, :],
                                       rhs=cnlo[:, sl], start=False, stop=True)
                        if th == 1:
                            mm.then_inc(sPE.h, 1)
                sPE.n += 1
                bb_pe = sPE.n

                @blk.scalar
                def _(s, bb_pe=bb_pe):
                    s.wait_ge(sPE.h, bb_pe)
                    for th in range(2):
                        sl = slice(th * 512, (th + 1) * 512)
                        ins = s.copy(out=cb_s[:, sl], in_=psCB[th][:, :])
                        if th == 1:
                            ins.then_inc(sA.h, 1)
                sA.n += 1
                bb_a = sA.n

                def emit_yc(k, kn, kj, kp, kv):
                    @blk.gpsimd
                    def _(gp, kn=kn, kj=kj, kp=kp, kv=kv,
                          ym=yc_marks.get(kj, 0)):
                        gp.wait_ge(sP.h, sP.n)
                        gp.wait_ge(sV.h, kv)
                        if ym:
                            gp.wait_ge(sP.h, ym)
                        if kn == 0:
                            gp.tensor_copy(out=yc_g[:, kj * T:(kj + 1) * T],
                                           in_=tmpb[kp][:]).then_inc(sP.h, 1)
                        else:
                            gp.tensor_add(out=yc_g[:, kj * T:(kj + 1) * T],
                                          in0=yc_g[:, kj * T:(kj + 1) * T],
                                          in1=tmpb[kp][:]).then_inc(sP.h, 1)
                    sP.n += 1
                    p2_ya[k] = sP.n
                    yc_marks[kj] = sP.n

                for jj in range(GG):
                    dc = g * GG + jj
                    gii = (g * NST + n) * GG + jj
                    pp = gii % 2

                    @blk.scalar
                    def _(s, dc=dc, jj=jj, n=n, pp=pp, ct_l=ct_l,
                          war=p2_dve.get(gii - 2, 0)):
                        s.wait_ge(sA.h, sA.n)
                        s.wait_ge(qt.sems[0], ct_l)
                        if n == 0 and jj == 0:
                            s.wait_ge(sA.h, PREP_A)
                        if war:
                            s.wait_ge(sV.h, war)
                        col = dc * NST + n
                        s.activation(out=corr[pp][:],
                                     in_=ct_g[:, jj * T:(jj + 1) * T],
                                     func=AF.Exp, scale=a_sb[:, col:col + 1],
                                     bias=lnS[:, col:col + 1]).then_inc(sA.h, 1)
                    sA.n += 1
                    a_a = sA.n

                    @blk.vector
                    def _(v, dc=dc, n=n, pp=pp, a_a=a_a, bb_a=bb_a,
                          war=p2_ya.get(gii - 2, 0)):
                        v.wait_ge(sV.h, sV.n)
                        v.wait_ge(sA.h, a_a)
                        v.wait_ge(sA.h, bb_a)
                        if war:
                            v.wait_ge(sP.h, war)
                        col = dc * NST + n
                        v.scalar_tensor_tensor(out=tmpb[pp][:], in0=corr[pp][:],
                                               scalar=sgn[:, col:col + 1],
                                               in1=cb_s[:], op0=ALU.mult,
                                               op1=ALU.mult).then_inc(sV.h, 1)
                    sV.n += 1
                    p2_dve[gii] = sV.n

                    pend2.append((gii, n, jj, pp, sV.n))
                    while len(pend2) > 1 or (pend2 and pend2[0][0] < gii):
                        emit_yc(*pend2.pop(0))

            while pend2:
                emit_yc(*pend2.pop(0))

            for jj in range(GG):
                dc = g * GG + jj
                pp = jj % 2

                @blk.sync
                def _(sync, dc=dc, jj=jj, pp=pp, at=sV.n,
                      zst=z_st.get(g * GG + jj - 2, 0)):
                    if jj >= 2:
                        sync.wait_ge(sV.h, at)
                    if dc >= 2:
                        sync.wait_ge(qz.sems[pp], zst)
                    qe.inc(sync.dma_start(out=sgt[pp][:],
                                          in_=din["sg"][dc * 128:(dc + 1) * 128, :]),
                           pp)
                    qe.inc(sync.dma_start(out=z1t[pp][:],
                                          in_=din["z1"][dc * 128:(dc + 1) * 128, :]),
                           pp)
                ul = qe.cnt[pp]
                yp = sP.n

                @blk.vector
                def _(v, dc=dc, jj=jj, pp=pp, ul=ul, yp=yp):
                    v.wait_ge(qe.sems[pp], ul)
                    v.wait_ge(sP.h, yp)
                    v.tensor_tensor(out=z2b[:], in0=yc_g[:, jj * T:(jj + 1) * T],
                                    in1=sgt[pp][:], op=ALU.mult).then_inc(sV.h, 1)
                    sV.n += 1
                    v.wait_ge(sV.h, sV.n)
                    v.tensor_add(out=z1t[pp][:], in0=z2b[:], in1=z1t[pp][:]
                                 ).then_inc(sV.h, 1)
                sV.n += 1
                z_v = sV.n

                @blk.sync
                def _(sync, dc=dc, pp=pp, z_v=z_v):
                    sync.wait_ge(sV.h, z_v)
                    qz.inc(sync.dma_start(out=z_dram[dc * 128:(dc + 1) * 128, :],
                                          in_=z1t[pp][:]), pp)
                z_st[dc] = qz.cnt[pp]

    engine_barrier(blk, sBar, 2)
    mid.close()

    with ExitStack() as pes:
        psAT = [_ps(pes, nc, f"psAT{i}", [128, 512]) for i in range(2)]
        psF = [_ps(pes, nc, f"psF{i}", [128, 512]) for i in range(2)]
        z_sb = _sb(pes, nc, "z_sb", [128, DC * T], BF16)
        wof = [_sb(pes, nc, f"wof{i}", [128, 1024], F32) for i in range(2)]
        wo = [_sb(pes, nc, f"wo{i}", [128, DC * 128], BF16) for i in range(2)]
        fbf = _sb(pes, nc, "fbf", [128, 8 * 128], F32)
        fbb = [_sb(pes, nc, f"fbb{i}", [128, 8 * 128], BF16) for i in range(2)]
        hgt = [_sb(pes, nc, f"hgt{i}", [128, 8 * 512], BF16) for i in range(2)]
        xres = _sb(pes, nc, "xres", [128, T], F32)
        o1 = _sb(pes, nc, "o1", [128, 512], F32)
        o2 = _sb(pes, nc, "o2", [128, 512], F32)
        out_s = _sb(pes, nc, "out_s", [128, T], F32)

        @blk.sync
        def _(sync):
            qz.wait_all(sync)
            qZ.inc(sync.dma_start(
                out=z_sb[:].rearrange("p (c t) -> p c t", c=DC),
                in_=z_dram.rearrange("(c p) t -> p c t", p=128)))
            for th in range(2):
                qZ.inc(sync.dma_start(
                    out=hgt[th][:].rearrange("p (k c) -> p k c", k=8),
                    in_=din["hg"].rearrange("(k p) t -> p k t", p=128)[
                        :, :, th * 512:(th + 1) * 512]))

        wo_marks = {}
        mm_marks = {}
        st_marks = {}
        o1_mark = [0]
        for hmt in range(32):
            for q in range(8):
                qi = hmt * 8 + q

                @blk.sync
                def _(sync, hmt=hmt, q=q, qi=qi, at=wo_marks.get(qi - 2, 0)):
                    if qi >= 2:
                        sync.wait_ge(sP.h, at)
                    qw.inc(sync.dma_start(
                        out=wof[q % 2][:],
                        in_=din["woT"][hmt][:, q * 1024:(q + 1) * 1024]), q % 2)

                @blk.gpsimd
                def _(g, hmt=hmt, q=q, at=qw.cnt[q % 2],
                      pat=mm_marks.get(hmt - 2, 0)):
                    g.wait_ge(qw.sems[q % 2], at)
                    if hmt >= 2 and q == 0:
                        g.wait_ge(sPE.h, pat)
                    g.tensor_copy(out=wo[hmt % 2][:, q * 1024:(q + 1) * 1024],
                                  in_=wof[q % 2][:]).then_inc(sP.h, 1)
                sP.n += 1
                wo_marks[qi] = sP.n
            wo_p = sP.n

            @blk.sync
            def _(sync, hmt=hmt, pat=mm_marks.get(hmt - 2, 0), vat=sV.n,
                  fat=sP.n):
                if hmt >= 2:
                    sync.wait_ge(sPE.h, pat)
                if hmt >= 1:
                    sync.wait_ge(sV.h, vat)
                    sync.wait_ge(sP.h, fat)
                qf.inc(sync.dma_start(out=fbf[:], in_=din["fbT"][hmt]))
                qf.inc(sync.dma_start(
                    out=xres[:], in_=din["hsT"][hmt * 128:(hmt + 1) * 128, :]))

            @blk.gpsimd
            def _(g, hmt=hmt, at=qf.cnt[0]):
                g.wait_ge(qf.sems[0], at)
                g.tensor_copy(out=fbb[hmt % 2][:], in_=fbf[:]
                              ).then_inc(sP.h, 1)
            sP.n += 1
            fb_p = sP.n

            for th in range(2):
                @blk.tensor
                def _(te, hmt=hmt, th=th, wo_p=wo_p, fb_p=fb_p,
                      aat=sA.n, vat=sV.n):
                    te.wait_ge(sP.h, fb_p)
                    qZ.wait_all(te)
                    if hmt >= 1:
                        te.wait_ge(sV.h, vat)
                    for dc in range(DC):
                        te.matmul(out=psAT[th][:, :],
                                  lhsT=wo[hmt % 2][:, dc * 128:(dc + 1) * 128],
                                  rhs=z_sb[:, dc * T + th * 512:
                                           dc * T + th * 512 + 512],
                                  start=(dc == 0), stop=(dc == DC - 1))
                    for kr in range(8):
                        mm = te.matmul(out=psF[th][:, :],
                                       lhsT=fbb[hmt % 2][:, kr * 128:(kr + 1) * 128],
                                       rhs=hgt[th][:, kr * 512:(kr + 1) * 512],
                                       start=(kr == 0), stop=(kr == 7))
                        if kr == 7:
                            mm.then_inc(sPE.h, 1)
                sPE.n += 1
            mm_marks[hmt] = sPE.n

            for th in range(2):
                mm_pe = sPE.n - 1 + th if False else mm_marks[hmt]

                @blk.scalar
                def _(s, hmt=hmt, th=th, mm_pe=mm_pe, xat=qf.cnt[0],
                      om=o1_mark[0]):
                    s.wait_ge(sPE.h, mm_pe - (1 if th == 0 else 0))
                    s.wait_ge(qf.sems[0], xat)
                    if om:
                        s.wait_ge(sV.h, om)
                    s.activation(out=o1[:], in_=psF[th][:, :], func=AF.Identity,
                                 scale=2.0).then_inc(sA.h, 1)
                sA.n += 1
                ev_a = sA.n

                @blk.vector
                def _(v, hmt=hmt, th=th, ev_a=ev_a, stm=st_marks.get(hmt - 1, 0)):
                    v.wait_ge(sV.h, sV.n)
                    v.wait_ge(sA.h, ev_a)
                    if hmt >= 1 and th == 0:
                        v.wait_ge(qo.sems[0], stm)
                    v.tensor_add(out=o2[:], in0=o1[:], in1=psAT[th][:, :]
                                 ).then_inc(sV.h, 1)
                    sV.n += 1
                    o1_mark[0] = sV.n
                    v.wait_ge(sV.h, sV.n)
                    v.tensor_add(out=out_s[:, th * 512:(th + 1) * 512],
                                 in0=o2[:],
                                 in1=xres[:, th * 512:(th + 1) * 512]
                                 ).then_inc(sV.h, 1)
                sV.n += 1
            o_v = sV.n

            @blk.sync
            def _(sync, hmt=hmt, o_v=o_v):
                sync.wait_ge(sV.h, o_v)
                qo.inc(sync.dma_start(out=out_t[hmt * 128:(hmt + 1) * 128, :],
                                      in_=out_s[:]))
            st_marks[hmt] = qo.cnt[0]

    @blk.sync
    def _(sync):
        qo.wait_all(sync)
        if DEV_TAPS:
            qtap.wait_all(sync)

    es.close()
    return nc



def _stage_l1(inputs):
    x = np.asarray(inputs["L0_attn_hidden_states"], np.float32)[0]
    W_in = np.asarray(inputs["L0_attn_W_in"], np.float32)
    conv_w = np.asarray(inputs["L0_attn_conv_w"], np.float32)
    conv_b = np.asarray(inputs["L0_attn_conv_b"], np.float32)
    Wx = np.asarray(inputs["L0_attn_Wx"], np.float32)
    dt_ln_w = np.asarray(inputs["L0_attn_dt_ln_w"], np.float32)
    b_ln_w = np.asarray(inputs["L0_attn_b_ln_w"], np.float32)
    c_ln_w = np.asarray(inputs["L0_attn_c_ln_w"], np.float32)
    Wdt = np.asarray(inputs["L0_attn_Wdt"], np.float32)
    dt_bias = np.asarray(inputs["L0_attn_dt_bias"], np.float32)
    A = np.asarray(inputs["L0_attn_A"], np.float32)
    D_param = np.asarray(inputs["L0_attn_D_param"], np.float32)
    fA = np.asarray(inputs["L0_ffn_A"], np.float32)
    Wr = np.asarray(inputs["L0_ffn_Wr"], np.float32)

    hsT = np.ascontiguousarray(x.T)
    winT = np.ascontiguousarray(
        W_in.T.reshape(HC, 128, 128, 128).transpose(2, 1, 0, 3)).reshape(
        128, 128, HC * 128)
    wxT = np.ascontiguousarray(Wx.T.reshape(DC, 128, 288))
    wdtT = np.ascontiguousarray(
        Wdt.T.reshape(2, 128, DC, 128).transpose(2, 1, 0, 3)).reshape(DC, 128, 256)
    a_st = np.ascontiguousarray(
        A.reshape(DC, 128, NST).transpose(1, 0, 2)).reshape(128, DC * NST)
    convw = np.ascontiguousarray(
        conv_w.reshape(DC, 128, 4).transpose(1, 0, 2)).reshape(128, DC * 4)
    def pad128(a):
        out = np.zeros((a.shape[0], 128), np.float32)
        out[:, :a.shape[1]] = a
        return out
    convb = pad128(np.ascontiguousarray(conv_b.reshape(DC, 128).T))
    dtb = pad128(np.ascontiguousarray(dt_bias.reshape(DC, 128).T))
    dparam = pad128(np.ascontiguousarray(D_param.reshape(DC, 128).T))
    dtlnw = pad128(np.ascontiguousarray(dt_ln_w.reshape(2, 128).T))
    bclnw = pad128(np.concatenate([b_ln_w, c_ln_w]).reshape(32, 1).astype(np.float32))
    bc_ones = np.zeros((32, 128), np.float32)
    bc_ones[0:16, 0] = 1.0
    bc_ones[16:32, 1] = 1.0
    bc_exp = np.zeros((2, 128), np.float32)
    bc_exp[0, 0:16] = 1.0
    bc_exp[1, 16:32] = 1.0
    eye128 = np.eye(128, dtype=np.float32)
    wr_st = np.ascontiguousarray(
        Wr.T.reshape(HC, 128, 64).transpose(1, 0, 2)).reshape(128, HC * 64)
    fAT = np.ascontiguousarray(fA.reshape(1024, H).T)
    ffnA = np.ascontiguousarray(
        fAT.reshape(HC, 128, 8, 128).transpose(2, 1, 0, 3)).reshape(
        8, 128, HC * 128)
    exp16 = np.zeros((8, 64, 128), np.float32)
    for c in range(8):
        for m in range(128):
            exp16[c, (c * 128 + m) // 16, m] = 1.0
    seln = np.zeros((32, 32 * 128), np.float32)
    for n in range(32):
        seln[n, n * 128:(n + 1) * 128] = 1.0

    shared = dict(winT=winT, wxT=wxT, wdtT=wdtT, a_st=a_st, convw=convw,
                  convb=convb, dtb=dtb, dparam=dparam, dtlnw=dtlnw, bclnw=bclnw,
                  bc_ones=bc_ones, bc_exp=bc_exp, eye128=eye128, wr_st=wr_st,
                  ffnA=ffnA, exp16=exp16, seln=seln)
    in_maps = []
    for c in range(NCORES):
        c0 = c * T
        halo = (np.zeros((H, 3), np.float32) if c == 0
                else np.ascontiguousarray(hsT[:, c0 - 3:c0]))
        m = dict(shared)
        m["hsT"] = np.ascontiguousarray(
            np.concatenate([halo, hsT[:, c0:c0 + T]], axis=1))
        in_maps.append(m)
    return in_maps


def _stage_l2(inputs, res1):
    x = np.asarray(inputs["L0_attn_hidden_states"], np.float32)[0]
    W_out = np.asarray(inputs["L0_attn_W_out"], np.float32)
    A = np.asarray(inputs["L0_attn_A"], np.float32)
    fB = np.asarray(inputs["L0_ffn_B"], np.float32)

    hsT = np.ascontiguousarray(x.T)
    a_st = np.ascontiguousarray(
        A.reshape(DC, 128, NST).transpose(1, 0, 2)).reshape(128, DC * NST)
    seln = np.zeros((32, 32 * 128), np.float32)
    for n in range(32):
        seln[n, n * 128:(n + 1) * 128] = 1.0
    woT = np.ascontiguousarray(
        W_out.T.reshape(DC, 128, 32, 128).transpose(2, 1, 0, 3)).reshape(
        32, 128, DC * 128)
    fbT = np.ascontiguousarray(
        fB.transpose(0, 2, 1).reshape(8, 128, 32, 128).transpose(2, 1, 0, 3)
    ).reshape(32, 128, 8 * 128)
    s_all = np.stack([np.asarray(res1.results[c]["s_out"], np.float32)
                      for c in range(NCORES)])
    ctot_all = np.stack([np.asarray(res1.results[c]["ctot_out"], np.float32)
                         for c in range(NCORES)])
    in_maps = []
    for c in range(NCORES):
        mA = np.zeros((128, 128), np.float32)
        mB = np.full((128, 128), 1.0e5, np.float32)
        mA[:, 0:c] = 1.0
        mB[:, 0:c] = 0.0
        r = res1.results[c]
        m = dict(
            z1=np.asarray(r["z1"]), sg=np.asarray(r["sg_out"], np.float32),
            ct=np.asarray(r["ct_out"], np.float32),
            cn=np.asarray(r["cn_out"], np.float32), hg=np.asarray(r["hg_out"]),
            s_all=s_all, ctot_all=ctot_all, a_st=a_st, maskA=mA, maskB=mB,
            seln=seln, woT=woT, fbT=fbT,
            hsT=np.ascontiguousarray(hsT[:, c * T:(c + 1) * T]),
        )
        in_maps.append(m)
    return in_maps


def kernel(**inputs):
    in_maps1 = _stage_l1(inputs)
    nc1 = build_l1()
    res1 = run_bass_kernel_spmd(nc1, in_maps1, core_ids=list(range(NCORES)))
    in_maps2 = _stage_l2(inputs, res1)
    nc2 = build_l2()
    res2 = run_bass_kernel_spmd(nc2, in_maps2, core_ids=list(range(NCORES)))
    out = np.concatenate(
        [np.asarray(res2.results[c]["out"], np.float32).T for c in range(NCORES)],
        axis=0)
    return out.reshape(1, 8192, H)


if __name__ == "__main__":
    pass


# version 03
# speedup vs baseline: 1.2363x; 1.0326x over previous
import numpy as np
from contextlib import ExitStack

import concourse.bass as bass
import concourse.mybir as mybir
from concourse.bass_utils import run_bass_kernel_spmd

F32 = mybir.dt.float32
BF16 = mybir.dt.bfloat16
AF = mybir.ActivationFunctionType
ALU = mybir.AluOpType
AX = mybir.AxisListType

NCORES = 8
H = 4096
DD = 8192
NST = 16
RR = 256
T = 1024
TH = T + 3
HC = H // 128
DC = DD // 128
GG = 8
NEG = -1.0e30

DEV_TAPS = False


def _sb(es, nc, name, shape, dt):
    return es.enter_context(nc.sbuf_tensor(name, shape, dt))


def _ps(es, nc, name, shape, dt=F32):
    return es.enter_context(nc.psum_tensor(name, shape, dt))


class Ctr:
    def __init__(self, es, nc, name):
        self.h = es.enter_context(nc.semaphore(name))
        self.n = 0


class DRing:

    def __init__(self, es, nc, name, slots):
        self.sems = [es.enter_context(nc.semaphore(f"{name}{i}"))
                     for i in range(slots)]
        self.cnt = [0] * slots

    def inc(self, instr, slot=0, k=16):
        instr.then_inc(self.sems[slot], k)
        self.cnt[slot] += k
        return self.cnt[slot]

    def wait(self, eng, slot=0, val=None):
        eng.wait_ge(self.sems[slot], self.cnt[slot] if val is None else val)

    def wait_all(self, eng):
        for i, s in enumerate(self.sems):
            eng.wait_ge(s, self.cnt[i])


def load_tbl(eng, set_id):
    eng.add_instruction(mybir.InstLoadActFuncSet(
        name=eng.bass.get_next_instruction_name(),
        act_func_set_id=set_id, ins=[], outs=[]))


def engine_barrier(blk, bar, k):
    for name in ["sync", "tensor", "scalar", "vector", "gpsimd"]:
        dec = getattr(blk, name)

        def _f(eng):
            eng.nop().then_inc(bar.h, 1)
        dec(_f)
    bar.n += 5
    tgt = bar.n
    for name in ["sync", "tensor", "scalar", "vector", "gpsimd"]:
        dec = getattr(blk, name)

        def _g(eng, tgt=tgt):
            eng.wait_ge(bar.h, tgt)
        dec(_g)



def build_l1():
    nc = bass.Bass("TRN2", target_bir_lowering=False)
    din = {}
    for name, shape in [
        ("hsT", [H, TH]),
        ("winT", [128, 128, HC * 128]),
        ("wxT", [DC, 128, 288]),
        ("wdtT", [DC, 128, 256]),
        ("a_st", [128, DC * NST]),
        ("convw", [128, DC * 4]),
        ("convb", [128, 128]),
        ("dtb", [128, 128]),
        ("dparam", [128, 128]),
        ("dtlnw", [128, 128]),
        ("bclnw", [32, 128]),
        ("bc_ones", [32, 128]),
        ("bc_exp", [2, 128]),
        ("eye128", [128, 128]),
        ("wr_st", [128, HC * 64]),
        ("ffnA", [8, 128, HC * 128]),
        ("exp16", [8, 64, 128]),
        ("seln", [32, 32 * 128]),
    ]:
        din[name] = nc.dram_tensor(name, shape, F32, kind="ExternalInput").ap()

    dout = {}
    for name, shape, dt in [
        ("z1", [DD, T], BF16),
        ("sg_out", [DD, T], F32),
        ("ct_out", [DD, T], F32),
        ("s_out", [128, DC * NST], F32),
        ("ctot_out", [128, DC], F32),
        ("cn_out", [16, T], F32),
        ("hg_out", [1024, T], BF16),
    ]:
        dout[name] = nc.dram_tensor(name, shape, dt, kind="ExternalOutput").ap()
    if DEV_TAPS:
        for name, shape, dt in [
            ("xn_tap", [128, HC * TH], BF16),
            ("u_tap", [DD, T], F32),
            ("ssm_tap", [288, T], F32),
            ("dtn_tap", [RR, T], BF16),
            ("bcn_tap", [32, T], F32),
            ("g_tap", [T, 64], F32),
            ("dl_tap", [DD, T], F32),
        ]:
            dout[name] = nc.dram_tensor(name, shape, dt, kind="ExternalOutput").ap()

    u_dram = nc.dram_tensor("u_dram", [DD, T], F32, kind="Internal").ap()
    sg_dram = nc.dram_tensor("sg_dram", [DD, T], F32, kind="Internal").ap()

    es = ExitStack()
    dtn0 = _sb(es, nc, "dtn0", [128, T], BF16)
    dtn1 = _sb(es, nc, "dtn1", [128, T], BF16)
    bcn = _sb(es, nc, "bcn", [32, T], F32)
    bchi = _sb(es, nc, "bchi", [32, T], BF16)
    bclo = _sb(es, nc, "bclo", [32, T], BF16)
    seln_sb = _sb(es, nc, "seln_sb", [32, 32 * 128], BF16)
    a_sb = _sb(es, nc, "a_sb", [128, DC * NST], F32)
    convw_sb = _sb(es, nc, "convw_sb", [128, DC * 4], F32)
    convb_sb = _sb(es, nc, "convb_sb", [128, 128], F32)
    dtb_sb = _sb(es, nc, "dtb_sb", [128, 128], F32)
    dparam_sb = _sb(es, nc, "dparam_sb", [128, 128], F32)
    dtlnw_sb = _sb(es, nc, "dtlnw_sb", [128, 128], F32)
    bclnw_sb = _sb(es, nc, "bclnw_sb", [32, 128], F32)
    bco_sb = _sb(es, nc, "bco_sb", [32, 2], BF16)
    bce_sb = _sb(es, nc, "bce_sb", [2, 32], BF16)
    eye_sb = _sb(es, nc, "eye_sb", [128, 128], F32)
    ones_col = _sb(es, nc, "ones_col", [128, 1], BF16)
    ones_row = _sb(es, nc, "ones_row", [1, 128], BF16)
    ctot_sb = _sb(es, nc, "ctot_sb", [128, DC], F32)
    s_cap = _sb(es, nc, "s_cap", [128, DC * NST], F32)
    cpool = _sb(es, nc, "cpool", [128, 4], F32)

    sPE = Ctr(es, nc, "sPE")
    sA = Ctr(es, nc, "sA")
    sV = Ctr(es, nc, "sV")
    sP = Ctr(es, nc, "sP")
    rc = DRing(es, nc, "rc", 1)
    rx = DRing(es, nc, "rx", 2)
    rw = DRing(es, nc, "rw", 2)
    ru = DRing(es, nc, "ru", 2)
    rs = DRing(es, nc, "rs", 2)
    rt = DRing(es, nc, "rt", 2)
    rd = DRing(es, nc, "rd", 2)
    rct = DRing(es, nc, "rct", 2)
    rzr = DRing(es, nc, "rz", 2)
    rf = DRing(es, nc, "rf", 2)
    rfw = DRing(es, nc, "rfw", 1)
    rh = DRing(es, nc, "rh", 2)
    rfin = DRing(es, nc, "rfin", 1)
    rtap = DRing(es, nc, "rtap", 1)
    sBar = Ctr(es, nc, "sBar")

    xn_es = ExitStack()
    xn = _sb(xn_es, nc, "xn", [128, HC * TH], BF16)
    ssm_es = ExitStack()
    ssm0 = _sb(ssm_es, nc, "ssm0", [128, T], F32)
    ssm1 = _sb(ssm_es, nc, "ssm1", [128, T], F32)
    ssmBC = _sb(ssm_es, nc, "ssmBC", [32, T], F32)

    blk = es.enter_context(nc.Block())

    cst_es = ExitStack()
    bcof = _sb(cst_es, nc, "bcof", [32, 128], F32)
    bcef = _sb(cst_es, nc, "bcef", [2, 128], F32)
    selnf = _sb(cst_es, nc, "selnf", [32, 32 * 128], F32)

    @blk.sync
    def _(sync):
        for t_, src in [
            (a_sb, din["a_st"]), (convw_sb, din["convw"]), (convb_sb, din["convb"]),
            (dtb_sb, din["dtb"]), (dparam_sb, din["dparam"]), (dtlnw_sb, din["dtlnw"]),
            (bclnw_sb, din["bclnw"]), (eye_sb, din["eye128"]),
            (bcof, din["bc_ones"]), (bcef, din["bc_exp"]), (selnf, din["seln"]),
        ]:
            rc.inc(sync.dma_start(out=t_[:], in_=src))

    for eng_name in ["tensor", "scalar", "vector", "gpsimd"]:
        getattr(blk, eng_name)(lambda eng: rc.wait_all(eng))

    @blk.vector
    def _(v):
        v.tensor_copy(out=bco_sb[:], in_=bcof[:, 0:2])
        v.tensor_copy(out=bce_sb[:], in_=bcef[:, 0:32])
        v.tensor_copy(out=seln_sb[:], in_=selnf[:])
        for i, val in enumerate([1e-5, 1e-6, 1.0]):
            v.memset(cpool[:, i:i + 1], val)
            nc.const_aps.register(F32, val, cpool[:, i:i + 1])
        v.memset(ones_col[:], 1.0)
        v.memset(ones_row[:], 1.0).then_inc(sV.h, 1)
        sV.n += 1

    CONST_V = sV.n


    @blk.scalar
    def _(s):
        load_tbl(s, 6)
    LNG = [(0, 512), (512, 512), (899, 128)]
    with ExitStack() as pes:
        psA_ = _ps(pes, nc, "psA", [1, 512])
        psB_ = _ps(pes, nc, "psB", [1, 512])
        psMU = _ps(pes, nc, "psMU", [128, 512])
        psRS = _ps(pes, nc, "psRS", [128, 512])
        xbuf = [_sb(pes, nc, f"xbuf{i}", [128, 512], F32) for i in range(2)]
        xbf = [_sb(pes, nc, f"xbf{i}", [128, 512], BF16) for i in range(2)]
        x2bf = [_sb(pes, nc, f"x2bf{i}", [128, 512], BF16) for i in range(2)]
        murow = _sb(pes, nc, "murow", [1, 512], F32)
        m2row = _sb(pes, nc, "m2row", [1, 512], F32)
        musq = _sb(pes, nc, "musq", [1, 512], F32)
        varrow = _sb(pes, nc, "varrow", [1, 512], F32)
        lnrow = _sb(pes, nc, "lnrow", [1, 512], F32)
        rstdrow = _sb(pes, nc, "rstdrow", [1, 512], F32)
        mu_hi = _sb(pes, nc, "mu_hi", [1, 512], BF16)
        mu_lo = _sb(pes, nc, "mu_lo", [1, 512], BF16)
        rs_hi = _sb(pes, nc, "rs_hi", [1, 512], BF16)
        rs_lo = _sb(pes, nc, "rs_lo", [1, 512], BF16)
        t1b = [_sb(pes, nc, f"t1b{i}", [128, 512], F32) for i in range(2)]

        for (c0, gw) in LNG:
            gV, gA, gPE = sV.n, sA.n, sPE.n
            src_g = din["hsT"][:, c0:c0 + gw].rearrange("(hc p) c -> hc p c", p=128)
            xv = []

            @blk.sync
            def _(sync, gV=gV, gA=gA, src_g=src_g, gw=gw, xv=xv):
                for k in range(HC):
                    if k >= 2:
                        sync.wait_ge(sV.h, gV + (k - 1))
                        sync.wait_ge(sA.h, gA + (k - 1))
                    else:
                        sync.wait_ge(sV.h, gV)
                    xv.append(rx.inc(sync.dma_start(out=xbuf[k % 2][:, :gw],
                                                    in_=src_g[k]), k % 2))

            @blk.vector
            def _(v, gPE=gPE, gw=gw, xv=xv):
                for k in range(HC):
                    v.wait_ge(rx.sems[k % 2], xv[k])
                    if k >= 2:
                        v.wait_ge(sPE.h, gPE + (k - 1))
                    else:
                        v.wait_ge(sPE.h, gPE)
                    v.tensor_copy(out=xbf[k % 2][:, :gw], in_=xbuf[k % 2][:, :gw]
                                  ).then_inc(sV.h, 1)
            sV.n += HC

            @blk.scalar
            def _(s, gPE=gPE, gw=gw, xv=xv):
                s.wait_ge(sA.h, sA.n)
                for k in range(HC):
                    s.wait_ge(rx.sems[k % 2], xv[k])
                    if k >= 2:
                        s.wait_ge(sPE.h, gPE + (k - 1))
                    else:
                        s.wait_ge(sPE.h, gPE)
                    s.activation(out=x2bf[k % 2][:, :gw], in_=xbuf[k % 2][:, :gw],
                                 func=AF.Square).then_inc(sA.h, 1)
            sA.n += HC

            @blk.tensor
            def _(te, gV=gV, gA=gA, gw=gw):
                te.wait_ge(sV.h, CONST_V)
                for k in range(HC):
                    te.wait_ge(sV.h, gV + (k + 1))
                    te.wait_ge(sA.h, gA + (k + 1))
                    te.matmul(out=psA_[:, :gw], lhsT=ones_col[:],
                              rhs=xbf[k % 2][:, :gw],
                              start=(k == 0), stop=(k == HC - 1))
                    te.matmul(out=psB_[:, :gw], lhsT=ones_col[:],
                              rhs=x2bf[k % 2][:, :gw],
                              start=(k == 0), stop=(k == HC - 1)).then_inc(sPE.h, 1)
            sPE.n += HC

            eV, eA, ePE = sV.n, sA.n, sPE.n

            @blk.vector
            def _(v, gw=gw, ePE=ePE, eA=eA):
                def edge(ins):
                    ins.then_inc(sV.h, 1)
                    sV.n += 1
                    v.wait_ge(sV.h, sV.n)
                v.wait_ge(sPE.h, ePE)
                edge(v.tensor_scalar_mul(out=murow[:, :gw], in0=psA_[:, :gw],
                                         scalar1=1.0 / H))
                edge(v.tensor_scalar_mul(out=m2row[:, :gw], in0=psB_[:, :gw],
                                         scalar1=1.0 / H))
                edge(v.tensor_mul(out=musq[:, :gw], in0=murow[:, :gw],
                                  in1=murow[:, :gw]))
                v.tensor_sub(out=varrow[:, :gw], in0=m2row[:, :gw],
                             in1=musq[:, :gw]).then_inc(sV.h, 1)
                sV.n += 1
                v.wait_ge(sA.h, eA + 2)
                edge(v.tensor_copy(out=mu_hi[:, :gw], in_=murow[:, :gw]))
                edge(v.tensor_copy(out=rs_hi[:, :gw], in_=rstdrow[:, :gw]))
                v.tensor_sub(out=mu_lo[:, :gw], in0=murow[:, :gw], in1=mu_hi[:, :gw])
                v.tensor_sub(out=rs_lo[:, :gw], in0=rstdrow[:, :gw],
                             in1=rs_hi[:, :gw]).then_inc(sV.h, 1)
                sV.n += 1
            pass

            @blk.scalar
            def _(s, gw=gw, eV=eV):
                s.wait_ge(sV.h, eV + 4)
                s.activation(out=lnrow[:, :gw], in_=varrow[:, :gw], func=AF.Ln,
                             bias=1e-5).then_inc(sA.h, 1)
                sA.n += 1
                s.wait_ge(sA.h, sA.n)
                s.activation(out=rstdrow[:, :gw], in_=lnrow[:, :gw], func=AF.Exp,
                             scale=-0.5).then_inc(sA.h, 1)
            sA.n += 1

            @blk.tensor
            def _(te, gw=gw, eV=eV):
                te.wait_ge(sV.h, eV + 7)
                te.matmul(out=psMU[:, :gw], lhsT=ones_row[:], rhs=mu_hi[:, :gw],
                          start=True, stop=False)
                te.matmul(out=psMU[:, :gw], lhsT=ones_row[:], rhs=mu_lo[:, :gw],
                          start=False, stop=True)
                te.matmul(out=psRS[:, :gw], lhsT=ones_row[:], rhs=rs_hi[:, :gw],
                          start=True, stop=False)
                te.matmul(out=psRS[:, :gw], lhsT=ones_row[:], rhs=rs_lo[:, :gw],
                          start=False, stop=True).then_inc(sPE.h, 1)
            sPE.n += 1

            p2V, p2PE = sV.n, sPE.n
            xv2 = []

            @blk.sync
            def _(sync, p2V=p2V, p2PE=p2PE, src_g=src_g, gw=gw, xv2=xv2):
                sync.wait_ge(sPE.h, p2PE)
                for k in range(HC):
                    if k >= 2:
                        sync.wait_ge(sV.h, p2V + 2 * (k - 1))
                    xv2.append(rx.inc(sync.dma_start(out=xbuf[k % 2][:, :gw],
                                                     in_=src_g[k]), k % 2))

            @blk.vector
            def _(v, c0=c0, gw=gw, p2PE=p2PE, xv2=xv2):
                v.wait_ge(sPE.h, p2PE)
                for k in range(HC):
                    v.wait_ge(rx.sems[k % 2], xv2[k])
                    v.tensor_sub(out=t1b[k % 2][:, :gw], in0=xbuf[k % 2][:, :gw],
                                 in1=psMU[:, :gw]).then_inc(sV.h, 1)
                    sV.n += 1
                    v.wait_ge(sV.h, sV.n)
                    v.tensor_mul(out=xn[:, k * TH + c0:k * TH + c0 + gw],
                                 in0=t1b[k % 2][:, :gw], in1=psRS[:, :gw]
                                 ).then_inc(sV.h, 1)
                    sV.n += 1
            pass

    LN_V = sV.n
    cst_es.close()
    engine_barrier(blk, sBar, 0)

    if DEV_TAPS:
        @blk.sync
        def _(sync):
            sync.wait_ge(sV.h, LN_V)
            rtap.inc(sync.dma_start(out=dout["xn_tap"], in_=xn[:]))


    @blk.scalar
    def _(s):
        load_tbl(s, 18)
    with ExitStack() as pes:
        psP = [_ps(pes, nc, f"psP{i}", [128, 512]) for i in range(2)]
        psS0 = [_ps(pes, nc, f"psS0{i}", [128, 512]) for i in range(2)]
        psS1 = [_ps(pes, nc, f"psS1{i}", [128, 512]) for i in range(2)]
        psS2 = [_ps(pes, nc, f"psS2{i}", [32, 512]) for i in range(2)]
        wch = [_sb(pes, nc, f"wch{i}", [128, HC * 128], BF16) for i in range(2)]
        wxch = [_sb(pes, nc, f"wxch{i}", [128, 288], BF16) for i in range(2)]
        wchf = [_sb(pes, nc, f"wchf{i}", [128, HC * 128], F32) for i in range(2)]
        wxchf = [_sb(pes, nc, f"wxchf{i}", [128, 288], F32) for i in range(2)]
        upre = _sb(pes, nc, "upre", [128, TH], F32)
        conv1 = _sb(pes, nc, "conv1", [128, T], F32)
        ubuf = [_sb(pes, nc, f"ubuf{i}", [128, T], F32) for i in range(2)]
        ubf = [_sb(pes, nc, f"ubf{i}", [128, T], BF16) for i in range(2)]
        sgb = [_sb(pes, nc, f"sgb{i}", [128, T], F32) for i in range(2)]
        @blk.vector
        def _(v):
            v.memset(ssm0[:], 0.0)
            v.memset(ssm1[:], 0.0)
            v.memset(ssmBC[:], 0.0).then_inc(sV.h, 1)
        sV.n += 1

        MEMSET_V = sV.n
        pe_chunk = []
        evac_marks = {}
        silu_marks = {}
        store_marks = {}
        conv_marks = {}
        wx_marks = {}
        cast_marks = {}
        ssm_add_marks = []

        for j in range(128):
            is_u = j < 64
            jj = j if is_u else j - 64

            @blk.sync
            def _(sync, j=j, is_u=is_u):
                if j >= 2:
                    sync.wait_ge(sP.h, cast_marks[j - 2])
                for k8 in range(8):
                    cs = slice(k8 * 512, (k8 + 1) * 512)
                    rw.inc(sync.dma_start(out=wchf[j % 2][:, cs],
                                          in_=din["winT"][j][:, cs]), j % 2)
                if is_u:
                    rw.inc(sync.dma_start(out=wxchf[j % 2][:], in_=din["wxT"][j]),
                           j % 2)
            wLd = rw.cnt[j % 2]

            @blk.gpsimd
            def _(g, j=j, is_u=is_u, wLd=wLd):
                g.wait_ge(rw.sems[j % 2], wLd)
                if j >= 2:
                    g.wait_ge(sPE.h, pe_chunk[j - 2])
                g.tensor_copy(out=wch[j % 2][:], in_=wchf[j % 2][:]).then_inc(sP.h, 1)
                sP.n += 1
                if is_u:
                    g.tensor_copy(out=wxch[j % 2][:], in_=wxchf[j % 2][:]
                                  ).then_inc(sP.h, 1)
                    sP.n += 1
            cast_marks[j] = sP.n
            wG = sP.n

            if is_u:
                groups = [(0, 512, 0), (512, 512, 1), (899, 128, 0)]
            else:
                groups = [(3, 512, 0), (515, 512, 1)]

            g_pe_marks = []
            for gi, (c0, gw, pb) in enumerate(groups):
                @blk.tensor
                def _(te, j=j, gi=gi, c0=c0, gw=gw, pb=pb, wG=wG):
                    if j == 0 and gi == 0:
                        te.wait_ge(sV.h, MEMSET_V)
                    te.wait_ge(sP.h, wG)
                    if gi >= 2:
                        te.wait_ge(sA.h, evac_marks[(j, 0)])
                    elif j >= 1:
                        prev_groups = 3 if j - 1 < 64 else 2
                        for pgi in range(prev_groups - 1, -1, -1):
                            ppb = [0, 1, 0][pgi] if j - 1 < 64 else [0, 1][pgi]
                            if ppb == pb:
                                te.wait_ge(sA.h, evac_marks[(j - 1, pgi)])
                                break
                    for hc in range(HC):
                        mm = te.matmul(out=psP[pb][:, :gw],
                                       lhsT=wch[j % 2][:, hc * 128:(hc + 1) * 128],
                                       rhs=xn[:, hc * TH + c0:hc * TH + c0 + gw],
                                       start=(hc == 0), stop=(hc == HC - 1))
                        if hc == HC - 1:
                            mm.then_inc(sPE.h, 1)
                sPE.n += 1
                g_pe_marks.append(sPE.n)

                if is_u:
                    dsts = [(0, 512), (512, 512), (899, 128)]
                    d0, dw = dsts[gi]

                    @blk.scalar
                    def _(s, j=j, gi=gi, pb=pb, d0=d0, dw=dw, mark=sPE.n):
                        s.wait_ge(sA.h, sA.n)
                        s.wait_ge(sPE.h, mark)
                        if gi == 0 and j >= 1:
                            s.wait_ge(sV.h, conv_marks[j - 1])
                        s.copy(out=upre[:, d0:d0 + dw], in_=psP[pb][:, :dw]
                               ).then_inc(sA.h, 1)
                    sA.n += 1
                    evac_marks[(j, gi)] = sA.n
                else:
                    dsts = [(0, 512), (512, 512)]
                    d0, dw = dsts[gi]

                    @blk.scalar
                    def _(s, j=j, jj=jj, gi=gi, pb=pb, d0=d0, dw=dw, mark=sPE.n):
                        s.wait_ge(sA.h, sA.n)
                        s.wait_ge(sPE.h, mark)
                        if gi == 0 and jj >= 2:
                            s.wait_ge(rs.sems[jj % 2], store_marks[64 + jj - 2])
                        s.activation(out=sgb[jj % 2][:, d0:d0 + dw],
                                     in_=psP[pb][:, :dw], func=AF.Silu
                                     ).then_inc(sA.h, 1)
                    sA.n += 1
                    evac_marks[(j, gi)] = sA.n

            pe_chunk.append(sPE.n)

            if is_u:
                up_done = evac_marks[(j, 2)]

                @blk.vector
                def _(v, jj=jj, up_done=up_done):
                    v.wait_ge(sV.h, sV.n)
                    v.wait_ge(sA.h, up_done)
                    if jj >= 1:
                        v.wait_ge(sA.h, silu_marks[jj - 1])
                    v.tensor_scalar(out=conv1[:], in0=upre[:, 0:T],
                                    scalar1=convw_sb[:, 4 * jj:4 * jj + 1],
                                    scalar2=None, op0=ALU.mult).then_inc(sV.h, 1)
                    sV.n += 1
                    for kk in range(1, 4):
                        v.wait_ge(sV.h, sV.n)
                        v.scalar_tensor_tensor(
                            out=conv1[:], in0=upre[:, kk:T + kk],
                            scalar=convw_sb[:, 4 * jj + kk:4 * jj + kk + 1],
                            in1=conv1[:], op0=ALU.mult, op1=ALU.add
                        ).then_inc(sV.h, 1)
                        sV.n += 1
                pass
                conv_done = sV.n
                conv_marks[jj] = conv_done

                @blk.scalar
                def _(s, jj=jj, conv_done=conv_done):
                    s.wait_ge(sA.h, sA.n)
                    s.wait_ge(sV.h, conv_done)
                    if jj >= 2:
                        s.wait_ge(ru.sems[jj % 2], store_marks[jj - 2])
                    s.activation(out=ubuf[jj % 2][:], in_=conv1[:], func=AF.Silu,
                                 bias=convb_sb[:, jj:jj + 1]).then_inc(sA.h, 1)
                sA.n += 1
                silu_marks[jj] = sA.n
                u_done = sA.n

                @blk.vector
                def _(v, jj=jj, u_done=u_done):
                    v.wait_ge(sV.h, sV.n)
                    v.wait_ge(sA.h, u_done)
                    if jj >= 2:
                        v.wait_ge(sPE.h, wx_marks[jj - 2])
                    v.tensor_copy(out=ubf[jj % 2][:], in_=ubuf[jj % 2][:]
                                  ).then_inc(sV.h, 1)
                sV.n += 1
                ubf_done = sV.n

                @blk.tensor
                def _(te, j=j, jj=jj, ubf_done=ubf_done):
                    te.wait_ge(sV.h, ubf_done)
                    st = (jj % 16 == 0)
                    sp = (jj % 16 == 15)
                    for th in range(2):
                        rhs = ubf[jj % 2][:, th * 512:(th + 1) * 512]
                        te.matmul(out=psS0[th][:, :], lhsT=wxch[j % 2][:, 0:128],
                                  rhs=rhs, start=st, stop=sp)
                        te.matmul(out=psS1[th][:, :], lhsT=wxch[j % 2][:, 128:256],
                                  rhs=rhs, start=st, stop=sp)
                        mm = te.matmul(out=psS2[th][:, :],
                                       lhsT=wxch[j % 2][:, 256:288],
                                       rhs=rhs, start=st, stop=sp)
                        if th == 1:
                            mm.then_inc(sPE.h, 1)
                sPE.n += 1
                wx_pe = sPE.n
                wx_marks[jj] = sPE.n
                pe_chunk[-1] = sPE.n

                if jj % 16 == 15:
                    @blk.vector
                    def _(v, wx_pe=wx_pe, prevm=ssm_add_marks[-1] if ssm_add_marks
                          else 0):
                        v.wait_ge(sPE.h, wx_pe)
                        if prevm:
                            v.wait_ge(sV.h, prevm)
                        for th in range(2):
                            sl = slice(th * 512, (th + 1) * 512)
                            v.tensor_add(out=ssm0[:, sl], in0=ssm0[:, sl],
                                         in1=psS0[th][:, :])
                            v.tensor_add(out=ssm1[:, sl], in0=ssm1[:, sl],
                                         in1=psS1[th][:, :])
                            ins = v.tensor_add(out=ssmBC[:, sl], in0=ssmBC[:, sl],
                                               in1=psS2[th][:, :])
                            if th == 1:
                                ins.then_inc(sV.h, 1)
                    sV.n += 1
                    ssm_add_marks.append(sV.n)

                @blk.sync
                def _(sync, jj=jj, u_done=u_done, ubf_done=ubf_done):
                    sync.wait_ge(sA.h, u_done)
                    sync.wait_ge(sV.h, ubf_done)
                    ru.inc(sync.dma_start(out=u_dram[jj * 128:(jj + 1) * 128, :],
                                          in_=ubuf[jj % 2][:]), jj % 2)
                    if DEV_TAPS:
                        ru.inc(sync.dma_start(
                            out=dout["u_tap"][jj * 128:(jj + 1) * 128, :],
                            in_=ubuf[jj % 2][:]), jj % 2)
                store_marks[jj] = ru.cnt[jj % 2]
            else:
                sg_done = evac_marks[(j, 1)]

                @blk.sync
                def _(sync, jj=jj, sg_done=sg_done):
                    sync.wait_ge(sA.h, sg_done)
                    rs.inc(sync.dma_start(out=sg_dram[jj * 128:(jj + 1) * 128, :],
                                          in_=sgb[jj % 2][:]), jj % 2)
                    rs.inc(sync.dma_start(
                        out=dout["sg_out"][jj * 128:(jj + 1) * 128, :],
                        in_=sgb[jj % 2][:]), jj % 2)
                store_marks[64 + jj] = rs.cnt[jj % 2]

    PROJ_V, PROJ_A, PROJ_PE = sV.n, sA.n, sPE.n

    if DEV_TAPS:
        @blk.sync
        def _(sync):
            sync.wait_ge(sV.h, PROJ_V)
            rtap.inc(sync.dma_start(out=dout["ssm_tap"][0:128, :], in_=ssm0[:]))
            rtap.inc(sync.dma_start(out=dout["ssm_tap"][128:256, :], in_=ssm1[:]))
            rtap.inc(sync.dma_start(out=dout["ssm_tap"][256:288, :], in_=ssmBC[:]))

    @blk.sync
    def _(sync):
        ru.wait_all(sync)
        rs.wait_all(sync)
        if DEV_TAPS:
            rtap.wait_all(sync)

    engine_barrier(blk, sBar, 0)


    @blk.scalar
    def _(s):
        load_tbl(s, 6)
    with ExitStack() as pes:
        psq = [_ps(pes, nc, f"psq{i}", [1, 512]) for i in range(2)]
        psq2 = [_ps(pes, nc, f"psq2{i}", [2, 512]) for i in range(2)]
        psRD = [_ps(pes, nc, f"psRD{i}", [128, 512]) for i in range(2)]
        psRB = [_ps(pes, nc, f"psRB{i}", [32, 512]) for i in range(2)]
        dsq0 = _sb(pes, nc, "dsq0", [128, T], BF16)
        dsq1 = _sb(pes, nc, "dsq1", [128, T], BF16)
        bsq = _sb(pes, nc, "bsq", [32, T], BF16)
        r_dt = _sb(pes, nc, "r_dt", [1, T], F32)
        r_bc = _sb(pes, nc, "r_bc", [2, T], F32)
        lnt = _sb(pes, nc, "lnt", [2, T], F32)
        rdt_hi = _sb(pes, nc, "rdt_hi", [1, T], BF16)
        rdt_lo = _sb(pes, nc, "rdt_lo", [1, T], BF16)
        rbc_hi = _sb(pes, nc, "rbc_hi", [2, T], BF16)
        rbc_lo = _sb(pes, nc, "rbc_lo", [2, T], BF16)

        @blk.scalar
        def _(s):
            s.wait_ge(sV.h, PROJ_V)
            s.activation(out=dsq0[:], in_=ssm0[:], func=AF.Square)
            s.activation(out=dsq1[:], in_=ssm1[:], func=AF.Square)
            s.activation(out=bsq[:], in_=ssmBC[:], func=AF.Square).then_inc(sA.h, 1)
        sA.n += 1
        sq_done = sA.n

        @blk.tensor
        def _(te, sq_done=sq_done):
            te.wait_ge(sA.h, sq_done)
            te.wait_ge(sV.h, CONST_V)
            for th in range(2):
                sl = slice(th * 512, (th + 1) * 512)
                te.matmul(out=psq[th][:, :], lhsT=ones_col[:], rhs=dsq0[:, sl],
                          start=True, stop=False)
                te.matmul(out=psq[th][:, :], lhsT=ones_col[:], rhs=dsq1[:, sl],
                          start=False, stop=True)
                mm = te.matmul(out=psq2[th][:, :], lhsT=bco_sb[:], rhs=bsq[:, sl],
                               start=True, stop=True)
                if th == 1:
                    mm.then_inc(sPE.h, 1)
        sPE.n += 1
        q_done = sPE.n

        @blk.scalar
        def _(s, q_done=q_done):
            s.wait_ge(sPE.h, q_done)
            for th in range(2):
                sl = slice(th * 512, (th + 1) * 512)
                s.activation(out=r_dt[:, sl], in_=psq[th][:, :], func=AF.Ln,
                             scale=1.0 / RR, bias=1e-6)
                s.activation(out=lnt[:, sl], in_=psq2[th][:, :], func=AF.Ln,
                             scale=1.0 / NST, bias=1e-6).then_inc(sA.h, 1)
                sA.n += 1
            s.wait_ge(sA.h, sA.n)
            s.activation(out=r_dt[:], in_=r_dt[:], func=AF.Exp, scale=-0.5)
            s.activation(out=r_bc[:], in_=lnt[:], func=AF.Exp, scale=-0.5
                         ).then_inc(sA.h, 1)
        sA.n += 1
        r_done = sA.n

        @blk.vector
        def _(v, r_done=r_done):
            v.wait_ge(sA.h, r_done)
            v.tensor_copy(out=rdt_hi[:], in_=r_dt[:]).then_inc(sV.h, 1)
            sV.n += 1
            v.wait_ge(sV.h, sV.n)
            v.tensor_sub(out=rdt_lo[:], in0=r_dt[:], in1=rdt_hi[:])
            v.tensor_copy(out=rbc_hi[:], in_=r_bc[:]).then_inc(sV.h, 1)
            sV.n += 1
            v.wait_ge(sV.h, sV.n)
            v.tensor_sub(out=rbc_lo[:], in0=r_bc[:], in1=rbc_hi[:]).then_inc(sV.h, 1)
        sV.n += 1
        hl_done = sV.n

        @blk.tensor
        def _(te, hl_done=hl_done):
            te.wait_ge(sV.h, hl_done)
            for th in range(2):
                sl = slice(th * 512, (th + 1) * 512)
                te.matmul(out=psRD[th][:, :], lhsT=ones_row[:], rhs=rdt_hi[:, sl],
                          start=True, stop=False)
                te.matmul(out=psRD[th][:, :], lhsT=ones_row[:], rhs=rdt_lo[:, sl],
                          start=False, stop=True)
                te.matmul(out=psRB[th][:, :], lhsT=bce_sb[:], rhs=rbc_hi[:, sl],
                          start=True, stop=False)
                mm = te.matmul(out=psRB[th][:, :], lhsT=bce_sb[:], rhs=rbc_lo[:, sl],
                               start=False, stop=True)
                if th == 1:
                    mm.then_inc(sPE.h, 1)
        sPE.n += 1
        bb_done = sPE.n

        @blk.vector
        def _(v, bb_done=bb_done):
            v.wait_ge(sPE.h, bb_done)
            for th in range(2):
                sl = slice(th * 512, (th + 1) * 512)
                v.scalar_tensor_tensor(out=dtn0[:, sl], in0=ssm0[:, sl],
                                       scalar=dtlnw_sb[:, 0:1], in1=psRD[th][:, :],
                                       op0=ALU.mult, op1=ALU.mult)
                v.scalar_tensor_tensor(out=dtn1[:, sl], in0=ssm1[:, sl],
                                       scalar=dtlnw_sb[:, 1:2], in1=psRD[th][:, :],
                                       op0=ALU.mult, op1=ALU.mult)
                v.scalar_tensor_tensor(out=bcn[:, sl], in0=ssmBC[:, sl],
                                       scalar=bclnw_sb[:, 0:1], in1=psRB[th][:, :],
                                       op0=ALU.mult, op1=ALU.mult).then_inc(sV.h, 1)
                sV.n += 1
            v.wait_ge(sV.h, sV.n)
            v.tensor_copy(out=bchi[:], in_=bcn[:]).then_inc(sV.h, 1)
            sV.n += 1
            v.wait_ge(sV.h, sV.n)
            v.tensor_sub(out=bclo[:], in0=bcn[:], in1=bchi[:]).then_inc(sV.h, 1)
        sV.n += 1
        bcn_done = sV.n

        @blk.sync
        def _(sync, bcn_done=bcn_done):
            sync.wait_ge(sV.h, bcn_done)
            rfin.inc(sync.dma_start(out=dout["cn_out"], in_=bcn[16:32, :]))
            if DEV_TAPS:
                rtap.inc(sync.dma_start(out=dout["dtn_tap"][0:128, :], in_=dtn0[:]))
                rtap.inc(sync.dma_start(out=dout["dtn_tap"][128:256, :], in_=dtn1[:]))
                rtap.inc(sync.dma_start(out=dout["bcn_tap"], in_=bcn[:]))

    B_V = sV.n

    engine_barrier(blk, sBar, 0)

    with ExitStack() as pes:
        psSC = _ps(pes, nc, "psSC", [128, 64])
        psGT = _ps(pes, nc, "psGT", [64, 128])
        psH = [_ps(pes, nc, f"psH{i}", [128, 512]) for i in range(2)]
        psGB = [_ps(pes, nc, f"psGB{i}", [128, 512]) for i in range(2)]
        wrb = _sb(pes, nc, "wrb", [128, HC * 64], BF16)
        ach = [_sb(pes, nc, f"ach{i}", [128, HC * 128], BF16) for i in range(2)]
        e16b = [_sb(pes, nc, f"e16b{i}", [64, 128], BF16) for i in range(2)]
        sc = _sb(pes, nc, "sc", [128, 64], F32)
        wk = _sb(pes, nc, "wk", [128, 64], F32)
        wk2 = _sb(pes, nc, "wk2", [128, 64], F32)
        wk3 = _sb(pes, nc, "wk3", [128, 64], F32)
        vals = _sb(pes, nc, "vals", [128, 16], F32)
        nmax = _sb(pes, nc, "nmax", [128, 1], F32)
        e16v = _sb(pes, nc, "e16v", [128, 16], F32)
        zrow = _sb(pes, nc, "zrow", [128, 1], F32)
        rz = _sb(pes, nc, "rz", [128, 1], F32)
        mask = _sb(pes, nc, "mask", [128, 64], F32)
        eo = _sb(pes, nc, "eo", [128, 64], F32)
        gmat = _sb(pes, nc, "gmat", [128, 64], F32)
        gt_s = _sb(pes, nc, "gt_s", [64, T], F32)
        gt_hi = _sb(pes, nc, "gt_hi", [64, T], BF16)
        gt_lo = _sb(pes, nc, "gt_lo", [64, T], BF16)
        h_s = _sb(pes, nc, "h_s", [128, T], F32)
        hg_b = [_sb(pes, nc, f"hg_b{i}", [128, T], BF16) for i in range(2)]

        wrbf = _sb(pes, nc, "wrbf", [128, HC * 64], F32)
        achf = [_sb(pes, nc, f"achf{i}", [128, HC * 128], F32) for i in range(2)]
        e16f = [_sb(pes, nc, f"e16f{i}", [64, 128], F32) for i in range(2)]

        @blk.sync
        def _(sync):
            rfw.inc(sync.dma_start(out=wrbf[:], in_=din["wr_st"]))

        @blk.gpsimd
        def _(g):
            rfw.wait(g)
            g.tensor_copy(out=wrb[:], in_=wrbf[:]).then_inc(sP.h, 1)
            sP.n += 1
        wr_g = sP.n

        for i in range(8):
            @blk.tensor
            def _(te, i=i, wr_g=wr_g):
                te.wait_ge(sP.h, wr_g)
                te.wait_ge(sV.h, B_V if i == 0 else sV.n)
                for hc in range(HC):
                    mm = te.matmul(
                        out=psSC[:, :],
                        lhsT=xn[:, hc * TH + 3 + i * 128: hc * TH + 3 + (i + 1) * 128],
                        rhs=wrb[:, hc * 64:(hc + 1) * 64],
                        start=(hc == 0), stop=(hc == HC - 1))
                    if hc == HC - 1:
                        mm.then_inc(sPE.h, 1)
            sPE.n += 1
            sc_done = sPE.n

            @blk.vector
            def _(v, sc_done=sc_done):
                v.wait_ge(sV.h, sV.n)
                v.wait_ge(sPE.h, sc_done)
                v.tensor_copy(out=sc[:], in_=psSC[:, :]).then_inc(sV.h, 1)
                sV.n += 1
                v.wait_ge(sV.h, sV.n)
                v.tensor_copy(out=wk[:], in_=sc[:]).then_inc(sV.h, 1)
                sV.n += 1
                v.wait_ge(sV.h, sV.n)
                v.max(out=vals[:, 0:8], in_=wk[:]).then_inc(sV.h, 1)
                sV.n += 1
                v.wait_ge(sV.h, sV.n)
                v.match_replace(out=wk2[:], in_to_replace=vals[:, 0:8],
                                in_values=wk[:], imm_value=NEG).then_inc(sV.h, 1)
                sV.n += 1
                v.wait_ge(sV.h, sV.n)
                v.max(out=vals[:, 8:16], in_=wk2[:]).then_inc(sV.h, 1)
                sV.n += 1
                v.wait_ge(sV.h, sV.n)
                v.match_replace(out=wk3[:], in_to_replace=vals[:, 8:16],
                                in_values=wk2[:], imm_value=NEG).then_inc(sV.h, 1)
                sV.n += 1
                v.wait_ge(sV.h, sV.n)
                v.tensor_scalar_mul(out=nmax[:], in0=vals[:, 0:1], scalar1=-1.0
                                    ).then_inc(sV.h, 1)
            sV.n += 1
            tk_done = sV.n

            @blk.scalar
            def _(s, tk_done=tk_done):
                s.wait_ge(sA.h, sA.n)
                s.wait_ge(sV.h, tk_done)
                s.activation(out=e16v[:], in_=vals[:], func=AF.Exp,
                             bias=nmax[:, 0:1])
                s.activation(out=eo[:], in_=sc[:], func=AF.Exp,
                             bias=nmax[:, 0:1]).then_inc(sA.h, 1)
            sA.n += 1
            exp_done = sA.n

            @blk.vector
            def _(v, exp_done=exp_done):
                v.wait_ge(sV.h, sV.n)
                v.wait_ge(sA.h, exp_done)
                v.reduce_sum(out=zrow[:], in_=e16v[:], axis=AX.X).then_inc(sV.h, 1)
                sV.n += 1
                v.wait_ge(sV.h, sV.n)
                v.reciprocal(out=rz[:], in_=zrow[:]).then_inc(sV.h, 1)
                sV.n += 1
                v.wait_ge(sV.h, sV.n)
                v.tensor_tensor(out=mask[:], in0=sc[:], in1=wk3[:], op=ALU.is_gt
                                ).then_inc(sV.h, 1)
                sV.n += 1
                v.wait_ge(sV.h, sV.n)
                v.scalar_tensor_tensor(out=gmat[:], in0=eo[:], scalar=rz[:, 0:1],
                                       in1=mask[:], op0=ALU.mult, op1=ALU.mult
                                       ).then_inc(sV.h, 1)
            sV.n += 1
            g_done = sV.n

            @blk.tensor
            def _(te, g_done=g_done):
                te.wait_ge(sV.h, g_done)
                te.transpose(out=psGT[:, :], in_=gmat[:], identity=eye_sb[:]
                             ).then_inc(sPE.h, 1)
            sPE.n += 1
            t_done = sPE.n

            @blk.vector
            def _(v, i=i, t_done=t_done):
                v.wait_ge(sV.h, sV.n)
                v.wait_ge(sPE.h, t_done)
                v.tensor_copy(out=gt_s[:, i * 128:(i + 1) * 128], in_=psGT[:, :]
                              ).then_inc(sV.h, 1)
            sV.n += 1

            if DEV_TAPS:
                gm_done = sV.n

                @blk.sync
                def _(sync, i=i, gm_done=gm_done):
                    sync.wait_ge(sV.h, gm_done)
                    rtap.inc(sync.dma_start(
                        out=dout["g_tap"][i * 128:(i + 1) * 128, :], in_=gmat[:]))

        @blk.vector
        def _(v, at=sV.n):
            v.wait_ge(sV.h, at)
            v.tensor_copy(out=gt_hi[:], in_=gt_s[:]).then_inc(sV.h, 1)
            sV.n += 1
            v.wait_ge(sV.h, sV.n)
            v.tensor_sub(out=gt_lo[:], in0=gt_s[:], in1=gt_hi[:]).then_inc(sV.h, 1)
        sV.n += 1
        gt_done = sV.n

        hg_marks = {}
        for c in range(8):
            @blk.sync
            def _(sync, c=c, at=sP.n):
                if c >= 2:
                    sync.wait_ge(sP.h, at)
                for k8 in range(8):
                    cs = slice(k8 * 512, (k8 + 1) * 512)
                    rf.inc(sync.dma_start(out=achf[c % 2][:, cs],
                                          in_=din["ffnA"][c][:, cs]), c % 2)
                rf.inc(sync.dma_start(out=e16f[c % 2][:], in_=din["exp16"][c]),
                       c % 2)

            @blk.gpsimd
            def _(g, c=c, at=rf.cnt[c % 2], pat=sPE.n):
                g.wait_ge(rf.sems[c % 2], at)
                if c >= 2:
                    g.wait_ge(sPE.h, pat)
                g.tensor_copy(out=ach[c % 2][:], in_=achf[c % 2][:])
                g.tensor_copy(out=e16b[c % 2][:], in_=e16f[c % 2][:]
                              ).then_inc(sP.h, 1)
                sP.n += 1
            a_g = sP.n

            @blk.tensor
            def _(te, c=c, a_g=a_g, gt_done=gt_done):
                te.wait_ge(sP.h, a_g)
                te.wait_ge(sV.h, gt_done)
                if c >= 1:
                    te.wait_ge(sV.h, hg_marks[c - 1])
                for th in range(2):
                    for hc in range(HC):
                        te.matmul(
                            out=psH[th][:, :],
                            lhsT=ach[c % 2][:, hc * 128:(hc + 1) * 128],
                            rhs=xn[:, hc * TH + 3 + th * 512:
                                    hc * TH + 3 + th * 512 + 512],
                            start=(hc == 0), stop=(hc == HC - 1))
                for th in range(2):
                    sl = slice(th * 512, (th + 1) * 512)
                    te.matmul(out=psGB[th][:, :], lhsT=e16b[c % 2][:],
                              rhs=gt_hi[:, sl], start=True, stop=False)
                    mm = te.matmul(out=psGB[th][:, :], lhsT=e16b[c % 2][:],
                                   rhs=gt_lo[:, sl], start=False, stop=True)
                    if th == 1:
                        mm.then_inc(sPE.h, 1)
            sPE.n += 1
            h_done = sPE.n

            @blk.scalar
            def _(s, h_done=h_done):
                s.wait_ge(sA.h, sA.n)
                s.wait_ge(sPE.h, h_done)
                s.copy(out=h_s[:, 0:512], in_=psH[0][:, :])
                s.copy(out=h_s[:, 512:1024], in_=psH[1][:, :]).then_inc(sA.h, 1)
            sA.n += 1
            he_done = sA.n

            @blk.vector
            def _(v, c=c, he_done=he_done):
                v.wait_ge(sV.h, sV.n)
                v.wait_ge(sA.h, he_done)
                if c >= 2:
                    v.wait_ge(rh.sems[c % 2], hg_marks[(c - 2, "st")])
                v.tensor_tensor(out=hg_b[c % 2][:, 0:512], in0=h_s[:, 0:512],
                                in1=psGB[0][:, :], op=ALU.mult)
                v.tensor_tensor(out=hg_b[c % 2][:, 512:1024], in0=h_s[:, 512:1024],
                                in1=psGB[1][:, :], op=ALU.mult).then_inc(sV.h, 1)
            sV.n += 1
            hg_marks[c] = sV.n

            @blk.sync
            def _(sync, c=c, at=sV.n):
                sync.wait_ge(sV.h, at)
                rh.inc(sync.dma_start(out=dout["hg_out"][c * 128:(c + 1) * 128, :],
                                      in_=hg_b[c % 2][:]), c % 2)
            hg_marks[(c, "st")] = rh.cnt[c % 2]

    ssm_es.close()
    xn_es.close()

    @blk.sync
    def _(sync):
        rh.wait_all(sync)
        if DEV_TAPS:
            rtap.wait_all(sync)

    engine_barrier(blk, sBar, 0)

    with ExitStack() as pes:
        psD = [_ps(pes, nc, f"psD{i}", [128, 512]) for i in range(2)]
        psBB = [_ps(pes, nc, f"psBB{i}", [128, 512]) for i in range(2)]
        psCB = [_ps(pes, nc, f"psCB{i}", [128, 512]) for i in range(2)]
        wdch = [_sb(pes, nc, f"wdch{i}", [128, 256], BF16) for i in range(2)]
        wdchf = [_sb(pes, nc, f"wdchf{i}", [128, 256], F32) for i in range(2)]
        dl_g = _sb(pes, nc, "dl_g", [128, GG * T], F32)
        du_g = _sb(pes, nc, "du_g", [128, GG * T], F32)
        y_g = _sb(pes, nc, "y_g", [128, GG * T], F32)
        ab1 = _sb(pes, nc, "ab1", [128, 512], F32)
        exn = _sb(pes, nc, "exn", [128, 512], F32)
        l1p = _sb(pes, nc, "l1p", [128, 512], F32)
        rlu = _sb(pes, nc, "rlu", [128, 512], F32)
        utmp = [_sb(pes, nc, f"utmp{i}", [128, T], F32) for i in range(2)]
        sgt = [_sb(pes, nc, f"sgt{i}", [128, T], F32) for i in range(2)]
        z1b = [_sb(pes, nc, f"z1b{i}", [128, T], BF16) for i in range(2)]
        ctt = [_sb(pes, nc, f"ctt{i}", [128, T], F32) for i in range(2)]
        bb_s = _sb(pes, nc, "bb_s", [128, T], F32)
        cb_s = _sb(pes, nc, "cb_s", [128, T], F32)
        abuf = [_sb(pes, nc, f"abuf{i}", [128, T], F32) for i in range(2)]
        bbuf = [_sb(pes, nc, f"bbuf{i}", [128, T], F32) for i in range(2)]
        sbx = [_sb(pes, nc, f"sbx{i}", [128, T], F32) for i in range(2)]
        ytmp = _sb(pes, nc, "ytmp", [128, T], F32)

        ct_store = {}
        z1_store = {}
        dve_marks = {}
        for g in range(DC // GG):
            dl_marks = {}
            yg_marks = {}
            for jj in range(GG):
                dc = g * GG + jj

                @blk.sync
                def _(sync, dc=dc, at=sA.n):
                    if dc >= 2:
                        sync.wait_ge(sA.h, at)
                    rd.inc(sync.dma_start(out=wdchf[dc % 2][:],
                                          in_=din["wdtT"][dc]), dc % 2)

                @blk.scalar
                def _(s, dc=dc, at=rd.cnt[dc % 2], pat=sPE.n):
                    s.wait_ge(sA.h, sA.n)
                    s.wait_ge(rd.sems[dc % 2], at)
                    if dc >= 2:
                        s.wait_ge(sPE.h, pat)
                    s.copy(out=wdch[dc % 2][:], in_=wdchf[dc % 2][:]
                           ).then_inc(sA.h, 1)
                    sA.n += 1
                wd_g = sA.n

                @blk.sync
                def _(sync, dc=dc, at=sV.n):
                    if dc == 0:
                        ru.wait_all(sync)
                        rs.wait_all(sync)
                    if dc >= 2:
                        sync.wait_ge(sV.h, at)
                    rt.inc(sync.dma_start(out=utmp[dc % 2][:],
                                          in_=u_dram[dc * 128:(dc + 1) * 128, :]),
                           dc % 2)
                u_l = rt.cnt[dc % 2]

                @blk.tensor
                def _(te, dc=dc, wd_g=wd_g, at=sA.n):
                    te.wait_ge(sA.h, wd_g)
                    te.wait_ge(sV.h, B_V)
                    if dc >= 1:
                        te.wait_ge(sA.h, at)
                    for th in range(2):
                        sl = slice(th * 512, (th + 1) * 512)
                        te.matmul(out=psD[th][:, :], lhsT=wdch[dc % 2][:, 0:128],
                                  rhs=dtn0[:, sl], start=True, stop=False)
                        mm = te.matmul(out=psD[th][:, :],
                                       lhsT=wdch[dc % 2][:, 128:256],
                                       rhs=dtn1[:, sl], start=False, stop=True)
                        if th == 1:
                            mm.then_inc(sPE.h, 1)
                sPE.n += 1
                d_pe = sPE.n

                for th in range(2):
                    @blk.scalar
                    def _(s, dc=dc, th=th, d_pe=d_pe, at=sV.n):
                        s.wait_ge(sPE.h, d_pe)
                        if dc == 0 and th == 0:
                            s.wait_ge(sV.h, CONST_V)
                        else:
                            s.wait_ge(sV.h, at)
                        s.activation(out=ab1[:], in_=psD[th][:, :], func=AF.Abs,
                                     bias=dtb_sb[:, dc:dc + 1]).then_inc(sA.h, 1)
                        sA.n += 1
                        s.wait_ge(sA.h, sA.n)
                        s.activation(out=exn[:], in_=ab1[:], func=AF.Exp,
                                     scale=-1.0).then_inc(sA.h, 1)
                        sA.n += 1
                        s.wait_ge(sA.h, sA.n)
                        s.activation(out=l1p[:], in_=exn[:], func=AF.Ln, bias=1.0)
                        s.activation(out=rlu[:], in_=psD[th][:, :], func=AF.Relu,
                                     bias=dtb_sb[:, dc:dc + 1]).then_inc(sA.h, 1)
                    sA.n += 1

                    @blk.vector
                    def _(v, dc=dc, jj=jj, th=th, at=sA.n, u_l=u_l):
                        v.wait_ge(sA.h, at)
                        sl = slice(jj * T + th * 512, jj * T + (th + 1) * 512)
                        ins = v.tensor_add(out=dl_g[:, sl], in0=rlu[:], in1=l1p[:])
                        ins.then_inc(sV.h, 1)
                        sV.n += 1
                        if th == 1:
                            v.wait_ge(sV.h, sV.n)
                            v.wait_ge(rt.sems[dc % 2], u_l)
                            if dc >= 2:
                                v.wait_ge(rct.sems[dc % 2], ct_store[dc - 2])
                            v.tensor_tensor_scan(
                                out=ctt[dc % 2][:],
                                data0=dl_g[:, jj * T:(jj + 1) * T],
                                data1=dl_g[:, jj * T:(jj + 1) * T],
                                initial=0.0, op0=ALU.add, op1=ALU.bypass
                            ).then_inc(sV.h, 1)
                            sV.n += 1
                            v.tensor_tensor(out=du_g[:, jj * T:(jj + 1) * T],
                                            in0=dl_g[:, jj * T:(jj + 1) * T],
                                            in1=utmp[dc % 2][:], op=ALU.mult)
                            v.wait_ge(sV.h, sV.n)
                            v.tensor_copy(out=ctot_sb[:, dc:dc + 1],
                                          in_=ctt[dc % 2][:, T - 1:T]
                                          ).then_inc(sV.h, 1)
                            sV.n += 1

                ct_v = sV.n
                dl_marks[jj] = sV.n

                @blk.sync
                def _(sync, dc=dc, ct_v=ct_v):
                    sync.wait_ge(sV.h, ct_v)
                    rct.inc(sync.dma_start(
                        out=dout["ct_out"][dc * 128:(dc + 1) * 128, :],
                        in_=ctt[dc % 2][:]), dc % 2)
                    if DEV_TAPS:
                        rtap.inc(sync.dma_start(
                            out=dout["dl_tap"][dc * 128:(dc + 1) * 128, :],
                            in_=dl_g[:, (dc % GG) * T:((dc % GG) + 1) * T]))
                ct_store[dc] = rct.cnt[dc % 2]

            for n in range(NST):
                @blk.tensor
                def _(te, g=g, n=n, aat=sA.n, vat=sV.n, pat=sP.n):
                    if g > 0 or n > 0:
                        te.wait_ge(sA.h, aat)
                        te.wait_ge(sV.h, vat)
                        te.wait_ge(sP.h, pat)
                    te.wait_ge(sV.h, B_V)
                    selB = seln_sb[:, n * 128:(n + 1) * 128]
                    selC = seln_sb[:, (16 + n) * 128:(16 + n + 1) * 128]
                    for th in range(2):
                        sl = slice(th * 512, (th + 1) * 512)
                        te.matmul(out=psBB[th][:, :], lhsT=selB,
                                  rhs=bchi[:, sl], start=True, stop=False)
                        te.matmul(out=psBB[th][:, :], lhsT=selB,
                                  rhs=bclo[:, sl], start=False, stop=True)
                        te.matmul(out=psCB[th][:, :], lhsT=selC,
                                  rhs=bchi[:, sl], start=True, stop=False)
                        mm = te.matmul(out=psCB[th][:, :], lhsT=selC,
                                       rhs=bclo[:, sl], start=False, stop=True)
                        if th == 1:
                            mm.then_inc(sPE.h, 1)
                sPE.n += 1
                bb_pe = sPE.n

                @blk.scalar
                def _(s, bb_pe=bb_pe):
                    s.wait_ge(sA.h, sA.n)
                    s.wait_ge(sPE.h, bb_pe)
                    for th in range(2):
                        sl = slice(th * 512, (th + 1) * 512)
                        s.copy(out=bb_s[:, sl], in_=psBB[th][:, :])
                        ins = s.copy(out=cb_s[:, sl], in_=psCB[th][:, :])
                        if th == 1:
                            ins.then_inc(sA.h, 1)
                sA.n += 1
                bb_a = sA.n

                for jj in range(GG):
                    dc = g * GG + jj
                    gii = (g * NST + n) * GG + jj
                    pp = gii % 2

                    @blk.scalar
                    def _(s, dc=dc, jj=jj, n=n, pp=pp,
                          war=dve_marks.get(gii - 2, 0)):
                        s.wait_ge(sA.h, sA.n)
                        if war:
                            s.wait_ge(sV.h, war)
                        s.activation(out=abuf[pp][:],
                                     in_=dl_g[:, jj * T:(jj + 1) * T],
                                     func=AF.Exp,
                                     scale=a_sb[:, dc * NST + n:dc * NST + n + 1]
                                     ).then_inc(sA.h, 1)
                    sA.n += 1
                    a_a = sA.n

                    @blk.gpsimd
                    def _(gp, jj=jj, pp=pp, bb_a=bb_a,
                          war=dve_marks.get(gii - 2, 0)):
                        gp.wait_ge(sP.h, sP.n)
                        if war:
                            gp.wait_ge(sV.h, war)
                        gp.wait_ge(sA.h, bb_a)
                        gp.tensor_tensor(out=bbuf[pp][:],
                                         in0=du_g[:, jj * T:(jj + 1) * T],
                                         in1=bb_s[:], op=ALU.mult).then_inc(sP.h, 1)
                    sP.n += 1
                    b_p = sP.n

                    @blk.vector
                    def _(v, dc=dc, jj=jj, n=n, pp=pp, a_a=a_a, b_p=b_p):
                        v.wait_ge(sA.h, a_a)
                        v.wait_ge(sP.h, b_p)
                        v.tensor_tensor_scan(out=sbx[pp][:], data0=abuf[pp][:],
                                             data1=bbuf[pp][:], initial=0.0,
                                             op0=ALU.mult, op1=ALU.add
                                             ).then_inc(sV.h, 1)
                        sV.n += 1
                        v.wait_ge(sV.h, sV.n)
                        if n == 0:
                            v.tensor_tensor(out=y_g[:, jj * T:(jj + 1) * T],
                                            in0=sbx[pp][:], in1=cb_s[:],
                                            op=ALU.mult).then_inc(sV.h, 1)
                            sV.n += 1
                        else:
                            v.tensor_tensor(out=ytmp[:], in0=sbx[pp][:],
                                            in1=cb_s[:], op=ALU.mult
                                            ).then_inc(sV.h, 1)
                            sV.n += 1
                            v.wait_ge(sV.h, sV.n)
                            v.tensor_add(out=y_g[:, jj * T:(jj + 1) * T],
                                         in0=y_g[:, jj * T:(jj + 1) * T],
                                         in1=ytmp[:]).then_inc(sV.h, 1)
                            sV.n += 1
                        v.wait_ge(sV.h, sV.n)
                        v.tensor_copy(out=s_cap[:, dc * NST + n:dc * NST + n + 1],
                                      in_=sbx[pp][:, T - 1:T]).then_inc(sV.h, 1)
                    sV.n += 1
                    dve_marks[gii] = sV.n

            for jj in range(GG):
                dc = g * GG + jj
                pp = jj % 2

                @blk.sync
                def _(sync, dc=dc, pp=pp, at=sV.n):
                    if jj >= 2:
                        sync.wait_ge(sV.h, at)
                    rt.inc(sync.dma_start(out=utmp[pp][:],
                                          in_=u_dram[dc * 128:(dc + 1) * 128, :]),
                           pp)
                    rt.inc(sync.dma_start(out=sgt[pp][:],
                                          in_=sg_dram[dc * 128:(dc + 1) * 128, :]),
                           pp)
                ul = rt.cnt[pp]
                yp = sP.n

                @blk.vector
                def _(v, dc=dc, jj=jj, pp=pp, ul=ul, yp=yp):
                    v.wait_ge(sV.h, sV.n)
                    v.wait_ge(rt.sems[pp], ul)
                    v.wait_ge(sP.h, yp)
                    if dc >= 2:
                        v.wait_ge(rzr.sems[pp], z1_store.get(dc - 2, 0))
                    v.scalar_tensor_tensor(out=sbx[pp][:], in0=utmp[pp][:],
                                           scalar=dparam_sb[:, dc:dc + 1],
                                           in1=y_g[:, jj * T:(jj + 1) * T],
                                           op0=ALU.mult, op1=ALU.add
                                           ).then_inc(sV.h, 1)
                    sV.n += 1
                    v.wait_ge(sV.h, sV.n)
                    v.tensor_tensor(out=z1b[pp][:], in0=sbx[pp][:],
                                    in1=sgt[pp][:],
                                    op=ALU.mult).then_inc(sV.h, 1)
                sV.n += 1
                z_v = sV.n

                @blk.sync
                def _(sync, dc=dc, pp=pp, z_v=z_v):
                    sync.wait_ge(sV.h, z_v)
                    rzr.inc(sync.dma_start(
                        out=dout["z1"][dc * 128:(dc + 1) * 128, :],
                        in_=z1b[pp][:]), pp)
                z1_store[dc] = rzr.cnt[pp]

    @blk.sync
    def _(sync, at=sV.n):
        sync.wait_ge(sV.h, at)
        rfin.inc(sync.dma_start(out=dout["s_out"], in_=s_cap[:]))
        rfin.inc(sync.dma_start(out=dout["ctot_out"], in_=ctot_sb[:]))
        for r_ in [rfin, rct, rzr, rh, rs]:
            r_.wait_all(sync)
        if DEV_TAPS:
            rtap.wait_all(sync)

    es.close()
    return nc



def build_l2():
    nc = bass.Bass("TRN2", target_bir_lowering=False)
    din = {}
    for name, shape, dt in [
        ("z1", [DD, T], BF16),
        ("sg", [DD, T], F32),
        ("ct", [DD, T], F32),
        ("cn", [16, T], F32),
        ("hg", [1024, T], BF16),
        ("s_all", [8, 128, DC * NST], F32),
        ("ctot_all", [8, 128, DC], F32),
        ("a_st", [128, DC * NST], F32),
        ("maskA", [128, 128], F32),
        ("maskB", [128, 128], F32),
        ("seln", [32, 32 * 128], F32),
        ("woT", [32, 128, 64 * 128], F32),
        ("fbT", [32, 128, 8 * 128], F32),
        ("hsT", [H, T], F32),
    ]:
        din[name] = nc.dram_tensor(name, shape, dt, kind="ExternalInput").ap()
    out_t = nc.dram_tensor("out", [H, T], F32, kind="ExternalOutput").ap()
    z_dram = nc.dram_tensor("z_dram", [DD, T], BF16, kind="Internal").ap()
    if DEV_TAPS:
        sin_tap = nc.dram_tensor("sin_tap", [128, DC * NST], F32,
                                 kind="ExternalOutput").ap()
        hj_tap = nc.dram_tensor("hj_tap", [128, 8 * DC], F32,
                                 kind="ExternalOutput").ap()
        lns_tap = nc.dram_tensor("lns_tap", [128, DC * NST], F32,
                                 kind="ExternalOutput").ap()
        sgn_tap = nc.dram_tensor("sgn_tap", [128, DC * NST], F32,
                                 kind="ExternalOutput").ap()

    es = ExitStack()
    cpool = _sb(es, nc, "cpool", [128, 4], F32)
    sPE = Ctr(es, nc, "sPE")
    sA = Ctr(es, nc, "sA")
    sV = Ctr(es, nc, "sV")
    sP = Ctr(es, nc, "sP")
    sBar = Ctr(es, nc, "sBar")
    qc = DRing(es, nc, "qc", 1)
    qt = DRing(es, nc, "qt", 1)
    qe = DRing(es, nc, "qe", 2)
    qz = DRing(es, nc, "qz", 2)
    qZ = DRing(es, nc, "qZ", 1)
    qw = DRing(es, nc, "qw", 3)
    qf = DRing(es, nc, "qf", 1)
    qo = DRing(es, nc, "qo", 1)
    qtap = DRing(es, nc, "qtap", 1)

    mid = ExitStack()
    a_sb = _sb(mid, nc, "a_sb", [128, DC * NST], F32)
    lnS = _sb(mid, nc, "lnS", [128, DC * NST], F32)
    sgn = _sb(mid, nc, "sgn", [128, DC * NST], F32)
    cnn = _sb(mid, nc, "cnn", [16, T], F32)
    cnhi = _sb(mid, nc, "cnhi", [16, T], BF16)
    cnlo = _sb(mid, nc, "cnlo", [16, T], BF16)
    seln_sb = _sb(mid, nc, "seln_sb", [32, 32 * 128], BF16)

    blk = es.enter_context(nc.Block())


    @blk.scalar
    def _(s):
        load_tbl(s, 6)
    with ExitStack() as pes:
        selnf = _sb(pes, nc, "selnf", [32, 32 * 128], F32)
        sall = _sb(pes, nc, "sall", [128, 8 * DC * NST], F32)
        ctall = _sb(pes, nc, "ctall", [128, 8 * DC], F32)
        mA = _sb(pes, nc, "mA", [128, 128], F32)
        mB = _sb(pes, nc, "mB", [128, 128], F32)
        pfx = _sb(pes, nc, "pfx", [128, 8 * DC], F32)
        myp = _sb(pes, nc, "myp", [128, DC], F32)
        hj_sb = _sb(pes, nc, "hj_sb", [128, 8 * DC], F32)
        sin = _sb(pes, nc, "sin", [128, DC * NST], F32)
        tj = _sb(pes, nc, "tj", [128, DC], F32)

        @blk.sync
        def _(sync):
            for t_, srcn in [(sall, "s_all"), (ctall, "ctot_all")]:
                qc.inc(sync.dma_start(
                    out=t_[:].rearrange("p (j c) -> p j c", j=8),
                    in_=din[srcn].rearrange("j p c -> p j c")))
            for t_, srcn in [(a_sb, "a_st"), (mA, "maskA"), (mB, "maskB"),
                             (selnf, "seln")]:
                qc.inc(sync.dma_start(out=t_[:], in_=din[srcn]))
            qc.inc(sync.dma_start(out=cnn[:], in_=din["cn"]))

        def edge(v, ins):
            ins.then_inc(sV.h, 1)
            sV.n += 1
            v.wait_ge(sV.h, sV.n)

        @blk.vector
        def _(v):
            for i, val in enumerate([1.0]):
                v.memset(cpool[:, i:i + 1], val)
                nc.const_aps.register(F32, val, cpool[:, i:i + 1])
            qc.wait_all(v)
            v.tensor_copy(out=seln_sb[:], in_=selnf[:])
            edge(v, v.tensor_copy(out=cnhi[:], in_=cnn[:]))
            v.tensor_sub(out=cnlo[:], in0=cnn[:], in1=cnhi[:])
            edge(v, v.memset(pfx[:, 0:DC], 0.0))
            for j in range(1, 8):
                edge(v, v.tensor_add(out=pfx[:, j * DC:(j + 1) * DC],
                                     in0=pfx[:, (j - 1) * DC:j * DC],
                                     in1=ctall[:, (j - 1) * DC:j * DC]))
            edge(v, v.memset(myp[:], 0.0))
            for j in range(8):
                edge(v, v.scalar_tensor_tensor(
                    out=myp[:], in0=ctall[:, j * DC:(j + 1) * DC],
                    scalar=mA[:, j:j + 1], in1=myp[:],
                    op0=ALU.mult, op1=ALU.add))
            for j in range(8):
                sl = slice(j * DC, (j + 1) * DC)
                edge(v, v.tensor_sub(out=tj[:], in0=myp[:], in1=pfx[:, sl]))
                edge(v, v.tensor_sub(out=tj[:], in0=tj[:], in1=ctall[:, sl]))
                edge(v, v.tensor_scalar(out=hj_sb[:, sl], in0=tj[:],
                                        scalar1=mA[:, j:j + 1],
                                        scalar2=mB[:, j:j + 1],
                                        op0=ALU.mult, op1=ALU.add))
            v.memset(sin[:], 0.0).then_inc(sV.h, 1)
        sV.n += 1
        PREP_V = sV.n

        argb = [_sb(pes, nc, f"argb{i}", [128, DC * NST], F32) for i in range(2)]
        wb = [_sb(pes, nc, f"wb{i}", [128, DC * NST], F32) for i in range(2)]
        tb2 = _sb(pes, nc, "tb2", [128, DC * NST], F32)
        for j in range(8):
            pp = j % 2

            @blk.vector
            def _(v, j=j, pp=pp, aat=sA.n):
                if j == 0:
                    pass
                if j >= 2:
                    v.wait_ge(sA.h, aat)
                hb = hj_sb[:, j * DC:(j + 1) * DC].rearrange(
                    "p (c one) -> p c one", one=1).broadcast_to([128, DC, NST])
                v.tensor_tensor(out=argb[pp][:].rearrange("p (c n) -> p c n", n=NST),
                                in0=a_sb[:].rearrange("p (c n) -> p c n", n=NST),
                                in1=hb, op=ALU.mult).then_inc(sV.h, 1)
            sV.n += 1
            arg_v = sV.n

            @blk.scalar
            def _(s, j=j, pp=pp, arg_v=arg_v, vat=sV.n):
                s.wait_ge(sV.h, arg_v)
                s.activation(out=wb[pp][:], in_=argb[pp][:], func=AF.Exp
                             ).then_inc(sA.h, 1)
            sA.n += 1
            w_a = sA.n

            @blk.vector
            def _(v, j=j, pp=pp, w_a=w_a):
                v.wait_ge(sA.h, w_a)
                ins = v.tensor_mul(out=tb2[:], in0=wb[pp][:],
                                   in1=sall[:, j * DC * NST:(j + 1) * DC * NST])
                ins.then_inc(sV.h, 1)
                sV.n += 1
                v.wait_ge(sV.h, sV.n)
                v.tensor_add(out=sin[:], in0=sin[:], in1=tb2[:]).then_inc(sV.h, 1)
                sV.n += 1
                v.wait_ge(sV.h, sV.n)

        SIN_V = sV.n

        @blk.scalar
        def _(s, at=SIN_V):
            s.wait_ge(sV.h, at)
            s.activation(out=lnS[:], in_=sin[:], func=AF.Abs).then_inc(sA.h, 1)
            sA.n += 1
            s.wait_ge(sV.h, sV.n + 1)
            s.activation(out=lnS[:], in_=lnS[:], func=AF.Ln)
            s.activation(out=sgn[:], in_=sin[:], func=AF.Sign).then_inc(sA.h, 1)
            sA.n += 1

        @blk.vector
        def _(v, at=sA.n - 1):
            v.wait_ge(sA.h, at)
            v.tensor_scalar_max(out=lnS[:], in0=lnS[:], scalar1=1e-18
                                ).then_inc(sV.h, 1)
        sV.n += 1
        PREP_A = sA.n
        if DEV_TAPS:
            @blk.sync
            def _(sync, at=SIN_V):
                sync.wait_ge(sV.h, at)
                qtap.inc(sync.dma_start(out=sin_tap, in_=sin[:]))
                qtap.inc(sync.dma_start(out=hj_tap, in_=hj_sb[:]))

    if DEV_TAPS:
        @blk.sync
        def _(sync, at=sA.n, vt=sV.n):
            sync.wait_ge(sA.h, at)
            sync.wait_ge(sV.h, vt)
            qtap.inc(sync.dma_start(out=lns_tap, in_=lnS[:]))
            qtap.inc(sync.dma_start(out=sgn_tap, in_=sgn[:]))

    engine_barrier(blk, sBar, 1)

    with ExitStack() as pes:
        psCB = [_ps(pes, nc, f"psCB{i}", [128, 512]) for i in range(2)]
        ct_g = _sb(pes, nc, "ct_g", [128, GG * T], F32)
        yc_g = _sb(pes, nc, "yc_g", [128, GG * T], F32)
        sgt = [_sb(pes, nc, f"sgt{i}", [128, T], F32) for i in range(2)]
        z1t = [_sb(pes, nc, f"z1t{i}", [128, T], BF16) for i in range(2)]
        cb_s = _sb(pes, nc, "cb_s", [128, T], F32)
        corr = [_sb(pes, nc, f"corr{i}", [128, T], F32) for i in range(4)]
        tmpb = _sb(pes, nc, "tmpb", [128, T], F32)
        z2b = _sb(pes, nc, "z2b", [128, T], F32)
        z_st = {}

        p2_dve = {}
        for g in range(DC // GG):
            @blk.sync
            def _(sync, g=g, at=sA.n):
                if g > 0:
                    sync.wait_ge(sA.h, at)
                for jj in range(GG):
                    dc = g * GG + jj
                    qt.inc(sync.dma_start(out=ct_g[:, jj * T:(jj + 1) * T],
                                          in_=din["ct"][dc * 128:(dc + 1) * 128, :]))
            ct_l = qt.cnt[0]

            for n in range(NST):
                @blk.tensor
                def _(te, g=g, n=n, aat=sA.n, vat=sV.n, pat=sP.n):
                    if g > 0 or n > 0:
                        te.wait_ge(sA.h, aat)
                        te.wait_ge(sV.h, vat)
                        te.wait_ge(sP.h, pat)
                    te.wait_ge(sV.h, PREP_V)
                    selC = seln_sb[:, n * 128:(n + 1) * 128]
                    for th in range(2):
                        sl = slice(th * 512, (th + 1) * 512)
                        te.matmul(out=psCB[th][:, :], lhsT=selC[0:16, :],
                                  rhs=cnhi[:, sl], start=True, stop=False)
                        mm = te.matmul(out=psCB[th][:, :], lhsT=selC[0:16, :],
                                       rhs=cnlo[:, sl], start=False, stop=True)
                        if th == 1:
                            mm.then_inc(sPE.h, 1)
                sPE.n += 1
                bb_pe = sPE.n

                @blk.scalar
                def _(s, bb_pe=bb_pe):
                    s.wait_ge(sPE.h, bb_pe)
                    for th in range(2):
                        sl = slice(th * 512, (th + 1) * 512)
                        ins = s.copy(out=cb_s[:, sl], in_=psCB[th][:, :])
                        if th == 1:
                            ins.then_inc(sA.h, 1)
                sA.n += 1
                bb_a = sA.n

                for jj in range(GG):
                    dc = g * GG + jj
                    gii = (g * NST + n) * GG + jj
                    pp = gii % 4

                    @blk.scalar
                    def _(s, dc=dc, jj=jj, n=n, pp=pp, ct_l=ct_l,
                          war=p2_dve.get(gii - 4, 0)):
                        s.wait_ge(sA.h, sA.n)
                        s.wait_ge(qt.sems[0], ct_l)
                        if n == 0 and jj == 0:
                            s.wait_ge(sA.h, PREP_A)
                        if war:
                            s.wait_ge(sV.h, war)
                        col = dc * NST + n
                        s.activation(out=corr[pp][:],
                                     in_=ct_g[:, jj * T:(jj + 1) * T],
                                     func=AF.Exp, scale=a_sb[:, col:col + 1],
                                     bias=lnS[:, col:col + 1]).then_inc(sA.h, 1)
                    sA.n += 1
                    a_a = sA.n

                    @blk.vector
                    def _(v, dc=dc, jj=jj, n=n, pp=pp, a_a=a_a, bb_a=bb_a):
                        v.wait_ge(sV.h, sV.n)
                        v.wait_ge(sA.h, a_a)
                        v.wait_ge(sA.h, bb_a)
                        col = dc * NST + n
                        if n == 0:
                            v.scalar_tensor_tensor(
                                out=yc_g[:, jj * T:(jj + 1) * T],
                                in0=corr[pp][:], scalar=sgn[:, col:col + 1],
                                in1=cb_s[:], op0=ALU.mult,
                                op1=ALU.mult).then_inc(sV.h, 1)
                            sV.n += 1
                        else:
                            v.scalar_tensor_tensor(
                                out=tmpb[:], in0=corr[pp][:],
                                scalar=sgn[:, col:col + 1], in1=cb_s[:],
                                op0=ALU.mult, op1=ALU.mult).then_inc(sV.h, 1)
                            sV.n += 1
                            v.wait_ge(sV.h, sV.n)
                            v.tensor_add(out=yc_g[:, jj * T:(jj + 1) * T],
                                         in0=yc_g[:, jj * T:(jj + 1) * T],
                                         in1=tmpb[:]).then_inc(sV.h, 1)
                            sV.n += 1
                    p2_dve[gii] = sV.n

            for jj in range(GG):
                dc = g * GG + jj
                pp = jj % 2

                @blk.sync
                def _(sync, dc=dc, jj=jj, pp=pp, at=sV.n,
                      zst=z_st.get(g * GG + jj - 2, 0)):
                    if jj >= 2:
                        sync.wait_ge(sV.h, at)
                    if dc >= 2:
                        sync.wait_ge(qz.sems[pp], zst)
                    qe.inc(sync.dma_start(out=sgt[pp][:],
                                          in_=din["sg"][dc * 128:(dc + 1) * 128, :]),
                           pp)
                    qe.inc(sync.dma_start(out=z1t[pp][:],
                                          in_=din["z1"][dc * 128:(dc + 1) * 128, :]),
                           pp)
                ul = qe.cnt[pp]
                yp = sP.n

                @blk.vector
                def _(v, dc=dc, jj=jj, pp=pp, ul=ul, yp=yp):
                    v.wait_ge(qe.sems[pp], ul)
                    v.wait_ge(sP.h, yp)
                    v.tensor_tensor(out=z2b[:], in0=yc_g[:, jj * T:(jj + 1) * T],
                                    in1=sgt[pp][:], op=ALU.mult).then_inc(sV.h, 1)
                    sV.n += 1
                    v.wait_ge(sV.h, sV.n)
                    v.tensor_add(out=z1t[pp][:], in0=z2b[:], in1=z1t[pp][:]
                                 ).then_inc(sV.h, 1)
                sV.n += 1
                z_v = sV.n

                @blk.sync
                def _(sync, dc=dc, pp=pp, z_v=z_v):
                    sync.wait_ge(sV.h, z_v)
                    qz.inc(sync.dma_start(out=z_dram[dc * 128:(dc + 1) * 128, :],
                                          in_=z1t[pp][:]), pp)
                z_st[dc] = qz.cnt[pp]

    engine_barrier(blk, sBar, 2)
    mid.close()

    with ExitStack() as pes:
        psAT = [_ps(pes, nc, f"psAT{i}", [128, 512]) for i in range(2)]
        psF = [_ps(pes, nc, f"psF{i}", [128, 512]) for i in range(2)]
        z_sb = _sb(pes, nc, "z_sb", [128, DC * T], BF16)
        wof = [_sb(pes, nc, f"wof{i}", [128, 1024], F32) for i in range(3)]
        wo = [_sb(pes, nc, f"wo{i}", [128, DC * 128], BF16) for i in range(2)]
        fbf = _sb(pes, nc, "fbf", [128, 8 * 128], F32)
        fbb = [_sb(pes, nc, f"fbb{i}", [128, 8 * 128], BF16) for i in range(2)]
        hgt = [_sb(pes, nc, f"hgt{i}", [128, 8 * 512], BF16) for i in range(2)]
        xres = _sb(pes, nc, "xres", [128, T], F32)
        o1 = _sb(pes, nc, "o1", [128, 512], F32)
        out_s = _sb(pes, nc, "out_s", [128, T], F32)

        @blk.sync
        def _(sync):
            qz.wait_all(sync)
            zv = z_sb[:].rearrange("p (c t) -> p c t", c=DC)
            zd = z_dram.rearrange("(c p) t -> p c t", p=128)
            for k8 in range(8):
                cs = slice(k8 * 8, (k8 + 1) * 8)
                qZ.inc(sync.dma_start(out=zv[:, cs], in_=zd[:, cs]))
            for th in range(2):
                hv = hgt[th][:].rearrange("p (k c) -> p k c", k=8)
                hd = din["hg"].rearrange("(k p) t -> p k t", p=128)[
                    :, :, th * 512:(th + 1) * 512]
                for k4 in range(4):
                    ks = slice(k4 * 2, (k4 + 1) * 2)
                    qZ.inc(sync.dma_start(out=hv[:, ks], in_=hd[:, ks]))

        wo_marks = {}
        mm_marks = {}
        st_marks = {}
        o1_mark = [0]
        for hmt in range(32):
            for q in range(8):
                qi = hmt * 8 + q

                @blk.sync
                def _(sync, hmt=hmt, q=q, qi=qi, at=wo_marks.get(qi - 3, 0)):
                    if qi >= 3:
                        sync.wait_ge(sP.h, at)
                    for k2 in range(2):
                        cs = slice(k2 * 512, (k2 + 1) * 512)
                        qw.inc(sync.dma_start(
                            out=wof[q % 3][:, cs],
                            in_=din["woT"][hmt][:, q * 1024 + k2 * 512:
                                                q * 1024 + k2 * 512 + 512]),
                               q % 3)

                @blk.gpsimd
                def _(g, hmt=hmt, q=q, at=qw.cnt[q % 3],
                      pat=mm_marks.get(hmt - 2, 0)):
                    g.wait_ge(qw.sems[q % 3], at)
                    if hmt >= 2 and q == 0:
                        g.wait_ge(sPE.h, pat)
                    g.tensor_copy(out=wo[hmt % 2][:, q * 1024:(q + 1) * 1024],
                                  in_=wof[q % 3][:]).then_inc(sP.h, 1)
                sP.n += 1
                wo_marks[qi] = sP.n
            wo_p = sP.n

            @blk.sync
            def _(sync, hmt=hmt, pat=mm_marks.get(hmt - 2, 0), vat=sV.n,
                  fat=sP.n):
                if hmt >= 2:
                    sync.wait_ge(sPE.h, pat)
                if hmt >= 1:
                    sync.wait_ge(sV.h, vat)
                    sync.wait_ge(sP.h, fat)
                qf.inc(sync.dma_start(out=fbf[:], in_=din["fbT"][hmt]))
                qf.inc(sync.dma_start(
                    out=xres[:], in_=din["hsT"][hmt * 128:(hmt + 1) * 128, :]))

            @blk.gpsimd
            def _(g, hmt=hmt, at=qf.cnt[0]):
                g.wait_ge(qf.sems[0], at)
                g.tensor_copy(out=fbb[hmt % 2][:], in_=fbf[:]
                              ).then_inc(sP.h, 1)
            sP.n += 1
            fb_p = sP.n

            for th in range(2):
                @blk.tensor
                def _(te, hmt=hmt, th=th, wo_p=wo_p, fb_p=fb_p,
                      aat=sA.n, vat=sV.n):
                    te.wait_ge(sP.h, fb_p)
                    qZ.wait_all(te)
                    if hmt >= 1:
                        te.wait_ge(sV.h, vat)
                    for dc in range(DC):
                        te.matmul(out=psAT[th][:, :],
                                  lhsT=wo[hmt % 2][:, dc * 128:(dc + 1) * 128],
                                  rhs=z_sb[:, dc * T + th * 512:
                                           dc * T + th * 512 + 512],
                                  start=(dc == 0), stop=(dc == DC - 1))
                    for kr in range(8):
                        mm = te.matmul(out=psF[th][:, :],
                                       lhsT=fbb[hmt % 2][:, kr * 128:(kr + 1) * 128],
                                       rhs=hgt[th][:, kr * 512:(kr + 1) * 512],
                                       start=(kr == 0), stop=(kr == 7))
                        if kr == 7:
                            mm.then_inc(sPE.h, 1)
                sPE.n += 1
            mm_marks[hmt] = sPE.n

            for th in range(2):
                mm_pe = sPE.n - 1 + th if False else mm_marks[hmt]

                @blk.scalar
                def _(s, hmt=hmt, th=th, mm_pe=mm_pe, xat=qf.cnt[0],
                      om=o1_mark[0]):
                    s.wait_ge(sPE.h, mm_pe - (1 if th == 0 else 0))
                    s.wait_ge(qf.sems[0], xat)
                    if om:
                        s.wait_ge(sV.h, om)
                    s.activation(out=o1[:], in_=psF[th][:, :], func=AF.Identity,
                                 scale=2.0).then_inc(sA.h, 1)
                sA.n += 1
                ev_a = sA.n

                @blk.vector
                def _(v, hmt=hmt, th=th, ev_a=ev_a, stm=st_marks.get(hmt - 1, 0)):
                    v.wait_ge(sV.h, sV.n)
                    v.wait_ge(sA.h, ev_a)
                    if hmt >= 1 and th == 0:
                        v.wait_ge(qo.sems[0], stm)
                    v.tensor_add(out=o1[:], in0=o1[:], in1=psAT[th][:, :]
                                 ).then_inc(sV.h, 1)
                    sV.n += 1
                    v.wait_ge(sV.h, sV.n)
                    v.tensor_add(out=out_s[:, th * 512:(th + 1) * 512],
                                 in0=o1[:],
                                 in1=xres[:, th * 512:(th + 1) * 512]
                                 ).then_inc(sV.h, 1)
                    sV.n += 1
                    o1_mark[0] = sV.n
                    sV.n -= 1
                sV.n += 1
            o_v = sV.n

            @blk.sync
            def _(sync, hmt=hmt, o_v=o_v):
                sync.wait_ge(sV.h, o_v)
                qo.inc(sync.dma_start(out=out_t[hmt * 128:(hmt + 1) * 128, :],
                                      in_=out_s[:]))
            st_marks[hmt] = qo.cnt[0]

    @blk.sync
    def _(sync):
        qo.wait_all(sync)
        if DEV_TAPS:
            qtap.wait_all(sync)

    es.close()
    return nc



def _stage_l1(inputs):
    x = np.asarray(inputs["L0_attn_hidden_states"], np.float32)[0]
    W_in = np.asarray(inputs["L0_attn_W_in"], np.float32)
    conv_w = np.asarray(inputs["L0_attn_conv_w"], np.float32)
    conv_b = np.asarray(inputs["L0_attn_conv_b"], np.float32)
    Wx = np.asarray(inputs["L0_attn_Wx"], np.float32)
    dt_ln_w = np.asarray(inputs["L0_attn_dt_ln_w"], np.float32)
    b_ln_w = np.asarray(inputs["L0_attn_b_ln_w"], np.float32)
    c_ln_w = np.asarray(inputs["L0_attn_c_ln_w"], np.float32)
    Wdt = np.asarray(inputs["L0_attn_Wdt"], np.float32)
    dt_bias = np.asarray(inputs["L0_attn_dt_bias"], np.float32)
    A = np.asarray(inputs["L0_attn_A"], np.float32)
    D_param = np.asarray(inputs["L0_attn_D_param"], np.float32)
    fA = np.asarray(inputs["L0_ffn_A"], np.float32)
    Wr = np.asarray(inputs["L0_ffn_Wr"], np.float32)

    hsT = np.ascontiguousarray(x.T)
    winT = np.ascontiguousarray(
        W_in.T.reshape(HC, 128, 128, 128).transpose(2, 1, 0, 3)).reshape(
        128, 128, HC * 128)
    wxT = np.ascontiguousarray(Wx.T.reshape(DC, 128, 288))
    wdtT = np.ascontiguousarray(
        Wdt.T.reshape(2, 128, DC, 128).transpose(2, 1, 0, 3)).reshape(DC, 128, 256)
    a_st = np.ascontiguousarray(
        A.reshape(DC, 128, NST).transpose(1, 0, 2)).reshape(128, DC * NST)
    convw = np.ascontiguousarray(
        conv_w.reshape(DC, 128, 4).transpose(1, 0, 2)).reshape(128, DC * 4)
    def pad128(a):
        out = np.zeros((a.shape[0], 128), np.float32)
        out[:, :a.shape[1]] = a
        return out
    convb = pad128(np.ascontiguousarray(conv_b.reshape(DC, 128).T))
    dtb = pad128(np.ascontiguousarray(dt_bias.reshape(DC, 128).T))
    dparam = pad128(np.ascontiguousarray(D_param.reshape(DC, 128).T))
    dtlnw = pad128(np.ascontiguousarray(dt_ln_w.reshape(2, 128).T))
    bclnw = pad128(np.concatenate([b_ln_w, c_ln_w]).reshape(32, 1).astype(np.float32))
    bc_ones = np.zeros((32, 128), np.float32)
    bc_ones[0:16, 0] = 1.0
    bc_ones[16:32, 1] = 1.0
    bc_exp = np.zeros((2, 128), np.float32)
    bc_exp[0, 0:16] = 1.0
    bc_exp[1, 16:32] = 1.0
    eye128 = np.eye(128, dtype=np.float32)
    wr_st = np.ascontiguousarray(
        Wr.T.reshape(HC, 128, 64).transpose(1, 0, 2)).reshape(128, HC * 64)
    fAT = np.ascontiguousarray(fA.reshape(1024, H).T)
    ffnA = np.ascontiguousarray(
        fAT.reshape(HC, 128, 8, 128).transpose(2, 1, 0, 3)).reshape(
        8, 128, HC * 128)
    exp16 = np.zeros((8, 64, 128), np.float32)
    for c in range(8):
        for m in range(128):
            exp16[c, (c * 128 + m) // 16, m] = 1.0
    seln = np.zeros((32, 32 * 128), np.float32)
    for n in range(32):
        seln[n, n * 128:(n + 1) * 128] = 1.0

    shared = dict(winT=winT, wxT=wxT, wdtT=wdtT, a_st=a_st, convw=convw,
                  convb=convb, dtb=dtb, dparam=dparam, dtlnw=dtlnw, bclnw=bclnw,
                  bc_ones=bc_ones, bc_exp=bc_exp, eye128=eye128, wr_st=wr_st,
                  ffnA=ffnA, exp16=exp16, seln=seln)
    in_maps = []
    for c in range(NCORES):
        c0 = c * T
        halo = (np.zeros((H, 3), np.float32) if c == 0
                else np.ascontiguousarray(hsT[:, c0 - 3:c0]))
        m = dict(shared)
        m["hsT"] = np.ascontiguousarray(
            np.concatenate([halo, hsT[:, c0:c0 + T]], axis=1))
        in_maps.append(m)
    return in_maps


def _stage_l2(inputs, res1):
    x = np.asarray(inputs["L0_attn_hidden_states"], np.float32)[0]
    W_out = np.asarray(inputs["L0_attn_W_out"], np.float32)
    A = np.asarray(inputs["L0_attn_A"], np.float32)
    fB = np.asarray(inputs["L0_ffn_B"], np.float32)

    hsT = np.ascontiguousarray(x.T)
    a_st = np.ascontiguousarray(
        A.reshape(DC, 128, NST).transpose(1, 0, 2)).reshape(128, DC * NST)
    seln = np.zeros((32, 32 * 128), np.float32)
    for n in range(32):
        seln[n, n * 128:(n + 1) * 128] = 1.0
    woT = np.ascontiguousarray(
        W_out.T.reshape(DC, 128, 32, 128).transpose(2, 1, 0, 3)).reshape(
        32, 128, DC * 128)
    fbT = np.ascontiguousarray(
        fB.transpose(0, 2, 1).reshape(8, 128, 32, 128).transpose(2, 1, 0, 3)
    ).reshape(32, 128, 8 * 128)
    s_all = np.stack([np.asarray(res1.results[c]["s_out"], np.float32)
                      for c in range(NCORES)])
    ctot_all = np.stack([np.asarray(res1.results[c]["ctot_out"], np.float32)
                         for c in range(NCORES)])
    in_maps = []
    for c in range(NCORES):
        mA = np.zeros((128, 128), np.float32)
        mB = np.full((128, 128), 1.0e5, np.float32)
        mA[:, 0:c] = 1.0
        mB[:, 0:c] = 0.0
        r = res1.results[c]
        m = dict(
            z1=np.asarray(r["z1"]), sg=np.asarray(r["sg_out"], np.float32),
            ct=np.asarray(r["ct_out"], np.float32),
            cn=np.asarray(r["cn_out"], np.float32), hg=np.asarray(r["hg_out"]),
            s_all=s_all, ctot_all=ctot_all, a_st=a_st, maskA=mA, maskB=mB,
            seln=seln, woT=woT, fbT=fbT,
            hsT=np.ascontiguousarray(hsT[:, c * T:(c + 1) * T]),
        )
        in_maps.append(m)
    return in_maps


def kernel(**inputs):
    in_maps1 = _stage_l1(inputs)
    nc1 = build_l1()
    res1 = run_bass_kernel_spmd(nc1, in_maps1, core_ids=list(range(NCORES)))
    in_maps2 = _stage_l2(inputs, res1)
    nc2 = build_l2()
    res2 = run_bass_kernel_spmd(nc2, in_maps2, core_ids=list(range(NCORES)))
    out = np.concatenate(
        [np.asarray(res2.results[c]["out"], np.float32).T for c in range(NCORES)],
        axis=0)
    return out.reshape(1, 8192, H)


if __name__ == "__main__":
    pass


# version 04 (final)
# speedup vs baseline: 1.3187x; 1.0667x over previous
import numpy as np
from contextlib import ExitStack

import concourse.bass as bass
import concourse.mybir as mybir
from concourse.bass_utils import run_bass_kernel_spmd

F32 = mybir.dt.float32
BF16 = mybir.dt.bfloat16
AF = mybir.ActivationFunctionType
ALU = mybir.AluOpType
AX = mybir.AxisListType

NCORES = 8
H = 4096
DD = 8192
NST = 16
RR = 256
T = 1024
TH = T + 3
HC = H // 128
DC = DD // 128
GG = 8
NEG = -1.0e30

DEV_TAPS = False


def _sb(es, nc, name, shape, dt):
    return es.enter_context(nc.sbuf_tensor(name, shape, dt))


def _ps(es, nc, name, shape, dt=F32):
    return es.enter_context(nc.psum_tensor(name, shape, dt))


class Ctr:
    def __init__(self, es, nc, name):
        self.h = es.enter_context(nc.semaphore(name))
        self.n = 0


class DRing:

    def __init__(self, es, nc, name, slots):
        self.sems = [es.enter_context(nc.semaphore(f"{name}{i}"))
                     for i in range(slots)]
        self.cnt = [0] * slots

    def inc(self, instr, slot=0, k=16):
        instr.then_inc(self.sems[slot], k)
        self.cnt[slot] += k
        return self.cnt[slot]

    def wait(self, eng, slot=0, val=None):
        eng.wait_ge(self.sems[slot], self.cnt[slot] if val is None else val)

    def wait_all(self, eng):
        for i, s in enumerate(self.sems):
            eng.wait_ge(s, self.cnt[i])


def load_tbl(eng, set_id):
    eng.add_instruction(mybir.InstLoadActFuncSet(
        name=eng.bass.get_next_instruction_name(),
        act_func_set_id=set_id, ins=[], outs=[]))


def engine_barrier(blk, bar, k):
    for name in ["sync", "tensor", "scalar", "vector", "gpsimd"]:
        dec = getattr(blk, name)

        def _f(eng):
            eng.nop().then_inc(bar.h, 1)
        dec(_f)
    bar.n += 5
    tgt = bar.n
    for name in ["sync", "tensor", "scalar", "vector", "gpsimd"]:
        dec = getattr(blk, name)

        def _g(eng, tgt=tgt):
            eng.wait_ge(bar.h, tgt)
        dec(_g)



def build_l1():
    nc = bass.Bass("TRN2", target_bir_lowering=False)
    din = {}
    for name, shape in [
        ("hsT", [H, TH]),
        ("winT", [128, 128, HC * 128]),
        ("wxT", [DC, 128, 288]),
        ("wdtT", [DC, 128, 256]),
        ("a_st", [128, DC * NST]),
        ("convw", [128, DC * 4]),
        ("convb", [128, 128]),
        ("dtb", [128, 128]),
        ("dparam", [128, 128]),
        ("dtlnw", [128, 128]),
        ("bclnw", [32, 128]),
        ("bc_ones", [32, 128]),
        ("bc_exp", [2, 128]),
        ("eye128", [128, 128]),
        ("wr_st", [128, HC * 64]),
        ("ffnA", [8, 128, HC * 128]),
        ("exp16", [8, 64, 128]),
        ("seln", [32, 32 * 128]),
    ]:
        din[name] = nc.dram_tensor(name, shape, F32, kind="ExternalInput").ap()

    dout = {}
    for name, shape, dt in [
        ("z1", [DD, T], BF16),
        ("sg_out", [DD, T], F32),
        ("ct_out", [DD, T], F32),
        ("s_out", [128, DC * NST], F32),
        ("ctot_out", [128, DC], F32),
        ("cn_out", [16, T], F32),
        ("hg_out", [1024, T], BF16),
    ]:
        dout[name] = nc.dram_tensor(name, shape, dt, kind="ExternalOutput").ap()
    if DEV_TAPS:
        for name, shape, dt in [
            ("xn_tap", [128, HC * TH], BF16),
            ("u_tap", [DD, T], F32),
            ("ssm_tap", [288, T], F32),
            ("dtn_tap", [RR, T], BF16),
            ("bcn_tap", [32, T], F32),
            ("g_tap", [T, 64], F32),
            ("dl_tap", [DD, T], F32),
        ]:
            dout[name] = nc.dram_tensor(name, shape, dt, kind="ExternalOutput").ap()

    u_dram = nc.dram_tensor("u_dram", [DD, T], F32, kind="Internal").ap()
    sg_dram = nc.dram_tensor("sg_dram", [DD, T], F32, kind="Internal").ap()

    es = ExitStack()
    dtn0 = _sb(es, nc, "dtn0", [128, T], BF16)
    dtn1 = _sb(es, nc, "dtn1", [128, T], BF16)
    bcn = _sb(es, nc, "bcn", [32, T], F32)
    bchi = _sb(es, nc, "bchi", [32, T], BF16)
    bclo = _sb(es, nc, "bclo", [32, T], BF16)
    seln_sb = _sb(es, nc, "seln_sb", [32, 32 * 128], BF16)
    a_sb = _sb(es, nc, "a_sb", [128, DC * NST], F32)
    convw_sb = _sb(es, nc, "convw_sb", [128, DC * 4], F32)
    convb_sb = _sb(es, nc, "convb_sb", [128, 128], F32)
    dtb_sb = _sb(es, nc, "dtb_sb", [128, 128], F32)
    dparam_sb = _sb(es, nc, "dparam_sb", [128, 128], F32)
    dtlnw_sb = _sb(es, nc, "dtlnw_sb", [128, 128], F32)
    bclnw_sb = _sb(es, nc, "bclnw_sb", [32, 128], F32)
    bco_sb = _sb(es, nc, "bco_sb", [32, 2], BF16)
    bce_sb = _sb(es, nc, "bce_sb", [2, 32], BF16)
    eye_sb = _sb(es, nc, "eye_sb", [128, 128], F32)
    ones_col = _sb(es, nc, "ones_col", [128, 1], BF16)
    ones_row = _sb(es, nc, "ones_row", [1, 128], BF16)
    ctot_sb = _sb(es, nc, "ctot_sb", [128, DC], F32)
    s_cap = _sb(es, nc, "s_cap", [128, DC * NST], F32)
    cpool = _sb(es, nc, "cpool", [128, 4], F32)

    sPE = Ctr(es, nc, "sPE")
    sA = Ctr(es, nc, "sA")
    sV = Ctr(es, nc, "sV")
    sP = Ctr(es, nc, "sP")
    rc = DRing(es, nc, "rc", 1)
    rx = DRing(es, nc, "rx", 2)
    rw = DRing(es, nc, "rw", 2)
    ru = DRing(es, nc, "ru", 2)
    rs = DRing(es, nc, "rs", 2)
    rt = DRing(es, nc, "rt", 2)
    rd = DRing(es, nc, "rd", 2)
    rct = DRing(es, nc, "rct", 2)
    rzr = DRing(es, nc, "rz", 2)
    rf = DRing(es, nc, "rf", 2)
    rfw = DRing(es, nc, "rfw", 1)
    rh = DRing(es, nc, "rh", 2)
    rfin = DRing(es, nc, "rfin", 1)
    rtap = DRing(es, nc, "rtap", 1)
    sBar = Ctr(es, nc, "sBar")

    xn_es = ExitStack()
    xn = _sb(xn_es, nc, "xn", [128, HC * TH], BF16)
    ssm_es = ExitStack()
    ssm0 = _sb(ssm_es, nc, "ssm0", [128, T], F32)
    ssm1 = _sb(ssm_es, nc, "ssm1", [128, T], F32)
    ssmBC = _sb(ssm_es, nc, "ssmBC", [32, T], F32)

    blk = es.enter_context(nc.Block())

    cst_es = ExitStack()
    bcof = _sb(cst_es, nc, "bcof", [32, 128], F32)
    bcef = _sb(cst_es, nc, "bcef", [2, 128], F32)
    selnf = _sb(cst_es, nc, "selnf", [32, 32 * 128], F32)

    @blk.sync
    def _(sync):
        for t_, src in [
            (a_sb, din["a_st"]), (convw_sb, din["convw"]), (convb_sb, din["convb"]),
            (dtb_sb, din["dtb"]), (dparam_sb, din["dparam"]), (dtlnw_sb, din["dtlnw"]),
            (bclnw_sb, din["bclnw"]), (eye_sb, din["eye128"]),
            (bcof, din["bc_ones"]), (bcef, din["bc_exp"]), (selnf, din["seln"]),
        ]:
            rc.inc(sync.dma_start(out=t_[:], in_=src))

    for eng_name in ["tensor", "scalar", "vector", "gpsimd"]:
        getattr(blk, eng_name)(lambda eng: rc.wait_all(eng))

    @blk.vector
    def _(v):
        v.tensor_copy(out=bco_sb[:], in_=bcof[:, 0:2])
        v.tensor_copy(out=bce_sb[:], in_=bcef[:, 0:32])
        v.tensor_copy(out=seln_sb[:], in_=selnf[:])
        for i, val in enumerate([1e-5, 1e-6, 1.0]):
            v.memset(cpool[:, i:i + 1], val)
            nc.const_aps.register(F32, val, cpool[:, i:i + 1])
        v.memset(ones_col[:], 1.0)
        v.memset(ones_row[:], 1.0).then_inc(sV.h, 1)
        sV.n += 1

    CONST_V = sV.n


    @blk.scalar
    def _(s):
        load_tbl(s, 6)
    LNG = [(0, 512), (512, 512), (899, 128)]
    with ExitStack() as pes:
        psA_ = _ps(pes, nc, "psA", [1, 512])
        psB_ = _ps(pes, nc, "psB", [1, 512])
        psMU = _ps(pes, nc, "psMU", [128, 512])
        psRS = _ps(pes, nc, "psRS", [128, 512])
        xbuf = [_sb(pes, nc, f"xbuf{i}", [128, 512], F32) for i in range(2)]
        xbf = [_sb(pes, nc, f"xbf{i}", [128, 512], BF16) for i in range(2)]
        x2bf = [_sb(pes, nc, f"x2bf{i}", [128, 512], BF16) for i in range(2)]
        murow = _sb(pes, nc, "murow", [1, 512], F32)
        m2row = _sb(pes, nc, "m2row", [1, 512], F32)
        musq = _sb(pes, nc, "musq", [1, 512], F32)
        varrow = _sb(pes, nc, "varrow", [1, 512], F32)
        lnrow = _sb(pes, nc, "lnrow", [1, 512], F32)
        rstdrow = _sb(pes, nc, "rstdrow", [1, 512], F32)
        mu_hi = _sb(pes, nc, "mu_hi", [1, 512], BF16)
        mu_lo = _sb(pes, nc, "mu_lo", [1, 512], BF16)
        rs_hi = _sb(pes, nc, "rs_hi", [1, 512], BF16)
        rs_lo = _sb(pes, nc, "rs_lo", [1, 512], BF16)
        t1b = [_sb(pes, nc, f"t1b{i}", [128, 512], F32) for i in range(2)]

        for (c0, gw) in LNG:
            gV, gA, gPE = sV.n, sA.n, sPE.n
            src_g = din["hsT"][:, c0:c0 + gw].rearrange("(hc p) c -> hc p c", p=128)
            xv = []

            @blk.sync
            def _(sync, gV=gV, gA=gA, src_g=src_g, gw=gw, xv=xv):
                for k in range(HC):
                    if k >= 2:
                        sync.wait_ge(sV.h, gV + (k - 1))
                        sync.wait_ge(sA.h, gA + (k - 1))
                    else:
                        sync.wait_ge(sV.h, gV)
                    xv.append(rx.inc(sync.dma_start(out=xbuf[k % 2][:, :gw],
                                                    in_=src_g[k]), k % 2))

            @blk.vector
            def _(v, gPE=gPE, gw=gw, xv=xv):
                for k in range(HC):
                    v.wait_ge(rx.sems[k % 2], xv[k])
                    if k >= 2:
                        v.wait_ge(sPE.h, gPE + (k - 1))
                    else:
                        v.wait_ge(sPE.h, gPE)
                    v.tensor_copy(out=xbf[k % 2][:, :gw], in_=xbuf[k % 2][:, :gw]
                                  ).then_inc(sV.h, 1)
            sV.n += HC

            @blk.scalar
            def _(s, gPE=gPE, gw=gw, xv=xv):
                s.wait_ge(sA.h, sA.n)
                for k in range(HC):
                    s.wait_ge(rx.sems[k % 2], xv[k])
                    if k >= 2:
                        s.wait_ge(sPE.h, gPE + (k - 1))
                    else:
                        s.wait_ge(sPE.h, gPE)
                    s.activation(out=x2bf[k % 2][:, :gw], in_=xbuf[k % 2][:, :gw],
                                 func=AF.Square).then_inc(sA.h, 1)
            sA.n += HC

            @blk.tensor
            def _(te, gV=gV, gA=gA, gw=gw):
                te.wait_ge(sV.h, CONST_V)
                for k in range(HC):
                    te.wait_ge(sV.h, gV + (k + 1))
                    te.wait_ge(sA.h, gA + (k + 1))
                    te.matmul(out=psA_[:, :gw], lhsT=ones_col[:],
                              rhs=xbf[k % 2][:, :gw],
                              start=(k == 0), stop=(k == HC - 1))
                    te.matmul(out=psB_[:, :gw], lhsT=ones_col[:],
                              rhs=x2bf[k % 2][:, :gw],
                              start=(k == 0), stop=(k == HC - 1)).then_inc(sPE.h, 1)
            sPE.n += HC

            eV, eA, ePE = sV.n, sA.n, sPE.n

            @blk.vector
            def _(v, gw=gw, ePE=ePE, eA=eA):
                def edge(ins):
                    ins.then_inc(sV.h, 1)
                    sV.n += 1
                    v.wait_ge(sV.h, sV.n)
                v.wait_ge(sPE.h, ePE)
                edge(v.tensor_scalar_mul(out=murow[:, :gw], in0=psA_[:, :gw],
                                         scalar1=1.0 / H))
                edge(v.tensor_scalar_mul(out=m2row[:, :gw], in0=psB_[:, :gw],
                                         scalar1=1.0 / H))
                edge(v.tensor_mul(out=musq[:, :gw], in0=murow[:, :gw],
                                  in1=murow[:, :gw]))
                v.tensor_sub(out=varrow[:, :gw], in0=m2row[:, :gw],
                             in1=musq[:, :gw]).then_inc(sV.h, 1)
                sV.n += 1
                v.wait_ge(sA.h, eA + 2)
                edge(v.tensor_copy(out=mu_hi[:, :gw], in_=murow[:, :gw]))
                edge(v.tensor_copy(out=rs_hi[:, :gw], in_=rstdrow[:, :gw]))
                v.tensor_sub(out=mu_lo[:, :gw], in0=murow[:, :gw], in1=mu_hi[:, :gw])
                v.tensor_sub(out=rs_lo[:, :gw], in0=rstdrow[:, :gw],
                             in1=rs_hi[:, :gw]).then_inc(sV.h, 1)
                sV.n += 1
            pass

            @blk.scalar
            def _(s, gw=gw, eV=eV):
                s.wait_ge(sV.h, eV + 4)
                s.activation(out=lnrow[:, :gw], in_=varrow[:, :gw], func=AF.Ln,
                             bias=1e-5).then_inc(sA.h, 1)
                sA.n += 1
                s.wait_ge(sA.h, sA.n)
                s.activation(out=rstdrow[:, :gw], in_=lnrow[:, :gw], func=AF.Exp,
                             scale=-0.5).then_inc(sA.h, 1)
            sA.n += 1

            @blk.tensor
            def _(te, gw=gw, eV=eV):
                te.wait_ge(sV.h, eV + 7)
                te.matmul(out=psMU[:, :gw], lhsT=ones_row[:], rhs=mu_hi[:, :gw],
                          start=True, stop=False)
                te.matmul(out=psMU[:, :gw], lhsT=ones_row[:], rhs=mu_lo[:, :gw],
                          start=False, stop=True)
                te.matmul(out=psRS[:, :gw], lhsT=ones_row[:], rhs=rs_hi[:, :gw],
                          start=True, stop=False)
                te.matmul(out=psRS[:, :gw], lhsT=ones_row[:], rhs=rs_lo[:, :gw],
                          start=False, stop=True).then_inc(sPE.h, 1)
            sPE.n += 1

            p2V, p2PE = sV.n, sPE.n
            xv2 = []

            @blk.sync
            def _(sync, p2V=p2V, p2PE=p2PE, src_g=src_g, gw=gw, xv2=xv2):
                sync.wait_ge(sPE.h, p2PE)
                for k in range(HC):
                    if k >= 2:
                        sync.wait_ge(sV.h, p2V + 2 * (k - 1))
                    xv2.append(rx.inc(sync.dma_start(out=xbuf[k % 2][:, :gw],
                                                     in_=src_g[k]), k % 2))

            @blk.vector
            def _(v, c0=c0, gw=gw, p2PE=p2PE, xv2=xv2):
                v.wait_ge(sPE.h, p2PE)
                for k in range(HC):
                    v.wait_ge(rx.sems[k % 2], xv2[k])
                    v.tensor_sub(out=t1b[k % 2][:, :gw], in0=xbuf[k % 2][:, :gw],
                                 in1=psMU[:, :gw]).then_inc(sV.h, 1)
                    sV.n += 1
                    v.wait_ge(sV.h, sV.n)
                    v.tensor_mul(out=xn[:, k * TH + c0:k * TH + c0 + gw],
                                 in0=t1b[k % 2][:, :gw], in1=psRS[:, :gw]
                                 ).then_inc(sV.h, 1)
                    sV.n += 1
            pass

    LN_V = sV.n
    cst_es.close()
    engine_barrier(blk, sBar, 0)

    if DEV_TAPS:
        @blk.sync
        def _(sync):
            sync.wait_ge(sV.h, LN_V)
            rtap.inc(sync.dma_start(out=dout["xn_tap"], in_=xn[:]))


    @blk.scalar
    def _(s):
        load_tbl(s, 18)
    with ExitStack() as pes:
        psP = [_ps(pes, nc, f"psP{i}", [128, 512]) for i in range(2)]
        psS0 = [_ps(pes, nc, f"psS0{i}", [128, 512]) for i in range(2)]
        psS1 = [_ps(pes, nc, f"psS1{i}", [128, 512]) for i in range(2)]
        psS2 = [_ps(pes, nc, f"psS2{i}", [32, 512]) for i in range(2)]
        wch = [_sb(pes, nc, f"wch{i}", [128, HC * 128], BF16) for i in range(2)]
        wxch = [_sb(pes, nc, f"wxch{i}", [128, 288], BF16) for i in range(2)]
        wchf = [_sb(pes, nc, f"wchf{i}", [128, HC * 128], F32) for i in range(2)]
        wxchf = [_sb(pes, nc, f"wxchf{i}", [128, 288], F32) for i in range(2)]
        upre = _sb(pes, nc, "upre", [128, TH], F32)
        conv1 = _sb(pes, nc, "conv1", [128, T], F32)
        ubuf = [_sb(pes, nc, f"ubuf{i}", [128, T], F32) for i in range(2)]
        ubf = [_sb(pes, nc, f"ubf{i}", [128, T], BF16) for i in range(2)]
        sgb = [_sb(pes, nc, f"sgb{i}", [128, T], F32) for i in range(2)]
        @blk.vector
        def _(v):
            v.memset(ssm0[:], 0.0)
            v.memset(ssm1[:], 0.0)
            v.memset(ssmBC[:], 0.0).then_inc(sV.h, 1)
        sV.n += 1

        MEMSET_V = sV.n
        pe_chunk = []
        evac_marks = {}
        silu_marks = {}
        store_marks = {}
        conv_marks = {}
        wx_marks = {}
        cast_marks = {}
        ssm_add_marks = []

        for j in range(128):
            is_u = j < 64
            jj = j if is_u else j - 64

            @blk.sync
            def _(sync, j=j, is_u=is_u):
                if j >= 2:
                    sync.wait_ge(sP.h, cast_marks[j - 2])
                for k8 in range(8):
                    cs = slice(k8 * 512, (k8 + 1) * 512)
                    rw.inc(sync.dma_start(out=wchf[j % 2][:, cs],
                                          in_=din["winT"][j][:, cs]), j % 2)
                if is_u:
                    rw.inc(sync.dma_start(out=wxchf[j % 2][:], in_=din["wxT"][j]),
                           j % 2)
            wLd = rw.cnt[j % 2]

            @blk.gpsimd
            def _(g, j=j, is_u=is_u, wLd=wLd):
                g.wait_ge(rw.sems[j % 2], wLd)
                if j >= 2:
                    g.wait_ge(sPE.h, pe_chunk[j - 2])
                g.tensor_copy(out=wch[j % 2][:], in_=wchf[j % 2][:]).then_inc(sP.h, 1)
                sP.n += 1
                if is_u:
                    g.tensor_copy(out=wxch[j % 2][:], in_=wxchf[j % 2][:]
                                  ).then_inc(sP.h, 1)
                    sP.n += 1
            cast_marks[j] = sP.n
            wG = sP.n

            if is_u:
                groups = [(0, 512, 0), (512, 512, 1), (899, 128, 0)]
            else:
                groups = [(3, 512, 0), (515, 512, 1)]

            g_pe_marks = []
            for gi, (c0, gw, pb) in enumerate(groups):
                @blk.tensor
                def _(te, j=j, gi=gi, c0=c0, gw=gw, pb=pb, wG=wG):
                    if j == 0 and gi == 0:
                        te.wait_ge(sV.h, MEMSET_V)
                    te.wait_ge(sP.h, wG)
                    if gi >= 2:
                        te.wait_ge(sA.h, evac_marks[(j, 0)])
                    elif j >= 1:
                        prev_groups = 3 if j - 1 < 64 else 2
                        for pgi in range(prev_groups - 1, -1, -1):
                            ppb = [0, 1, 0][pgi] if j - 1 < 64 else [0, 1][pgi]
                            if ppb == pb:
                                te.wait_ge(sA.h, evac_marks[(j - 1, pgi)])
                                break
                    for hc in range(HC):
                        mm = te.matmul(out=psP[pb][:, :gw],
                                       lhsT=wch[j % 2][:, hc * 128:(hc + 1) * 128],
                                       rhs=xn[:, hc * TH + c0:hc * TH + c0 + gw],
                                       start=(hc == 0), stop=(hc == HC - 1))
                        if hc == HC - 1:
                            mm.then_inc(sPE.h, 1)
                sPE.n += 1
                g_pe_marks.append(sPE.n)

                if is_u:
                    dsts = [(0, 512), (512, 512), (899, 128)]
                    d0, dw = dsts[gi]

                    @blk.scalar
                    def _(s, j=j, gi=gi, pb=pb, d0=d0, dw=dw, mark=sPE.n):
                        s.wait_ge(sA.h, sA.n)
                        s.wait_ge(sPE.h, mark)
                        if gi == 0 and j >= 1:
                            s.wait_ge(sV.h, conv_marks[j - 1])
                        s.copy(out=upre[:, d0:d0 + dw], in_=psP[pb][:, :dw]
                               ).then_inc(sA.h, 1)
                    sA.n += 1
                    evac_marks[(j, gi)] = sA.n
                else:
                    dsts = [(0, 512), (512, 512)]
                    d0, dw = dsts[gi]

                    @blk.scalar
                    def _(s, j=j, jj=jj, gi=gi, pb=pb, d0=d0, dw=dw, mark=sPE.n):
                        s.wait_ge(sA.h, sA.n)
                        s.wait_ge(sPE.h, mark)
                        if gi == 0 and jj >= 2:
                            s.wait_ge(rs.sems[jj % 2], store_marks[64 + jj - 2])
                        s.activation(out=sgb[jj % 2][:, d0:d0 + dw],
                                     in_=psP[pb][:, :dw], func=AF.Silu
                                     ).then_inc(sA.h, 1)
                    sA.n += 1
                    evac_marks[(j, gi)] = sA.n

            pe_chunk.append(sPE.n)

            if is_u:
                up_done = evac_marks[(j, 2)]

                @blk.vector
                def _(v, jj=jj, up_done=up_done):
                    v.wait_ge(sV.h, sV.n)
                    v.wait_ge(sA.h, up_done)
                    if jj >= 1:
                        v.wait_ge(sA.h, silu_marks[jj - 1])
                    v.tensor_scalar(out=conv1[:], in0=upre[:, 0:T],
                                    scalar1=convw_sb[:, 4 * jj:4 * jj + 1],
                                    scalar2=None, op0=ALU.mult).then_inc(sV.h, 1)
                    sV.n += 1
                    for kk in range(1, 4):
                        v.wait_ge(sV.h, sV.n)
                        v.scalar_tensor_tensor(
                            out=conv1[:], in0=upre[:, kk:T + kk],
                            scalar=convw_sb[:, 4 * jj + kk:4 * jj + kk + 1],
                            in1=conv1[:], op0=ALU.mult, op1=ALU.add
                        ).then_inc(sV.h, 1)
                        sV.n += 1
                pass
                conv_done = sV.n
                conv_marks[jj] = conv_done

                @blk.scalar
                def _(s, jj=jj, conv_done=conv_done):
                    s.wait_ge(sA.h, sA.n)
                    s.wait_ge(sV.h, conv_done)
                    if jj >= 2:
                        s.wait_ge(ru.sems[jj % 2], store_marks[jj - 2])
                    s.activation(out=ubuf[jj % 2][:], in_=conv1[:], func=AF.Silu,
                                 bias=convb_sb[:, jj:jj + 1]).then_inc(sA.h, 1)
                sA.n += 1
                silu_marks[jj] = sA.n
                u_done = sA.n

                @blk.vector
                def _(v, jj=jj, u_done=u_done):
                    v.wait_ge(sV.h, sV.n)
                    v.wait_ge(sA.h, u_done)
                    if jj >= 2:
                        v.wait_ge(sPE.h, wx_marks[jj - 2])
                    v.tensor_copy(out=ubf[jj % 2][:], in_=ubuf[jj % 2][:]
                                  ).then_inc(sV.h, 1)
                sV.n += 1
                ubf_done = sV.n

                @blk.tensor
                def _(te, j=j, jj=jj, ubf_done=ubf_done):
                    te.wait_ge(sV.h, ubf_done)
                    st = (jj % 16 == 0)
                    sp = (jj % 16 == 15)
                    for th in range(2):
                        rhs = ubf[jj % 2][:, th * 512:(th + 1) * 512]
                        te.matmul(out=psS0[th][:, :], lhsT=wxch[j % 2][:, 0:128],
                                  rhs=rhs, start=st, stop=sp)
                        te.matmul(out=psS1[th][:, :], lhsT=wxch[j % 2][:, 128:256],
                                  rhs=rhs, start=st, stop=sp)
                        mm = te.matmul(out=psS2[th][:, :],
                                       lhsT=wxch[j % 2][:, 256:288],
                                       rhs=rhs, start=st, stop=sp)
                        if th == 1:
                            mm.then_inc(sPE.h, 1)
                sPE.n += 1
                wx_pe = sPE.n
                wx_marks[jj] = sPE.n
                pe_chunk[-1] = sPE.n

                if jj % 16 == 15:
                    @blk.vector
                    def _(v, wx_pe=wx_pe, prevm=ssm_add_marks[-1] if ssm_add_marks
                          else 0):
                        v.wait_ge(sPE.h, wx_pe)
                        if prevm:
                            v.wait_ge(sV.h, prevm)
                        for th in range(2):
                            sl = slice(th * 512, (th + 1) * 512)
                            v.tensor_add(out=ssm0[:, sl], in0=ssm0[:, sl],
                                         in1=psS0[th][:, :])
                            v.tensor_add(out=ssm1[:, sl], in0=ssm1[:, sl],
                                         in1=psS1[th][:, :])
                            ins = v.tensor_add(out=ssmBC[:, sl], in0=ssmBC[:, sl],
                                               in1=psS2[th][:, :])
                            if th == 1:
                                ins.then_inc(sV.h, 1)
                    sV.n += 1
                    ssm_add_marks.append(sV.n)

                @blk.sync
                def _(sync, jj=jj, u_done=u_done, ubf_done=ubf_done):
                    sync.wait_ge(sA.h, u_done)
                    sync.wait_ge(sV.h, ubf_done)
                    ru.inc(sync.dma_start(out=u_dram[jj * 128:(jj + 1) * 128, :],
                                          in_=ubuf[jj % 2][:]), jj % 2)
                    if DEV_TAPS:
                        ru.inc(sync.dma_start(
                            out=dout["u_tap"][jj * 128:(jj + 1) * 128, :],
                            in_=ubuf[jj % 2][:]), jj % 2)
                store_marks[jj] = ru.cnt[jj % 2]
            else:
                sg_done = evac_marks[(j, 1)]

                @blk.sync
                def _(sync, jj=jj, sg_done=sg_done):
                    sync.wait_ge(sA.h, sg_done)
                    rs.inc(sync.dma_start(out=sg_dram[jj * 128:(jj + 1) * 128, :],
                                          in_=sgb[jj % 2][:]), jj % 2)
                    rs.inc(sync.dma_start(
                        out=dout["sg_out"][jj * 128:(jj + 1) * 128, :],
                        in_=sgb[jj % 2][:]), jj % 2)
                store_marks[64 + jj] = rs.cnt[jj % 2]

    PROJ_V, PROJ_A, PROJ_PE = sV.n, sA.n, sPE.n

    if DEV_TAPS:
        @blk.sync
        def _(sync):
            sync.wait_ge(sV.h, PROJ_V)
            rtap.inc(sync.dma_start(out=dout["ssm_tap"][0:128, :], in_=ssm0[:]))
            rtap.inc(sync.dma_start(out=dout["ssm_tap"][128:256, :], in_=ssm1[:]))
            rtap.inc(sync.dma_start(out=dout["ssm_tap"][256:288, :], in_=ssmBC[:]))

    @blk.sync
    def _(sync):
        ru.wait_all(sync)
        rs.wait_all(sync)
        if DEV_TAPS:
            rtap.wait_all(sync)

    engine_barrier(blk, sBar, 0)


    @blk.scalar
    def _(s):
        load_tbl(s, 6)
    with ExitStack() as pes:
        psq = [_ps(pes, nc, f"psq{i}", [1, 512]) for i in range(2)]
        psq2 = [_ps(pes, nc, f"psq2{i}", [2, 512]) for i in range(2)]
        psRD = [_ps(pes, nc, f"psRD{i}", [128, 512]) for i in range(2)]
        psRB = [_ps(pes, nc, f"psRB{i}", [32, 512]) for i in range(2)]
        dsq0 = _sb(pes, nc, "dsq0", [128, T], BF16)
        dsq1 = _sb(pes, nc, "dsq1", [128, T], BF16)
        bsq = _sb(pes, nc, "bsq", [32, T], BF16)
        r_dt = _sb(pes, nc, "r_dt", [1, T], F32)
        r_bc = _sb(pes, nc, "r_bc", [2, T], F32)
        lnt = _sb(pes, nc, "lnt", [2, T], F32)
        rdt_hi = _sb(pes, nc, "rdt_hi", [1, T], BF16)
        rdt_lo = _sb(pes, nc, "rdt_lo", [1, T], BF16)
        rbc_hi = _sb(pes, nc, "rbc_hi", [2, T], BF16)
        rbc_lo = _sb(pes, nc, "rbc_lo", [2, T], BF16)

        @blk.scalar
        def _(s):
            s.wait_ge(sV.h, PROJ_V)
            s.activation(out=dsq0[:], in_=ssm0[:], func=AF.Square)
            s.activation(out=dsq1[:], in_=ssm1[:], func=AF.Square)
            s.activation(out=bsq[:], in_=ssmBC[:], func=AF.Square).then_inc(sA.h, 1)
        sA.n += 1
        sq_done = sA.n

        @blk.tensor
        def _(te, sq_done=sq_done):
            te.wait_ge(sA.h, sq_done)
            te.wait_ge(sV.h, CONST_V)
            for th in range(2):
                sl = slice(th * 512, (th + 1) * 512)
                te.matmul(out=psq[th][:, :], lhsT=ones_col[:], rhs=dsq0[:, sl],
                          start=True, stop=False)
                te.matmul(out=psq[th][:, :], lhsT=ones_col[:], rhs=dsq1[:, sl],
                          start=False, stop=True)
                mm = te.matmul(out=psq2[th][:, :], lhsT=bco_sb[:], rhs=bsq[:, sl],
                               start=True, stop=True)
                if th == 1:
                    mm.then_inc(sPE.h, 1)
        sPE.n += 1
        q_done = sPE.n

        @blk.scalar
        def _(s, q_done=q_done):
            s.wait_ge(sPE.h, q_done)
            for th in range(2):
                sl = slice(th * 512, (th + 1) * 512)
                s.activation(out=r_dt[:, sl], in_=psq[th][:, :], func=AF.Ln,
                             scale=1.0 / RR, bias=1e-6)
                s.activation(out=lnt[:, sl], in_=psq2[th][:, :], func=AF.Ln,
                             scale=1.0 / NST, bias=1e-6).then_inc(sA.h, 1)
                sA.n += 1
            s.wait_ge(sA.h, sA.n)
            s.activation(out=r_dt[:], in_=r_dt[:], func=AF.Exp, scale=-0.5)
            s.activation(out=r_bc[:], in_=lnt[:], func=AF.Exp, scale=-0.5
                         ).then_inc(sA.h, 1)
        sA.n += 1
        r_done = sA.n

        @blk.vector
        def _(v, r_done=r_done):
            v.wait_ge(sA.h, r_done)
            v.tensor_copy(out=rdt_hi[:], in_=r_dt[:]).then_inc(sV.h, 1)
            sV.n += 1
            v.wait_ge(sV.h, sV.n)
            v.tensor_sub(out=rdt_lo[:], in0=r_dt[:], in1=rdt_hi[:])
            v.tensor_copy(out=rbc_hi[:], in_=r_bc[:]).then_inc(sV.h, 1)
            sV.n += 1
            v.wait_ge(sV.h, sV.n)
            v.tensor_sub(out=rbc_lo[:], in0=r_bc[:], in1=rbc_hi[:]).then_inc(sV.h, 1)
        sV.n += 1
        hl_done = sV.n

        @blk.tensor
        def _(te, hl_done=hl_done):
            te.wait_ge(sV.h, hl_done)
            for th in range(2):
                sl = slice(th * 512, (th + 1) * 512)
                te.matmul(out=psRD[th][:, :], lhsT=ones_row[:], rhs=rdt_hi[:, sl],
                          start=True, stop=False)
                te.matmul(out=psRD[th][:, :], lhsT=ones_row[:], rhs=rdt_lo[:, sl],
                          start=False, stop=True)
                te.matmul(out=psRB[th][:, :], lhsT=bce_sb[:], rhs=rbc_hi[:, sl],
                          start=True, stop=False)
                mm = te.matmul(out=psRB[th][:, :], lhsT=bce_sb[:], rhs=rbc_lo[:, sl],
                               start=False, stop=True)
                if th == 1:
                    mm.then_inc(sPE.h, 1)
        sPE.n += 1
        bb_done = sPE.n

        @blk.vector
        def _(v, bb_done=bb_done):
            v.wait_ge(sPE.h, bb_done)
            for th in range(2):
                sl = slice(th * 512, (th + 1) * 512)
                v.scalar_tensor_tensor(out=dtn0[:, sl], in0=ssm0[:, sl],
                                       scalar=dtlnw_sb[:, 0:1], in1=psRD[th][:, :],
                                       op0=ALU.mult, op1=ALU.mult)
                v.scalar_tensor_tensor(out=dtn1[:, sl], in0=ssm1[:, sl],
                                       scalar=dtlnw_sb[:, 1:2], in1=psRD[th][:, :],
                                       op0=ALU.mult, op1=ALU.mult)
                v.scalar_tensor_tensor(out=bcn[:, sl], in0=ssmBC[:, sl],
                                       scalar=bclnw_sb[:, 0:1], in1=psRB[th][:, :],
                                       op0=ALU.mult, op1=ALU.mult).then_inc(sV.h, 1)
                sV.n += 1
            v.wait_ge(sV.h, sV.n)
            v.tensor_copy(out=bchi[:], in_=bcn[:]).then_inc(sV.h, 1)
            sV.n += 1
            v.wait_ge(sV.h, sV.n)
            v.tensor_sub(out=bclo[:], in0=bcn[:], in1=bchi[:]).then_inc(sV.h, 1)
        sV.n += 1
        bcn_done = sV.n

        @blk.sync
        def _(sync, bcn_done=bcn_done):
            sync.wait_ge(sV.h, bcn_done)
            rfin.inc(sync.dma_start(out=dout["cn_out"], in_=bcn[16:32, :]))
            if DEV_TAPS:
                rtap.inc(sync.dma_start(out=dout["dtn_tap"][0:128, :], in_=dtn0[:]))
                rtap.inc(sync.dma_start(out=dout["dtn_tap"][128:256, :], in_=dtn1[:]))
                rtap.inc(sync.dma_start(out=dout["bcn_tap"], in_=bcn[:]))

    B_V = sV.n

    engine_barrier(blk, sBar, 0)

    with ExitStack() as pes:
        psSC = _ps(pes, nc, "psSC", [128, 64])
        psGT = _ps(pes, nc, "psGT", [64, 128])
        psH = [_ps(pes, nc, f"psH{i}", [128, 512]) for i in range(2)]
        psGB = [_ps(pes, nc, f"psGB{i}", [128, 512]) for i in range(2)]
        wrb = _sb(pes, nc, "wrb", [128, HC * 64], BF16)
        ach = [_sb(pes, nc, f"ach{i}", [128, HC * 128], BF16) for i in range(2)]
        e16b = [_sb(pes, nc, f"e16b{i}", [64, 128], BF16) for i in range(2)]
        sc = _sb(pes, nc, "sc", [128, 64], F32)
        wk = _sb(pes, nc, "wk", [128, 64], F32)
        wk2 = _sb(pes, nc, "wk2", [128, 64], F32)
        wk3 = _sb(pes, nc, "wk3", [128, 64], F32)
        vals = _sb(pes, nc, "vals", [128, 16], F32)
        nmax = _sb(pes, nc, "nmax", [128, 1], F32)
        e16v = _sb(pes, nc, "e16v", [128, 16], F32)
        zrow = _sb(pes, nc, "zrow", [128, 1], F32)
        rz = _sb(pes, nc, "rz", [128, 1], F32)
        mask = _sb(pes, nc, "mask", [128, 64], F32)
        eo = _sb(pes, nc, "eo", [128, 64], F32)
        gmat = _sb(pes, nc, "gmat", [128, 64], F32)
        gt_s = _sb(pes, nc, "gt_s", [64, T], F32)
        gt_hi = _sb(pes, nc, "gt_hi", [64, T], BF16)
        gt_lo = _sb(pes, nc, "gt_lo", [64, T], BF16)
        h_s = _sb(pes, nc, "h_s", [128, T], F32)
        hg_b = [_sb(pes, nc, f"hg_b{i}", [128, T], BF16) for i in range(2)]

        wrbf = _sb(pes, nc, "wrbf", [128, HC * 64], F32)
        achf = [_sb(pes, nc, f"achf{i}", [128, HC * 128], F32) for i in range(2)]
        e16f = [_sb(pes, nc, f"e16f{i}", [64, 128], F32) for i in range(2)]

        @blk.sync
        def _(sync):
            rfw.inc(sync.dma_start(out=wrbf[:], in_=din["wr_st"]))

        @blk.gpsimd
        def _(g):
            rfw.wait(g)
            g.tensor_copy(out=wrb[:], in_=wrbf[:]).then_inc(sP.h, 1)
            sP.n += 1
        wr_g = sP.n

        for i in range(8):
            @blk.tensor
            def _(te, i=i, wr_g=wr_g):
                te.wait_ge(sP.h, wr_g)
                te.wait_ge(sV.h, B_V if i == 0 else sV.n)
                for hc in range(HC):
                    mm = te.matmul(
                        out=psSC[:, :],
                        lhsT=xn[:, hc * TH + 3 + i * 128: hc * TH + 3 + (i + 1) * 128],
                        rhs=wrb[:, hc * 64:(hc + 1) * 64],
                        start=(hc == 0), stop=(hc == HC - 1))
                    if hc == HC - 1:
                        mm.then_inc(sPE.h, 1)
            sPE.n += 1
            sc_done = sPE.n

            @blk.vector
            def _(v, sc_done=sc_done):
                v.wait_ge(sV.h, sV.n)
                v.wait_ge(sPE.h, sc_done)
                v.tensor_copy(out=sc[:], in_=psSC[:, :]).then_inc(sV.h, 1)
                sV.n += 1
                v.wait_ge(sV.h, sV.n)
                v.tensor_copy(out=wk[:], in_=sc[:]).then_inc(sV.h, 1)
                sV.n += 1
                v.wait_ge(sV.h, sV.n)
                v.max(out=vals[:, 0:8], in_=wk[:]).then_inc(sV.h, 1)
                sV.n += 1
                v.wait_ge(sV.h, sV.n)
                v.match_replace(out=wk2[:], in_to_replace=vals[:, 0:8],
                                in_values=wk[:], imm_value=NEG).then_inc(sV.h, 1)
                sV.n += 1
                v.wait_ge(sV.h, sV.n)
                v.max(out=vals[:, 8:16], in_=wk2[:]).then_inc(sV.h, 1)
                sV.n += 1
                v.wait_ge(sV.h, sV.n)
                v.match_replace(out=wk3[:], in_to_replace=vals[:, 8:16],
                                in_values=wk2[:], imm_value=NEG).then_inc(sV.h, 1)
                sV.n += 1
                v.wait_ge(sV.h, sV.n)
                v.tensor_scalar_mul(out=nmax[:], in0=vals[:, 0:1], scalar1=-1.0
                                    ).then_inc(sV.h, 1)
            sV.n += 1
            tk_done = sV.n

            @blk.scalar
            def _(s, tk_done=tk_done):
                s.wait_ge(sA.h, sA.n)
                s.wait_ge(sV.h, tk_done)
                s.activation(out=e16v[:], in_=vals[:], func=AF.Exp,
                             bias=nmax[:, 0:1])
                s.activation(out=eo[:], in_=sc[:], func=AF.Exp,
                             bias=nmax[:, 0:1]).then_inc(sA.h, 1)
            sA.n += 1
            exp_done = sA.n

            @blk.vector
            def _(v, exp_done=exp_done):
                v.wait_ge(sV.h, sV.n)
                v.wait_ge(sA.h, exp_done)
                v.reduce_sum(out=zrow[:], in_=e16v[:], axis=AX.X).then_inc(sV.h, 1)
                sV.n += 1
                v.wait_ge(sV.h, sV.n)
                v.reciprocal(out=rz[:], in_=zrow[:]).then_inc(sV.h, 1)
                sV.n += 1
                v.wait_ge(sV.h, sV.n)
                v.tensor_tensor(out=mask[:], in0=sc[:], in1=wk3[:], op=ALU.is_gt
                                ).then_inc(sV.h, 1)
                sV.n += 1
                v.wait_ge(sV.h, sV.n)
                v.scalar_tensor_tensor(out=gmat[:], in0=eo[:], scalar=rz[:, 0:1],
                                       in1=mask[:], op0=ALU.mult, op1=ALU.mult
                                       ).then_inc(sV.h, 1)
            sV.n += 1
            g_done = sV.n

            @blk.tensor
            def _(te, g_done=g_done):
                te.wait_ge(sV.h, g_done)
                te.transpose(out=psGT[:, :], in_=gmat[:], identity=eye_sb[:]
                             ).then_inc(sPE.h, 1)
            sPE.n += 1
            t_done = sPE.n

            @blk.vector
            def _(v, i=i, t_done=t_done):
                v.wait_ge(sV.h, sV.n)
                v.wait_ge(sPE.h, t_done)
                v.tensor_copy(out=gt_s[:, i * 128:(i + 1) * 128], in_=psGT[:, :]
                              ).then_inc(sV.h, 1)
            sV.n += 1

            if DEV_TAPS:
                gm_done = sV.n

                @blk.sync
                def _(sync, i=i, gm_done=gm_done):
                    sync.wait_ge(sV.h, gm_done)
                    rtap.inc(sync.dma_start(
                        out=dout["g_tap"][i * 128:(i + 1) * 128, :], in_=gmat[:]))

        @blk.vector
        def _(v, at=sV.n):
            v.wait_ge(sV.h, at)
            v.tensor_copy(out=gt_hi[:], in_=gt_s[:]).then_inc(sV.h, 1)
            sV.n += 1
            v.wait_ge(sV.h, sV.n)
            v.tensor_sub(out=gt_lo[:], in0=gt_s[:], in1=gt_hi[:]).then_inc(sV.h, 1)
        sV.n += 1
        gt_done = sV.n

        hg_marks = {}
        for c in range(8):
            @blk.sync
            def _(sync, c=c, at=sP.n):
                if c >= 2:
                    sync.wait_ge(sP.h, at)
                for k8 in range(8):
                    cs = slice(k8 * 512, (k8 + 1) * 512)
                    rf.inc(sync.dma_start(out=achf[c % 2][:, cs],
                                          in_=din["ffnA"][c][:, cs]), c % 2)
                rf.inc(sync.dma_start(out=e16f[c % 2][:], in_=din["exp16"][c]),
                       c % 2)

            @blk.gpsimd
            def _(g, c=c, at=rf.cnt[c % 2], pat=sPE.n):
                g.wait_ge(rf.sems[c % 2], at)
                if c >= 2:
                    g.wait_ge(sPE.h, pat)
                g.tensor_copy(out=ach[c % 2][:], in_=achf[c % 2][:])
                g.tensor_copy(out=e16b[c % 2][:], in_=e16f[c % 2][:]
                              ).then_inc(sP.h, 1)
                sP.n += 1
            a_g = sP.n

            @blk.tensor
            def _(te, c=c, a_g=a_g, gt_done=gt_done):
                te.wait_ge(sP.h, a_g)
                te.wait_ge(sV.h, gt_done)
                if c >= 1:
                    te.wait_ge(sV.h, hg_marks[c - 1])
                for th in range(2):
                    for hc in range(HC):
                        te.matmul(
                            out=psH[th][:, :],
                            lhsT=ach[c % 2][:, hc * 128:(hc + 1) * 128],
                            rhs=xn[:, hc * TH + 3 + th * 512:
                                    hc * TH + 3 + th * 512 + 512],
                            start=(hc == 0), stop=(hc == HC - 1))
                for th in range(2):
                    sl = slice(th * 512, (th + 1) * 512)
                    te.matmul(out=psGB[th][:, :], lhsT=e16b[c % 2][:],
                              rhs=gt_hi[:, sl], start=True, stop=False)
                    mm = te.matmul(out=psGB[th][:, :], lhsT=e16b[c % 2][:],
                                   rhs=gt_lo[:, sl], start=False, stop=True)
                    if th == 1:
                        mm.then_inc(sPE.h, 1)
            sPE.n += 1
            h_done = sPE.n

            @blk.scalar
            def _(s, h_done=h_done):
                s.wait_ge(sA.h, sA.n)
                s.wait_ge(sPE.h, h_done)
                s.copy(out=h_s[:, 0:512], in_=psH[0][:, :])
                s.copy(out=h_s[:, 512:1024], in_=psH[1][:, :]).then_inc(sA.h, 1)
            sA.n += 1
            he_done = sA.n

            @blk.vector
            def _(v, c=c, he_done=he_done):
                v.wait_ge(sV.h, sV.n)
                v.wait_ge(sA.h, he_done)
                if c >= 2:
                    v.wait_ge(rh.sems[c % 2], hg_marks[(c - 2, "st")])
                v.tensor_tensor(out=hg_b[c % 2][:, 0:512], in0=h_s[:, 0:512],
                                in1=psGB[0][:, :], op=ALU.mult)
                v.tensor_tensor(out=hg_b[c % 2][:, 512:1024], in0=h_s[:, 512:1024],
                                in1=psGB[1][:, :], op=ALU.mult).then_inc(sV.h, 1)
            sV.n += 1
            hg_marks[c] = sV.n

            @blk.sync
            def _(sync, c=c, at=sV.n):
                sync.wait_ge(sV.h, at)
                rh.inc(sync.dma_start(out=dout["hg_out"][c * 128:(c + 1) * 128, :],
                                      in_=hg_b[c % 2][:]), c % 2)
            hg_marks[(c, "st")] = rh.cnt[c % 2]

    ssm_es.close()
    xn_es.close()

    @blk.sync
    def _(sync):
        rh.wait_all(sync)
        if DEV_TAPS:
            rtap.wait_all(sync)

    engine_barrier(blk, sBar, 0)

    with ExitStack() as pes:
        psD = [_ps(pes, nc, f"psD{i}", [128, 512]) for i in range(2)]
        psBB = [_ps(pes, nc, f"psBB{i}", [128, 512]) for i in range(2)]
        psCB = [_ps(pes, nc, f"psCB{i}", [128, 512]) for i in range(2)]
        wdch = [_sb(pes, nc, f"wdch{i}", [128, 256], BF16) for i in range(2)]
        wdchf = [_sb(pes, nc, f"wdchf{i}", [128, 256], F32) for i in range(2)]
        dl_g = _sb(pes, nc, "dl_g", [128, GG * T], F32)
        du_g = _sb(pes, nc, "du_g", [128, GG * T], F32)
        y_g = _sb(pes, nc, "y_g", [128, GG * T], F32)
        ab1 = _sb(pes, nc, "ab1", [128, 512], F32)
        exn = _sb(pes, nc, "exn", [128, 512], F32)
        l1p = _sb(pes, nc, "l1p", [128, 512], F32)
        rlu = _sb(pes, nc, "rlu", [128, 512], F32)
        utmp = [_sb(pes, nc, f"utmp{i}", [128, T], F32) for i in range(2)]
        sgt = [_sb(pes, nc, f"sgt{i}", [128, T], F32) for i in range(2)]
        z1b = [_sb(pes, nc, f"z1b{i}", [128, T], BF16) for i in range(2)]
        ctt = [_sb(pes, nc, f"ctt{i}", [128, T], F32) for i in range(2)]
        bb_s = _sb(pes, nc, "bb_s", [128, T], F32)
        cb_s = _sb(pes, nc, "cb_s", [128, T], F32)
        abuf = [_sb(pes, nc, f"abuf{i}", [128, T], F32) for i in range(2)]
        bbuf = [_sb(pes, nc, f"bbuf{i}", [128, T], F32) for i in range(2)]
        sbx = [_sb(pes, nc, f"sbx{i}", [128, T], F32) for i in range(2)]
        ytmp = [_sb(pes, nc, f"ytmp{i}", [128, T], F32) for i in range(2)]

        ct_store = {}
        z1_store = {}
        dve_marks = {}
        pend = []
        for g in range(DC // GG):
            dl_marks = {}
            yg_marks = {}
            for jj in range(GG):
                dc = g * GG + jj

                @blk.sync
                def _(sync, dc=dc, at=sA.n):
                    if dc >= 2:
                        sync.wait_ge(sA.h, at)
                    rd.inc(sync.dma_start(out=wdchf[dc % 2][:],
                                          in_=din["wdtT"][dc]), dc % 2)

                @blk.scalar
                def _(s, dc=dc, at=rd.cnt[dc % 2], pat=sPE.n):
                    s.wait_ge(sA.h, sA.n)
                    s.wait_ge(rd.sems[dc % 2], at)
                    if dc >= 2:
                        s.wait_ge(sPE.h, pat)
                    s.copy(out=wdch[dc % 2][:], in_=wdchf[dc % 2][:]
                           ).then_inc(sA.h, 1)
                    sA.n += 1
                wd_g = sA.n

                @blk.sync
                def _(sync, dc=dc, at=sV.n):
                    if dc == 0:
                        ru.wait_all(sync)
                        rs.wait_all(sync)
                    if dc >= 2:
                        sync.wait_ge(sV.h, at)
                    rt.inc(sync.dma_start(out=utmp[dc % 2][:],
                                          in_=u_dram[dc * 128:(dc + 1) * 128, :]),
                           dc % 2)
                u_l = rt.cnt[dc % 2]

                @blk.tensor
                def _(te, dc=dc, wd_g=wd_g, at=sA.n):
                    te.wait_ge(sA.h, wd_g)
                    te.wait_ge(sV.h, B_V)
                    if dc >= 1:
                        te.wait_ge(sA.h, at)
                    for th in range(2):
                        sl = slice(th * 512, (th + 1) * 512)
                        te.matmul(out=psD[th][:, :], lhsT=wdch[dc % 2][:, 0:128],
                                  rhs=dtn0[:, sl], start=True, stop=False)
                        mm = te.matmul(out=psD[th][:, :],
                                       lhsT=wdch[dc % 2][:, 128:256],
                                       rhs=dtn1[:, sl], start=False, stop=True)
                        if th == 1:
                            mm.then_inc(sPE.h, 1)
                sPE.n += 1
                d_pe = sPE.n

                for th in range(2):
                    @blk.scalar
                    def _(s, dc=dc, th=th, d_pe=d_pe, at=sV.n):
                        s.wait_ge(sPE.h, d_pe)
                        if dc == 0 and th == 0:
                            s.wait_ge(sV.h, CONST_V)
                        else:
                            s.wait_ge(sV.h, at)
                        s.activation(out=ab1[:], in_=psD[th][:, :], func=AF.Abs,
                                     bias=dtb_sb[:, dc:dc + 1]).then_inc(sA.h, 1)
                        sA.n += 1
                        s.wait_ge(sA.h, sA.n)
                        s.activation(out=exn[:], in_=ab1[:], func=AF.Exp,
                                     scale=-1.0).then_inc(sA.h, 1)
                        sA.n += 1
                        s.wait_ge(sA.h, sA.n)
                        s.activation(out=l1p[:], in_=exn[:], func=AF.Ln, bias=1.0)
                        s.activation(out=rlu[:], in_=psD[th][:, :], func=AF.Relu,
                                     bias=dtb_sb[:, dc:dc + 1]).then_inc(sA.h, 1)
                    sA.n += 1

                    @blk.vector
                    def _(v, dc=dc, jj=jj, th=th, at=sA.n, u_l=u_l):
                        v.wait_ge(sA.h, at)
                        sl = slice(jj * T + th * 512, jj * T + (th + 1) * 512)
                        ins = v.tensor_add(out=dl_g[:, sl], in0=rlu[:], in1=l1p[:])
                        ins.then_inc(sV.h, 1)
                        sV.n += 1
                        if th == 1:
                            v.wait_ge(sV.h, sV.n)
                            v.wait_ge(rt.sems[dc % 2], u_l)
                            if dc >= 2:
                                v.wait_ge(rct.sems[dc % 2], ct_store[dc - 2])
                            v.tensor_tensor_scan(
                                out=ctt[dc % 2][:],
                                data0=dl_g[:, jj * T:(jj + 1) * T],
                                data1=dl_g[:, jj * T:(jj + 1) * T],
                                initial=0.0, op0=ALU.add, op1=ALU.bypass
                            ).then_inc(sV.h, 1)
                            sV.n += 1
                            v.tensor_tensor(out=du_g[:, jj * T:(jj + 1) * T],
                                            in0=dl_g[:, jj * T:(jj + 1) * T],
                                            in1=utmp[dc % 2][:], op=ALU.mult)
                            v.wait_ge(sV.h, sV.n)
                            v.tensor_copy(out=ctot_sb[:, dc:dc + 1],
                                          in_=ctt[dc % 2][:, T - 1:T]
                                          ).then_inc(sV.h, 1)
                            sV.n += 1

                ct_v = sV.n
                dl_marks[jj] = sV.n

                @blk.sync
                def _(sync, dc=dc, ct_v=ct_v):
                    sync.wait_ge(sV.h, ct_v)
                    rct.inc(sync.dma_start(
                        out=dout["ct_out"][dc * 128:(dc + 1) * 128, :],
                        in_=ctt[dc % 2][:]), dc % 2)
                    if DEV_TAPS:
                        rtap.inc(sync.dma_start(
                            out=dout["dl_tap"][dc * 128:(dc + 1) * 128, :],
                            in_=dl_g[:, (dc % GG) * T:((dc % GG) + 1) * T]))
                ct_store[dc] = rct.cnt[dc % 2]

            for n in range(NST):
                @blk.tensor
                def _(te, g=g, n=n, aat=sA.n, vat=sV.n, pat=sP.n):
                    if g > 0 or n > 0:
                        te.wait_ge(sA.h, aat)
                        te.wait_ge(sV.h, vat)
                        te.wait_ge(sP.h, pat)
                    te.wait_ge(sV.h, B_V)
                    selB = seln_sb[:, n * 128:(n + 1) * 128]
                    selC = seln_sb[:, (16 + n) * 128:(16 + n + 1) * 128]
                    for th in range(2):
                        sl = slice(th * 512, (th + 1) * 512)
                        te.matmul(out=psBB[th][:, :], lhsT=selB,
                                  rhs=bchi[:, sl], start=True, stop=False)
                        te.matmul(out=psBB[th][:, :], lhsT=selB,
                                  rhs=bclo[:, sl], start=False, stop=True)
                        te.matmul(out=psCB[th][:, :], lhsT=selC,
                                  rhs=bchi[:, sl], start=True, stop=False)
                        mm = te.matmul(out=psCB[th][:, :], lhsT=selC,
                                       rhs=bclo[:, sl], start=False, stop=True)
                        if th == 1:
                            mm.then_inc(sPE.h, 1)
                sPE.n += 1
                bb_pe = sPE.n

                @blk.scalar
                def _(s, bb_pe=bb_pe):
                    s.wait_ge(sA.h, sA.n)
                    s.wait_ge(sPE.h, bb_pe)
                    for th in range(2):
                        sl = slice(th * 512, (th + 1) * 512)
                        s.copy(out=bb_s[:, sl], in_=psBB[th][:, :])
                        ins = s.copy(out=cb_s[:, sl], in_=psCB[th][:, :])
                        if th == 1:
                            ins.then_inc(sA.h, 1)
                sA.n += 1
                bb_a = sA.n

                for jj in range(GG):
                    dc = g * GG + jj
                    gii = (g * NST + n) * GG + jj
                    pp = gii % 2

                    @blk.scalar
                    def _(s, dc=dc, jj=jj, n=n, pp=pp,
                          war=dve_marks.get(gii - 2, 0)):
                        s.wait_ge(sA.h, sA.n)
                        if war:
                            s.wait_ge(sV.h, war)
                        s.activation(out=abuf[pp][:],
                                     in_=dl_g[:, jj * T:(jj + 1) * T],
                                     func=AF.Exp,
                                     scale=a_sb[:, dc * NST + n:dc * NST + n + 1]
                                     ).then_inc(sA.h, 1)
                    sA.n += 1
                    a_a = sA.n

                    @blk.gpsimd
                    def _(gp, jj=jj, pp=pp, bb_a=bb_a,
                          war=dve_marks.get(gii - 2, 0)):
                        gp.wait_ge(sP.h, sP.n)
                        if war:
                            gp.wait_ge(sV.h, war)
                        gp.wait_ge(sA.h, bb_a)
                        gp.tensor_tensor(out=bbuf[pp][:],
                                         in0=du_g[:, jj * T:(jj + 1) * T],
                                         in1=bb_s[:], op=ALU.mult).then_inc(sP.h, 1)
                    sP.n += 1
                    b_p = sP.n

                    @blk.vector
                    def _(v, dc=dc, jj=jj, n=n, pp=pp, a_a=a_a, b_p=b_p,
                          pe=pend[0] if pend else None):
                        v.wait_ge(sA.h, a_a)
                        v.wait_ge(sP.h, b_p)
                        v.tensor_tensor_scan(out=sbx[pp][:], data0=abuf[pp][:],
                                             data1=bbuf[pp][:], initial=0.0,
                                             op0=ALU.mult, op1=ALU.add
                                             ).then_inc(sV.h, 1)
                        sV.n += 1
                        if pe is not None:
                            v.wait_ge(sV.h, pe[1])
                            v.tensor_add(out=y_g[:, pe[0] * T:(pe[0] + 1) * T],
                                         in0=y_g[:, pe[0] * T:(pe[0] + 1) * T],
                                         in1=ytmp[pe[2]][:]).then_inc(sV.h, 1)
                            sV.n += 1
                        v.wait_ge(sV.h, sV.n)
                        if n == 0:
                            v.tensor_tensor(out=y_g[:, jj * T:(jj + 1) * T],
                                            in0=sbx[pp][:], in1=cb_s[:],
                                            op=ALU.mult).then_inc(sV.h, 1)
                            sV.n += 1
                        else:
                            v.tensor_tensor(out=ytmp[pp][:], in0=sbx[pp][:],
                                            in1=cb_s[:], op=ALU.mult
                                            ).then_inc(sV.h, 1)
                            sV.n += 1
                        v.tensor_copy(out=s_cap[:, dc * NST + n:dc * NST + n + 1],
                                      in_=sbx[pp][:, T - 1:T]).then_inc(sV.h, 1)
                    sV.n += 1
                    dve_marks[gii] = sV.n
                    if pend:
                        pend.pop(0)
                    if n > 0:
                        pend.append((jj, sV.n - 1, pp))

            if pend:
                @blk.vector
                def _(v, pe=pend.pop(0)):
                    v.wait_ge(sV.h, pe[1])
                    v.tensor_add(out=y_g[:, pe[0] * T:(pe[0] + 1) * T],
                                 in0=y_g[:, pe[0] * T:(pe[0] + 1) * T],
                                 in1=ytmp[pe[2]][:]).then_inc(sV.h, 1)
                sV.n += 1

            for jj in range(GG):
                dc = g * GG + jj
                pp = jj % 2

                @blk.sync
                def _(sync, dc=dc, pp=pp, at=sV.n):
                    if jj >= 2:
                        sync.wait_ge(sV.h, at)
                    rt.inc(sync.dma_start(out=utmp[pp][:],
                                          in_=u_dram[dc * 128:(dc + 1) * 128, :]),
                           pp)
                    rt.inc(sync.dma_start(out=sgt[pp][:],
                                          in_=sg_dram[dc * 128:(dc + 1) * 128, :]),
                           pp)
                ul = rt.cnt[pp]
                yp = sP.n

                @blk.vector
                def _(v, dc=dc, jj=jj, pp=pp, ul=ul, yp=yp):
                    v.wait_ge(sV.h, sV.n)
                    v.wait_ge(rt.sems[pp], ul)
                    v.wait_ge(sP.h, yp)
                    if dc >= 2:
                        v.wait_ge(rzr.sems[pp], z1_store.get(dc - 2, 0))
                    v.scalar_tensor_tensor(out=sbx[pp][:], in0=utmp[pp][:],
                                           scalar=dparam_sb[:, dc:dc + 1],
                                           in1=y_g[:, jj * T:(jj + 1) * T],
                                           op0=ALU.mult, op1=ALU.add
                                           ).then_inc(sV.h, 1)
                    sV.n += 1
                    v.wait_ge(sV.h, sV.n)
                    v.tensor_tensor(out=z1b[pp][:], in0=sbx[pp][:],
                                    in1=sgt[pp][:],
                                    op=ALU.mult).then_inc(sV.h, 1)
                sV.n += 1
                z_v = sV.n

                @blk.sync
                def _(sync, dc=dc, pp=pp, z_v=z_v):
                    sync.wait_ge(sV.h, z_v)
                    rzr.inc(sync.dma_start(
                        out=dout["z1"][dc * 128:(dc + 1) * 128, :],
                        in_=z1b[pp][:]), pp)
                z1_store[dc] = rzr.cnt[pp]

    @blk.sync
    def _(sync, at=sV.n):
        sync.wait_ge(sV.h, at)
        rfin.inc(sync.dma_start(out=dout["s_out"], in_=s_cap[:]))
        rfin.inc(sync.dma_start(out=dout["ctot_out"], in_=ctot_sb[:]))
        for r_ in [rfin, rct, rzr, rh, rs]:
            r_.wait_all(sync)
        if DEV_TAPS:
            rtap.wait_all(sync)

    es.close()
    return nc



def build_l2():
    nc = bass.Bass("TRN2", target_bir_lowering=False)
    din = {}
    for name, shape, dt in [
        ("z1", [DD, T], BF16),
        ("sg", [DD, T], F32),
        ("ct", [DD, T], F32),
        ("cn", [16, T], F32),
        ("hg", [1024, T], BF16),
        ("s_all", [8, 128, DC * NST], F32),
        ("ctot_all", [8, 128, DC], F32),
        ("a_st", [128, DC * NST], F32),
        ("maskA", [128, 128], F32),
        ("maskB", [128, 128], F32),
        ("seln", [32, 32 * 128], F32),
        ("woT", [32, 128, 64 * 128], F32),
        ("fbT", [32, 128, 8 * 128], F32),
        ("hsT", [H, T], F32),
    ]:
        din[name] = nc.dram_tensor(name, shape, dt, kind="ExternalInput").ap()
    out_t = nc.dram_tensor("out", [H, T], F32, kind="ExternalOutput").ap()
    z_dram = nc.dram_tensor("z_dram", [DD, T], BF16, kind="Internal").ap()
    if DEV_TAPS:
        sin_tap = nc.dram_tensor("sin_tap", [128, DC * NST], F32,
                                 kind="ExternalOutput").ap()
        hj_tap = nc.dram_tensor("hj_tap", [128, 8 * DC], F32,
                                 kind="ExternalOutput").ap()
        lns_tap = nc.dram_tensor("lns_tap", [128, DC * NST], F32,
                                 kind="ExternalOutput").ap()
        sgn_tap = nc.dram_tensor("sgn_tap", [128, DC * NST], F32,
                                 kind="ExternalOutput").ap()

    es = ExitStack()
    cpool = _sb(es, nc, "cpool", [128, 4], F32)
    sPE = Ctr(es, nc, "sPE")
    sA = Ctr(es, nc, "sA")
    sV = Ctr(es, nc, "sV")
    sP = Ctr(es, nc, "sP")
    sBar = Ctr(es, nc, "sBar")
    qc = DRing(es, nc, "qc", 1)
    qt = DRing(es, nc, "qt", 1)
    qe = DRing(es, nc, "qe", 2)
    qz = DRing(es, nc, "qz", 2)
    qZ = DRing(es, nc, "qZ", 1)
    qw = DRing(es, nc, "qw", 3)
    qf = DRing(es, nc, "qf", 1)
    qo = DRing(es, nc, "qo", 1)
    qtap = DRing(es, nc, "qtap", 1)

    mid = ExitStack()
    a_sb = _sb(mid, nc, "a_sb", [128, DC * NST], F32)
    lnS = _sb(mid, nc, "lnS", [128, DC * NST], F32)
    sgn = _sb(mid, nc, "sgn", [128, DC * NST], F32)
    cnn = _sb(mid, nc, "cnn", [16, T], F32)
    cnhi = _sb(mid, nc, "cnhi", [16, T], BF16)
    cnlo = _sb(mid, nc, "cnlo", [16, T], BF16)
    seln_sb = _sb(mid, nc, "seln_sb", [32, 32 * 128], BF16)

    blk = es.enter_context(nc.Block())


    @blk.scalar
    def _(s):
        load_tbl(s, 6)
    with ExitStack() as pes:
        selnf = _sb(pes, nc, "selnf", [32, 32 * 128], F32)
        sall = _sb(pes, nc, "sall", [128, 8 * DC * NST], F32)
        ctall = _sb(pes, nc, "ctall", [128, 8 * DC], F32)
        mA = _sb(pes, nc, "mA", [128, 128], F32)
        mB = _sb(pes, nc, "mB", [128, 128], F32)
        pfx = _sb(pes, nc, "pfx", [128, 8 * DC], F32)
        myp = _sb(pes, nc, "myp", [128, DC], F32)
        hj_sb = _sb(pes, nc, "hj_sb", [128, 8 * DC], F32)
        sin = _sb(pes, nc, "sin", [128, DC * NST], F32)
        tj = _sb(pes, nc, "tj", [128, DC], F32)

        @blk.sync
        def _(sync):
            for t_, srcn in [(sall, "s_all"), (ctall, "ctot_all")]:
                qc.inc(sync.dma_start(
                    out=t_[:].rearrange("p (j c) -> p j c", j=8),
                    in_=din[srcn].rearrange("j p c -> p j c")))
            for t_, srcn in [(a_sb, "a_st"), (mA, "maskA"), (mB, "maskB"),
                             (selnf, "seln")]:
                qc.inc(sync.dma_start(out=t_[:], in_=din[srcn]))
            qc.inc(sync.dma_start(out=cnn[:], in_=din["cn"]))

        def edge(v, ins):
            ins.then_inc(sV.h, 1)
            sV.n += 1
            v.wait_ge(sV.h, sV.n)

        @blk.vector
        def _(v):
            for i, val in enumerate([1.0]):
                v.memset(cpool[:, i:i + 1], val)
                nc.const_aps.register(F32, val, cpool[:, i:i + 1])
            qc.wait_all(v)
            v.tensor_copy(out=seln_sb[:], in_=selnf[:])
            edge(v, v.tensor_copy(out=cnhi[:], in_=cnn[:]))
            v.tensor_sub(out=cnlo[:], in0=cnn[:], in1=cnhi[:])
            edge(v, v.memset(pfx[:, 0:DC], 0.0))
            for j in range(1, 8):
                edge(v, v.tensor_add(out=pfx[:, j * DC:(j + 1) * DC],
                                     in0=pfx[:, (j - 1) * DC:j * DC],
                                     in1=ctall[:, (j - 1) * DC:j * DC]))
            edge(v, v.memset(myp[:], 0.0))
            for j in range(8):
                edge(v, v.scalar_tensor_tensor(
                    out=myp[:], in0=ctall[:, j * DC:(j + 1) * DC],
                    scalar=mA[:, j:j + 1], in1=myp[:],
                    op0=ALU.mult, op1=ALU.add))
            for j in range(8):
                sl = slice(j * DC, (j + 1) * DC)
                edge(v, v.tensor_sub(out=tj[:], in0=myp[:], in1=pfx[:, sl]))
                edge(v, v.tensor_sub(out=tj[:], in0=tj[:], in1=ctall[:, sl]))
                edge(v, v.tensor_scalar(out=hj_sb[:, sl], in0=tj[:],
                                        scalar1=mA[:, j:j + 1],
                                        scalar2=mB[:, j:j + 1],
                                        op0=ALU.mult, op1=ALU.add))
            v.memset(sin[:], 0.0).then_inc(sV.h, 1)
        sV.n += 1
        PREP_V = sV.n

        argb = [_sb(pes, nc, f"argb{i}", [128, DC * NST], F32) for i in range(2)]
        wb = [_sb(pes, nc, f"wb{i}", [128, DC * NST], F32) for i in range(2)]
        tb2 = _sb(pes, nc, "tb2", [128, DC * NST], F32)
        for j in range(8):
            pp = j % 2

            @blk.vector
            def _(v, j=j, pp=pp, aat=sA.n):
                if j == 0:
                    pass
                if j >= 2:
                    v.wait_ge(sA.h, aat)
                hb = hj_sb[:, j * DC:(j + 1) * DC].rearrange(
                    "p (c one) -> p c one", one=1).broadcast_to([128, DC, NST])
                v.tensor_tensor(out=argb[pp][:].rearrange("p (c n) -> p c n", n=NST),
                                in0=a_sb[:].rearrange("p (c n) -> p c n", n=NST),
                                in1=hb, op=ALU.mult).then_inc(sV.h, 1)
            sV.n += 1
            arg_v = sV.n

            @blk.scalar
            def _(s, j=j, pp=pp, arg_v=arg_v, vat=sV.n):
                s.wait_ge(sV.h, arg_v)
                s.activation(out=wb[pp][:], in_=argb[pp][:], func=AF.Exp
                             ).then_inc(sA.h, 1)
            sA.n += 1
            w_a = sA.n

            @blk.vector
            def _(v, j=j, pp=pp, w_a=w_a):
                v.wait_ge(sA.h, w_a)
                ins = v.tensor_mul(out=tb2[:], in0=wb[pp][:],
                                   in1=sall[:, j * DC * NST:(j + 1) * DC * NST])
                ins.then_inc(sV.h, 1)
                sV.n += 1
                v.wait_ge(sV.h, sV.n)
                v.tensor_add(out=sin[:], in0=sin[:], in1=tb2[:]).then_inc(sV.h, 1)
                sV.n += 1
                v.wait_ge(sV.h, sV.n)

        SIN_V = sV.n

        @blk.scalar
        def _(s, at=SIN_V):
            s.wait_ge(sV.h, at)
            s.activation(out=lnS[:], in_=sin[:], func=AF.Abs).then_inc(sA.h, 1)
            sA.n += 1
            s.wait_ge(sV.h, sV.n + 1)
            s.activation(out=lnS[:], in_=lnS[:], func=AF.Ln)
            s.activation(out=sgn[:], in_=sin[:], func=AF.Sign).then_inc(sA.h, 1)
            sA.n += 1

        @blk.vector
        def _(v, at=sA.n - 1):
            v.wait_ge(sA.h, at)
            v.tensor_scalar_max(out=lnS[:], in0=lnS[:], scalar1=1e-18
                                ).then_inc(sV.h, 1)
        sV.n += 1
        PREP_A = sA.n
        if DEV_TAPS:
            @blk.sync
            def _(sync, at=SIN_V):
                sync.wait_ge(sV.h, at)
                qtap.inc(sync.dma_start(out=sin_tap, in_=sin[:]))
                qtap.inc(sync.dma_start(out=hj_tap, in_=hj_sb[:]))

    if DEV_TAPS:
        @blk.sync
        def _(sync, at=sA.n, vt=sV.n):
            sync.wait_ge(sA.h, at)
            sync.wait_ge(sV.h, vt)
            qtap.inc(sync.dma_start(out=lns_tap, in_=lnS[:]))
            qtap.inc(sync.dma_start(out=sgn_tap, in_=sgn[:]))

    engine_barrier(blk, sBar, 1)

    with ExitStack() as pes:
        psCB = [_ps(pes, nc, f"psCB{i}", [128, 512]) for i in range(2)]
        ct_g = _sb(pes, nc, "ct_g", [128, GG * T], F32)
        yc_g = _sb(pes, nc, "yc_g", [128, GG * T], F32)
        sgt = [_sb(pes, nc, f"sgt{i}", [128, T], F32) for i in range(2)]
        z1t = [_sb(pes, nc, f"z1t{i}", [128, T], BF16) for i in range(2)]
        cb_s = _sb(pes, nc, "cb_s", [128, T], F32)
        corr = [_sb(pes, nc, f"corr{i}", [128, T], F32) for i in range(4)]
        tmpb = [_sb(pes, nc, f"tmpb{i}", [128, T], F32) for i in range(3)]
        z2b = _sb(pes, nc, "z2b", [128, T], F32)
        z_st = {}

        p2_dve = {}
        pend2 = []
        for g in range(DC // GG):
            @blk.sync
            def _(sync, g=g, at=sA.n):
                if g > 0:
                    sync.wait_ge(sA.h, at)
                for jj in range(GG):
                    dc = g * GG + jj
                    qt.inc(sync.dma_start(out=ct_g[:, jj * T:(jj + 1) * T],
                                          in_=din["ct"][dc * 128:(dc + 1) * 128, :]))
            ct_l = qt.cnt[0]

            for n in range(NST):
                @blk.tensor
                def _(te, g=g, n=n, aat=sA.n, vat=sV.n, pat=sP.n):
                    if g > 0 or n > 0:
                        te.wait_ge(sA.h, aat)
                        te.wait_ge(sV.h, vat)
                        te.wait_ge(sP.h, pat)
                    te.wait_ge(sV.h, PREP_V)
                    selC = seln_sb[:, n * 128:(n + 1) * 128]
                    for th in range(2):
                        sl = slice(th * 512, (th + 1) * 512)
                        te.matmul(out=psCB[th][:, :], lhsT=selC[0:16, :],
                                  rhs=cnhi[:, sl], start=True, stop=False)
                        mm = te.matmul(out=psCB[th][:, :], lhsT=selC[0:16, :],
                                       rhs=cnlo[:, sl], start=False, stop=True)
                        if th == 1:
                            mm.then_inc(sPE.h, 1)
                sPE.n += 1
                bb_pe = sPE.n

                @blk.scalar
                def _(s, bb_pe=bb_pe):
                    s.wait_ge(sPE.h, bb_pe)
                    for th in range(2):
                        sl = slice(th * 512, (th + 1) * 512)
                        ins = s.copy(out=cb_s[:, sl], in_=psCB[th][:, :])
                        if th == 1:
                            ins.then_inc(sA.h, 1)
                sA.n += 1
                bb_a = sA.n

                for jj in range(GG):
                    dc = g * GG + jj
                    gii = (g * NST + n) * GG + jj
                    pp = gii % 4

                    @blk.scalar
                    def _(s, dc=dc, jj=jj, n=n, pp=pp, ct_l=ct_l,
                          war=p2_dve.get(gii - 4, 0)):
                        s.wait_ge(sA.h, sA.n)
                        s.wait_ge(qt.sems[0], ct_l)
                        if n == 0 and jj == 0:
                            s.wait_ge(sA.h, PREP_A)
                        if war:
                            s.wait_ge(sV.h, war)
                        col = dc * NST + n
                        s.activation(out=corr[pp][:],
                                     in_=ct_g[:, jj * T:(jj + 1) * T],
                                     func=AF.Exp, scale=a_sb[:, col:col + 1],
                                     bias=lnS[:, col:col + 1]).then_inc(sA.h, 1)
                    sA.n += 1
                    a_a = sA.n

                    pp2 = gii % 3

                    @blk.vector
                    def _(v, dc=dc, jj=jj, n=n, pp=pp, pp2=pp2, a_a=a_a,
                          bb_a=bb_a, pe=pend2[0] if pend2 else None,
                          wr=p2_dve.get(gii - 2, 0)):
                        v.wait_ge(sV.h, wr)
                        v.wait_ge(sA.h, a_a)
                        v.wait_ge(sA.h, bb_a)
                        col = dc * NST + n
                        if n == 0:
                            v.scalar_tensor_tensor(
                                out=yc_g[:, jj * T:(jj + 1) * T],
                                in0=corr[pp][:], scalar=sgn[:, col:col + 1],
                                in1=cb_s[:], op0=ALU.mult,
                                op1=ALU.mult).then_inc(sV.h, 1)
                            sV.n += 1
                        else:
                            v.scalar_tensor_tensor(
                                out=tmpb[pp2][:], in0=corr[pp][:],
                                scalar=sgn[:, col:col + 1], in1=cb_s[:],
                                op0=ALU.mult, op1=ALU.mult).then_inc(sV.h, 1)
                            sV.n += 1
                        if pe is not None:
                            v.wait_ge(sV.h, pe[1])
                            v.tensor_add(out=yc_g[:, pe[0] * T:(pe[0] + 1) * T],
                                         in0=yc_g[:, pe[0] * T:(pe[0] + 1) * T],
                                         in1=tmpb[pe[2]][:]).then_inc(sV.h, 1)
                            sV.n += 1
                    p2_dve[gii] = sV.n
                    if pend2:
                        pend2.pop(0)
                    if n > 0:
                        pend2.append((jj, p2_dve[gii], pp2))

            if pend2:
                @blk.vector
                def _(v, pe=pend2.pop(0)):
                    v.wait_ge(sV.h, pe[1])
                    v.tensor_add(out=yc_g[:, pe[0] * T:(pe[0] + 1) * T],
                                 in0=yc_g[:, pe[0] * T:(pe[0] + 1) * T],
                                 in1=tmpb[pe[2]][:]).then_inc(sV.h, 1)
                sV.n += 1

            for jj in range(GG):
                dc = g * GG + jj
                pp = jj % 2

                @blk.sync
                def _(sync, dc=dc, jj=jj, pp=pp, at=sV.n,
                      zst=z_st.get(g * GG + jj - 2, 0)):
                    if jj >= 2:
                        sync.wait_ge(sV.h, at)
                    if dc >= 2:
                        sync.wait_ge(qz.sems[pp], zst)
                    qe.inc(sync.dma_start(out=sgt[pp][:],
                                          in_=din["sg"][dc * 128:(dc + 1) * 128, :]),
                           pp)
                    qe.inc(sync.dma_start(out=z1t[pp][:],
                                          in_=din["z1"][dc * 128:(dc + 1) * 128, :]),
                           pp)
                ul = qe.cnt[pp]
                yp = sP.n

                @blk.vector
                def _(v, dc=dc, jj=jj, pp=pp, ul=ul, yp=yp):
                    v.wait_ge(qe.sems[pp], ul)
                    v.wait_ge(sP.h, yp)
                    v.tensor_tensor(out=z2b[:], in0=yc_g[:, jj * T:(jj + 1) * T],
                                    in1=sgt[pp][:], op=ALU.mult).then_inc(sV.h, 1)
                    sV.n += 1
                    v.wait_ge(sV.h, sV.n)
                    v.tensor_add(out=z1t[pp][:], in0=z2b[:], in1=z1t[pp][:]
                                 ).then_inc(sV.h, 1)
                sV.n += 1
                z_v = sV.n

                @blk.sync
                def _(sync, dc=dc, pp=pp, z_v=z_v):
                    sync.wait_ge(sV.h, z_v)
                    qz.inc(sync.dma_start(out=z_dram[dc * 128:(dc + 1) * 128, :],
                                          in_=z1t[pp][:]), pp)
                z_st[dc] = qz.cnt[pp]

    engine_barrier(blk, sBar, 2)
    mid.close()

    with ExitStack() as pes:
        psAT = [_ps(pes, nc, f"psAT{i}", [128, 512]) for i in range(2)]
        psF = [_ps(pes, nc, f"psF{i}", [128, 512]) for i in range(2)]
        z_sb = _sb(pes, nc, "z_sb", [128, DC * T], BF16)
        wof = [_sb(pes, nc, f"wof{i}", [128, 1024], F32) for i in range(3)]
        wo = [_sb(pes, nc, f"wo{i}", [128, DC * 128], BF16) for i in range(2)]
        fbf = _sb(pes, nc, "fbf", [128, 8 * 128], F32)
        fbb = [_sb(pes, nc, f"fbb{i}", [128, 8 * 128], BF16) for i in range(2)]
        hgt = [_sb(pes, nc, f"hgt{i}", [128, 8 * 512], BF16) for i in range(2)]
        xres = _sb(pes, nc, "xres", [128, T], F32)
        o1 = _sb(pes, nc, "o1", [128, 512], F32)
        out_s = _sb(pes, nc, "out_s", [128, T], F32)

        @blk.sync
        def _(sync):
            qz.wait_all(sync)
            zv = z_sb[:].rearrange("p (c t) -> p c t", c=DC)
            zd = z_dram.rearrange("(c p) t -> p c t", p=128)
            for k8 in range(8):
                cs = slice(k8 * 8, (k8 + 1) * 8)
                qZ.inc(sync.dma_start(out=zv[:, cs], in_=zd[:, cs]))
            for th in range(2):
                hv = hgt[th][:].rearrange("p (k c) -> p k c", k=8)
                hd = din["hg"].rearrange("(k p) t -> p k t", p=128)[
                    :, :, th * 512:(th + 1) * 512]
                for k4 in range(4):
                    ks = slice(k4 * 2, (k4 + 1) * 2)
                    qZ.inc(sync.dma_start(out=hv[:, ks], in_=hd[:, ks]))

        wo_marks = {}
        mm_marks = {}
        st_marks = {}
        o1_mark = [0]
        for hmt in range(32):
            for q in range(8):
                qi = hmt * 8 + q

                @blk.sync
                def _(sync, hmt=hmt, q=q, qi=qi, at=wo_marks.get(qi - 3, 0)):
                    if qi >= 3:
                        sync.wait_ge(sP.h, at)
                    for k2 in range(2):
                        cs = slice(k2 * 512, (k2 + 1) * 512)
                        qw.inc(sync.dma_start(
                            out=wof[q % 3][:, cs],
                            in_=din["woT"][hmt][:, q * 1024 + k2 * 512:
                                                q * 1024 + k2 * 512 + 512]),
                               q % 3)

                @blk.gpsimd
                def _(g, hmt=hmt, q=q, at=qw.cnt[q % 3],
                      pat=mm_marks.get(hmt - 2, 0)):
                    g.wait_ge(qw.sems[q % 3], at)
                    if hmt >= 2 and q == 0:
                        g.wait_ge(sPE.h, pat)
                    g.tensor_copy(out=wo[hmt % 2][:, q * 1024:(q + 1) * 1024],
                                  in_=wof[q % 3][:]).then_inc(sP.h, 1)
                sP.n += 1
                wo_marks[qi] = sP.n
            wo_p = sP.n

            @blk.sync
            def _(sync, hmt=hmt, pat=mm_marks.get(hmt - 2, 0), vat=sV.n,
                  fat=sP.n):
                if hmt >= 2:
                    sync.wait_ge(sPE.h, pat)
                if hmt >= 1:
                    sync.wait_ge(sV.h, vat)
                    sync.wait_ge(sP.h, fat)
                qf.inc(sync.dma_start(out=fbf[:], in_=din["fbT"][hmt]))
                qf.inc(sync.dma_start(
                    out=xres[:], in_=din["hsT"][hmt * 128:(hmt + 1) * 128, :]))

            @blk.gpsimd
            def _(g, hmt=hmt, at=qf.cnt[0]):
                g.wait_ge(qf.sems[0], at)
                g.tensor_copy(out=fbb[hmt % 2][:], in_=fbf[:]
                              ).then_inc(sP.h, 1)
            sP.n += 1
            fb_p = sP.n

            for th in range(2):
                @blk.tensor
                def _(te, hmt=hmt, th=th, wo_p=wo_p, fb_p=fb_p,
                      aat=sA.n, vat=sV.n):
                    te.wait_ge(sP.h, fb_p)
                    qZ.wait_all(te)
                    if hmt >= 1:
                        te.wait_ge(sV.h, vat)
                    for dc in range(DC):
                        te.matmul(out=psAT[th][:, :],
                                  lhsT=wo[hmt % 2][:, dc * 128:(dc + 1) * 128],
                                  rhs=z_sb[:, dc * T + th * 512:
                                           dc * T + th * 512 + 512],
                                  start=(dc == 0), stop=(dc == DC - 1))
                    for kr in range(8):
                        mm = te.matmul(out=psF[th][:, :],
                                       lhsT=fbb[hmt % 2][:, kr * 128:(kr + 1) * 128],
                                       rhs=hgt[th][:, kr * 512:(kr + 1) * 512],
                                       start=(kr == 0), stop=(kr == 7))
                        if kr == 7:
                            mm.then_inc(sPE.h, 1)
                sPE.n += 1
            mm_marks[hmt] = sPE.n

            for th in range(2):
                mm_pe = sPE.n - 1 + th if False else mm_marks[hmt]

                @blk.scalar
                def _(s, hmt=hmt, th=th, mm_pe=mm_pe, xat=qf.cnt[0],
                      om=o1_mark[0]):
                    s.wait_ge(sPE.h, mm_pe - (1 if th == 0 else 0))
                    s.wait_ge(qf.sems[0], xat)
                    if om:
                        s.wait_ge(sV.h, om)
                    s.activation(out=o1[:], in_=psF[th][:, :], func=AF.Identity,
                                 scale=2.0).then_inc(sA.h, 1)
                sA.n += 1
                ev_a = sA.n

                @blk.vector
                def _(v, hmt=hmt, th=th, ev_a=ev_a, stm=st_marks.get(hmt - 1, 0)):
                    v.wait_ge(sV.h, sV.n)
                    v.wait_ge(sA.h, ev_a)
                    if hmt >= 1 and th == 0:
                        v.wait_ge(qo.sems[0], stm)
                    v.tensor_add(out=o1[:], in0=o1[:], in1=psAT[th][:, :]
                                 ).then_inc(sV.h, 1)
                    sV.n += 1
                    v.wait_ge(sV.h, sV.n)
                    v.tensor_add(out=out_s[:, th * 512:(th + 1) * 512],
                                 in0=o1[:],
                                 in1=xres[:, th * 512:(th + 1) * 512]
                                 ).then_inc(sV.h, 1)
                    sV.n += 1
                    o1_mark[0] = sV.n
                    sV.n -= 1
                sV.n += 1
            o_v = sV.n

            @blk.sync
            def _(sync, hmt=hmt, o_v=o_v):
                sync.wait_ge(sV.h, o_v)
                qo.inc(sync.dma_start(out=out_t[hmt * 128:(hmt + 1) * 128, :],
                                      in_=out_s[:]))
            st_marks[hmt] = qo.cnt[0]

    @blk.sync
    def _(sync):
        qo.wait_all(sync)
        if DEV_TAPS:
            qtap.wait_all(sync)

    es.close()
    return nc



def _stage_l1(inputs):
    x = np.asarray(inputs["L0_attn_hidden_states"], np.float32)[0]
    W_in = np.asarray(inputs["L0_attn_W_in"], np.float32)
    conv_w = np.asarray(inputs["L0_attn_conv_w"], np.float32)
    conv_b = np.asarray(inputs["L0_attn_conv_b"], np.float32)
    Wx = np.asarray(inputs["L0_attn_Wx"], np.float32)
    dt_ln_w = np.asarray(inputs["L0_attn_dt_ln_w"], np.float32)
    b_ln_w = np.asarray(inputs["L0_attn_b_ln_w"], np.float32)
    c_ln_w = np.asarray(inputs["L0_attn_c_ln_w"], np.float32)
    Wdt = np.asarray(inputs["L0_attn_Wdt"], np.float32)
    dt_bias = np.asarray(inputs["L0_attn_dt_bias"], np.float32)
    A = np.asarray(inputs["L0_attn_A"], np.float32)
    D_param = np.asarray(inputs["L0_attn_D_param"], np.float32)
    fA = np.asarray(inputs["L0_ffn_A"], np.float32)
    Wr = np.asarray(inputs["L0_ffn_Wr"], np.float32)

    hsT = np.ascontiguousarray(x.T)
    winT = np.ascontiguousarray(
        W_in.T.reshape(HC, 128, 128, 128).transpose(2, 1, 0, 3)).reshape(
        128, 128, HC * 128)
    wxT = np.ascontiguousarray(Wx.T.reshape(DC, 128, 288))
    wdtT = np.ascontiguousarray(
        Wdt.T.reshape(2, 128, DC, 128).transpose(2, 1, 0, 3)).reshape(DC, 128, 256)
    a_st = np.ascontiguousarray(
        A.reshape(DC, 128, NST).transpose(1, 0, 2)).reshape(128, DC * NST)
    convw = np.ascontiguousarray(
        conv_w.reshape(DC, 128, 4).transpose(1, 0, 2)).reshape(128, DC * 4)
    def pad128(a):
        out = np.zeros((a.shape[0], 128), np.float32)
        out[:, :a.shape[1]] = a
        return out
    convb = pad128(np.ascontiguousarray(conv_b.reshape(DC, 128).T))
    dtb = pad128(np.ascontiguousarray(dt_bias.reshape(DC, 128).T))
    dparam = pad128(np.ascontiguousarray(D_param.reshape(DC, 128).T))
    dtlnw = pad128(np.ascontiguousarray(dt_ln_w.reshape(2, 128).T))
    bclnw = pad128(np.concatenate([b_ln_w, c_ln_w]).reshape(32, 1).astype(np.float32))
    bc_ones = np.zeros((32, 128), np.float32)
    bc_ones[0:16, 0] = 1.0
    bc_ones[16:32, 1] = 1.0
    bc_exp = np.zeros((2, 128), np.float32)
    bc_exp[0, 0:16] = 1.0
    bc_exp[1, 16:32] = 1.0
    eye128 = np.eye(128, dtype=np.float32)
    wr_st = np.ascontiguousarray(
        Wr.T.reshape(HC, 128, 64).transpose(1, 0, 2)).reshape(128, HC * 64)
    fAT = np.ascontiguousarray(fA.reshape(1024, H).T)
    ffnA = np.ascontiguousarray(
        fAT.reshape(HC, 128, 8, 128).transpose(2, 1, 0, 3)).reshape(
        8, 128, HC * 128)
    exp16 = np.zeros((8, 64, 128), np.float32)
    for c in range(8):
        for m in range(128):
            exp16[c, (c * 128 + m) // 16, m] = 1.0
    seln = np.zeros((32, 32 * 128), np.float32)
    for n in range(32):
        seln[n, n * 128:(n + 1) * 128] = 1.0

    shared = dict(winT=winT, wxT=wxT, wdtT=wdtT, a_st=a_st, convw=convw,
                  convb=convb, dtb=dtb, dparam=dparam, dtlnw=dtlnw, bclnw=bclnw,
                  bc_ones=bc_ones, bc_exp=bc_exp, eye128=eye128, wr_st=wr_st,
                  ffnA=ffnA, exp16=exp16, seln=seln)
    in_maps = []
    for c in range(NCORES):
        c0 = c * T
        halo = (np.zeros((H, 3), np.float32) if c == 0
                else np.ascontiguousarray(hsT[:, c0 - 3:c0]))
        m = dict(shared)
        m["hsT"] = np.ascontiguousarray(
            np.concatenate([halo, hsT[:, c0:c0 + T]], axis=1))
        in_maps.append(m)
    return in_maps


def _stage_l2(inputs, res1):
    x = np.asarray(inputs["L0_attn_hidden_states"], np.float32)[0]
    W_out = np.asarray(inputs["L0_attn_W_out"], np.float32)
    A = np.asarray(inputs["L0_attn_A"], np.float32)
    fB = np.asarray(inputs["L0_ffn_B"], np.float32)

    hsT = np.ascontiguousarray(x.T)
    a_st = np.ascontiguousarray(
        A.reshape(DC, 128, NST).transpose(1, 0, 2)).reshape(128, DC * NST)
    seln = np.zeros((32, 32 * 128), np.float32)
    for n in range(32):
        seln[n, n * 128:(n + 1) * 128] = 1.0
    woT = np.ascontiguousarray(
        W_out.T.reshape(DC, 128, 32, 128).transpose(2, 1, 0, 3)).reshape(
        32, 128, DC * 128)
    fbT = np.ascontiguousarray(
        fB.transpose(0, 2, 1).reshape(8, 128, 32, 128).transpose(2, 1, 0, 3)
    ).reshape(32, 128, 8 * 128)
    s_all = np.stack([np.asarray(res1.results[c]["s_out"], np.float32)
                      for c in range(NCORES)])
    ctot_all = np.stack([np.asarray(res1.results[c]["ctot_out"], np.float32)
                         for c in range(NCORES)])
    in_maps = []
    for c in range(NCORES):
        mA = np.zeros((128, 128), np.float32)
        mB = np.full((128, 128), 1.0e5, np.float32)
        mA[:, 0:c] = 1.0
        mB[:, 0:c] = 0.0
        r = res1.results[c]
        m = dict(
            z1=np.asarray(r["z1"]), sg=np.asarray(r["sg_out"], np.float32),
            ct=np.asarray(r["ct_out"], np.float32),
            cn=np.asarray(r["cn_out"], np.float32), hg=np.asarray(r["hg_out"]),
            s_all=s_all, ctot_all=ctot_all, a_st=a_st, maskA=mA, maskB=mB,
            seln=seln, woT=woT, fbT=fbT,
            hsT=np.ascontiguousarray(hsT[:, c * T:(c + 1) * T]),
        )
        in_maps.append(m)
    return in_maps


def kernel(**inputs):
    in_maps1 = _stage_l1(inputs)
    nc1 = build_l1()
    res1 = run_bass_kernel_spmd(nc1, in_maps1, core_ids=list(range(NCORES)))
    in_maps2 = _stage_l2(inputs, res1)
    nc2 = build_l2()
    res2 = run_bass_kernel_spmd(nc2, in_maps2, core_ids=list(range(NCORES)))
    out = np.concatenate(
        [np.asarray(res2.results[c]["out"], np.float32).T for c in range(NCORES)],
        axis=0)
    return out.reshape(1, 8192, H)


if __name__ == "__main__":
    pass
